# version 07
# baseline (speedup 1.0000x reference)
import numpy as np
from contextlib import ExitStack

import concourse.bass as bass
import concourse.mybir as mybir
from concourse.bass_utils import run_bass_kernel_spmd

F32 = mybir.dt.float32
F32R = mybir.dt.float32r
ALU = mybir.AluOpType
ACT = mybir.ActivationFunctionType
AX = mybir.AxisListType.X

B, T, D = 2, 8192, 1024
NCORES = 8
TPC = 2048
HALFT = 1024
NQ = 4
QT = 512
NTAU = 8
KC = 8
NH = 16
DH = 64
HEC = 8
NE = 9
NHC = 22
HFF = 2730
HPAD = NHC * 128
PRIOR = 6144
NPT = PRIOR // 128
PI = float(np.pi)
NI = NE * NHC

C_PI2, C_PI4, C_I1024, C_EPSLN, C_EPSAT, C_35, C_NI1024 = 0, 1, 2, 3, 4, 5, 6

ENGINES = ("sync", "tensor", "vector", "scalar", "gpsimd")
ESEM = {"sync": "sL", "tensor": "sT", "vector": "sV", "scalar": "sS", "gpsimd": "sG"}


class Prog:
    def __init__(self):
        self.mode = 0
        self.counters = {}
        self.tags = {}
        self.used = set()
        self.filter = None
        self.eng = None
        self.sems = None
        self.chain = None
        self.pending = []
        self.anon = 0
        self.anon_marks = {}

    @property
    def counting(self):
        return self.mode != 2

    def op(self, engine, builder, waits=(), tag=None, dn=1, chain=True, sem=None):
        sem_name = sem or ESEM[engine]
        if tag is None:
            tag = f"_a{self.anon}"
            self.anon += 1
        if self.mode == 0:
            self.used.update(waits)
            if chain:
                self.used.update(self.pending)
                if self.chain is not None:
                    self.used.add(self.chain)
                self.pending = []
                self.chain = tag
            return tag
        live = (tag in self.used) or dn != 1
        if self.mode == 1:
            if live:
                self.counters[sem_name] = self.counters.get(sem_name, 0) + dn
            assert tag not in self.tags, f"dup tag {tag}"
            self.tags[tag] = (sem_name, self.counters.get(sem_name, 0), engine, dn)
            if chain:
                self.chain = tag
                self.pending = []
            return tag
        allw = list(waits)
        if chain:
            allw += self.pending
            if self.chain is not None:
                allw.append(self.chain)
            self.pending = []
            self.chain = tag
        if engine == self.filter:
            need = {}
            for lab in allw:
                s, v, prod_eng, pdn = self.tags[lab]
                if v > need.get(s, -1):
                    need[s] = v
            for s, v in need.items():
                self.eng.wait_ge(self.sems[s], v)
            ins = builder(self.eng)
            if live:
                ins.then_inc(self.sems[sem_name], dn)
        return tag

    def wait(self, engine, *labels):
        if self.mode == 0:
            self.used.update(labels)
            return
        if self.mode == 1 or engine != self.filter:
            return
        need = {}
        for lab in labels:
            s, v, prod_eng, pdn = self.tags[lab]
            if v > need.get(s, -1):
                need[s] = v
        for s, v in need.items():
            self.eng.wait_ge(self.sems[s], v)

    def after(self, *labels):
        if self.mode == 0:
            self.used.update(labels)
        self.pending += list(labels)

    def reset_chain(self):
        self.chain = None
        self.pending = []


def host_prepare(inp):
    out = {}
    q = np.ascontiguousarray(np.asarray(inp["L0_attn_query"], dtype=np.float32))

    WQ = np.asarray(inp["L0_attn_W_Q"], dtype=np.float32)[:, :, 0]
    WK = np.asarray(inp["L0_attn_W_K"], dtype=np.float32)[:, :, 0]
    WQK = np.concatenate([WQ.T, WK.T], axis=1)
    out["wqk"] = np.ascontiguousarray(
        WQK.reshape(KC, 128, 32).transpose(1, 0, 2).reshape(128, KC * 32))

    WV = np.asarray(inp["L0_attn_W_V"], dtype=np.float32)
    WVp = WV.transpose(1, 0, 2).reshape(D, NH * DH)
    out["wv"] = np.ascontiguousarray(
        WVp.reshape(KC, 128, HEC, 128).transpose(2, 1, 0, 3).reshape(HEC, 128, KC * 128))
    out["wvf"] = np.ascontiguousarray(
        WVp.reshape(KC, 128, NH * DH).transpose(1, 0, 2).reshape(128, KC * NH * DH))

    WO = np.asarray(inp["L0_attn_W_O"], dtype=np.float32)
    WOp = WO.transpose(2, 1, 0).reshape(NH * DH, D)
    out["wo"] = np.ascontiguousarray(WOp.reshape(HEC, 128, D))

    Wg = np.asarray(inp["L0_ffn_Wg"], dtype=np.float32)
    out["wg"] = np.ascontiguousarray(
        Wg.T.reshape(KC, 128, 8).transpose(1, 0, 2).reshape(128, KC * 8))
    out["gbias"] = np.ascontiguousarray(
        np.asarray(inp["L0_ffn_routing_bias"], dtype=np.float32).reshape(1, 8))

    def prep_w12(w):
        e = w.shape[0]
        wp = np.zeros((e, HPAD, D), dtype=np.float32)
        wp[:, :HFF, :] = w
        return np.ascontiguousarray(
            wp.reshape(e, NHC, 128, KC, 128).transpose(0, 1, 4, 3, 2)
            .reshape(e, NHC, 128, KC * 128))

    def prep_w3(w):
        e = w.shape[0]
        wp = np.zeros((e, D, HPAD), dtype=np.float32)
        wp[:, :, :HFF] = w
        return np.ascontiguousarray(wp.transpose(0, 2, 1).reshape(e, NHC, 128, D))

    out["w1"] = prep_w12(np.concatenate(
        [np.asarray(inp["L0_ffn_Ws1"], dtype=np.float32)[None],
         np.asarray(inp["L0_ffn_W1"], dtype=np.float32)], axis=0))
    out["w2"] = prep_w12(np.concatenate(
        [np.asarray(inp["L0_ffn_Ws2"], dtype=np.float32)[None],
         np.asarray(inp["L0_ffn_W2"], dtype=np.float32)], axis=0))
    out["w3"] = prep_w3(np.concatenate(
        [np.asarray(inp["L0_ffn_Ws3"], dtype=np.float32)[None],
         np.asarray(inp["L0_ffn_W3"], dtype=np.float32)], axis=0))

    out["idn"] = np.eye(128, dtype=np.float32)
    ind = np.zeros((NH, HEC * 128), dtype=np.float32)
    for j in range(HEC * 128):
        ind[j >> 6, j] = 1.0
    out["ind"] = np.ascontiguousarray(ind)
    lt = np.zeros((128, 64), dtype=np.float32)
    for e in range(8):
        lt[:, e * 8:e * 8 + e] = 1.0
    out["lt"] = lt

    per_core = []
    for c in range(NCORES):
        b, s = divmod(c, 4)
        xo = q[b, s * TPC:(s + 1) * TPC, :]
        xp = np.zeros((PRIOR, D), dtype=np.float32)
        pm = np.zeros((PRIOR, 1), dtype=np.float32)
        n_pr = s * TPC
        if n_pr:
            xp[PRIOR - n_pr:, :] = q[b, :n_pr, :]
            pm[PRIOR - n_pr:, :] = 1.0
        xpt = np.ascontiguousarray(
            xp.reshape(NPT, 128, KC, 128).transpose(3, 0, 2, 1).reshape(128, NPT * 1024))
        per_core.append({"x": np.ascontiguousarray(xo), "xp": xp, "xpt": xpt, "pm": pm})
    return out, per_core


class NS:
    pass


def make_kernel(with_attn=True, dbg=(), only=None):
    nc = bass.Bass("TRN2", target_bir_lowering=False)
    S = NS()
    dbg = set(dbg)

    dr = NS()
    dr.x = nc.dram_tensor("x", [TPC, D], F32, kind="ExternalInput").ap()
    dr.xp = nc.dram_tensor("xp", [PRIOR, D], F32, kind="ExternalInput").ap()
    dr.pm = nc.dram_tensor("pm", [PRIOR, 1], F32, kind="ExternalInput").ap()
    dr.xpt = nc.dram_tensor("xpt", [128, NPT * 1024], F32, kind="ExternalInput").ap()
    dr.wqk = nc.dram_tensor("wqk", [128, KC * 32], F32, kind="ExternalInput").ap()
    dr.wv = nc.dram_tensor("wv", [HEC, 128, KC * 128], F32, kind="ExternalInput").ap()
    dr.wvf = nc.dram_tensor("wvf", [128, KC * 1024], F32, kind="ExternalInput").ap()
    dr.wo = nc.dram_tensor("wo", [HEC, 128, D], F32, kind="ExternalInput").ap()
    dr.wg = nc.dram_tensor("wg", [128, KC * 8], F32, kind="ExternalInput").ap()
    dr.gbias = nc.dram_tensor("gbias", [1, 8], F32, kind="ExternalInput").ap()
    dr.w1 = nc.dram_tensor("w1", [NE, NHC, 128, KC * 128], F32, kind="ExternalInput").ap()
    dr.w2 = nc.dram_tensor("w2", [NE, NHC, 128, KC * 128], F32, kind="ExternalInput").ap()
    dr.w3 = nc.dram_tensor("w3", [NE, NHC, 128, D], F32, kind="ExternalInput").ap()
    dr.idn = nc.dram_tensor("idn", [128, 128], F32, kind="ExternalInput").ap()
    dr.ind = nc.dram_tensor("ind", [NH, HEC * 128], F32, kind="ExternalInput").ap()
    dr.lt = nc.dram_tensor("lt", [128, 64], F32, kind="ExternalInput").ap()
    dr.y = nc.dram_tensor("y", [TPC, D], F32, kind="ExternalOutput").ap()
    dbg_dr = {}

    def dbg_out(name, shape):
        if name in dbg:
            dbg_dr[name] = nc.dram_tensor(name, shape, F32, kind="ExternalOutput").ap()

    dbg_out("dbg_xnhi", [128, KC * HALFT])
    dbg_out("dbg_xnlo", [128, KC * HALFT])
    dbg_out("dbg_w", [128, NTAU * 8])
    dbg_out("dbg_acc", [128, NTAU * D])
    dbg_out("dbg_headshi", [128, HEC * QT])
    dbg_out("dbg_carry", [128, 2 * HEC])
    dbg_out("dbg_carryK", [NH, 2])
    dbg_out("dbg_trig", [NH, 4 * QT])
    dbg_out("dbg_wsumT", [128, KC * 32])

    ctx = ExitStack()
    sb = lambda name, shape, dt=F32: ctx.enter_context(
        nc.sbuf_tensor("sb_" + name, shape, dt))

    S.xn_hi = sb("xn_hi", [128, KC * HALFT], F32R)
    S.xn_lo = sb("xn_lo", [128, KC * HALFT], F32R)
    S.acc = sb("acc", [128, NTAU * D])
    S.arena = sb("arena", [128, 7168])
    S.arenb = sb("arenb", [128, 16384], F32R)
    S.ones = sb("ones", [128, QT])
    S.onecol = sb("onecol", [128, 1])
    S.consts = sb("consts", [128, 8])
    S.idn = sb("idn", [128, 128])
    S.lt = sb("lt", [128, 64])
    S.ind = sb("ind", [NH, HEC * 128])
    S.wqk = sb("wqk", [128, KC * 32])
    S.wqk_hi = sb("wqk_hi", [128, KC * 32], F32R)
    S.wqk_lo = sb("wqk_lo", [128, KC * 32], F32R)
    S.wg_hi = sb("wg_hi", [128, KC * 8], F32R)
    S.wg_lo = sb("wg_lo", [128, KC * 8], F32R)
    S.gbias = sb("gbias", [1, 8])
    S.w_all = sb("w_all", [128, NTAU * 8])
    S.carry = sb("carry", [128, 2 * HEC])
    S.carryK = sb("carryK", [NH, 2])
    S.t_cq = sb("t_cq", [NH, QT])
    S.t_sq = sb("t_sq", [NH, QT])
    S.t_ck = sb("t_ck", [NH, QT])
    S.t_sk = sb("t_sk", [NH, QT])
    S.g_s = sb("g_s", [128, 8])
    S.tmp8a = sb("tmp8a", [128, 8])
    S.tmp8b = sb("tmp8b", [128, 8])
    S.rank = sb("rank", [128, 8])
    S.rmask = sb("rmask", [128, 8])
    S.eg = sb("eg", [128, 8])
    S.red1 = sb("red1", [128, 2])
    S.red2 = sb("red2", [128, 2])
    S.ln_m = sb("ln_m", [128, 2])
    S.ln_r = sb("ln_r", [128, 2])
    S.rt_m = sb("rt_m", [128, 1])
    S.rt_s = sb("rt_s", [128, 1])
    S.rt_r = sb("rt_r", [128, 1])
    S.cs_k = sb("cs_k", [128, 64])
    S.cs_a = sb("cs_a", [128, 64])
    S.srow = sb("srow", [128, 16])
    S.negm = sb("negm", [128, 2])
    S.pmask = sb("pmask", [128, 2])

    A = S.arena

    def av(off, p, n, dt=F32):
        if off < 16384:
            ap = S.arenb[0:p, off:off + n]
            if dt is F32:
                ap = ap.bitcast(F32)
        else:
            ap = S.arena[0:p, off - 16384:off - 16384 + n]
            if dt is F32R:
                ap = ap.bitcast(F32R)
        return ap

    OFF = NS()
    OFF.whi = [[0, 1024], [2048, 3072], [4096, 5120]]
    OFF.wlo = [[6144, 7168], [8192, 9216], [10240, 11264]]
    OFF.thi = [12288, 13312]
    OFF.tlo = [14336, 15360]
    OFF.heads_hi = 0
    OFF.heads_lo = 4096
    OFF.wv_hi = [8192, 9216]
    OFF.wv_lo = [10240, 11264]
    OFF.wo_hi = [12288]
    OFF.wo_lo = [13312]
    OFF.wst = [[16384, 17408], [18432, 19456], [20480, 21504]]
    OFF.t2 = 22528
    OFF.xr = [16384, 17408]
    OFF.ob = [18432, 19456]
    OFF.wv_st = [16384, 17408]
    OFF.wo_st = [18432]
    OFF.bc_c = 19456
    OFF.bc_s = 19968
    OFF.bc_cq = 18432
    OFF.bc_sq = 18944
    OFF.kv_c = 20480
    OFF.kv_s = 20992
    OFF.cum_c = 21504
    OFF.cum_s = 22016
    OFF.p1_x = [16384, 17408]
    OFF.p1_tm = [18432, 19456]
    OFF.p0_x = [16384, 17408]
    OFF.p0_xt = [18432, 19456]
    OFF.p0_wvst = [20480]
    OFF.wsumT = 18432
    OFF.p0_wsc = 21504
    OFF.p0_wss = 22528
    OFF.p0_sq = 22528
    OFF.ln_sq = 22528
    OFF.t_den = 22528

    sems = {}
    for sname in ("sL", "sT", "sV", "sS", "sG", "sI", "sW0", "sW1", "sA", "sB",
                  "sWV0", "sWV1", "sWO", "sXR0", "sXR1", "sY0", "sY1"):
        sems[sname] = ctx.enter_context(nc.semaphore(sname))

    P = Prog()

    def ph_init(P):
        for nm in ("idn", "ind", "lt", "wqk", "gbias"):
            P.op("sync", lambda e, nm=nm: e.dma_start(out=getattr(S, nm)[:, :],
                                                      in_=getattr(dr, nm)),
                 dn=16, tag=f"ini.{nm}", chain=False, sem="sI")
        P.op("vector", lambda e: e.memset(S.ones[:, :], 1.0))
        P.op("vector", lambda e: e.memset(S.onecol[:, :], 1.0))
        P.op("vector", lambda e: e.memset(S.consts[:, C_PI2:C_PI2 + 1], PI / 2))
        P.op("vector", lambda e: e.memset(S.consts[:, C_PI4:C_PI4 + 1], PI / 4))
        P.op("vector", lambda e: e.memset(S.consts[:, C_I1024:C_I1024 + 1], 1.0 / 1024.0))
        P.op("vector", lambda e: e.memset(S.consts[:, C_EPSLN:C_EPSLN + 1], 1e-5))
        P.op("vector", lambda e: e.memset(S.consts[:, C_EPSAT:C_EPSAT + 1], 1e-6))
        P.op("vector", lambda e: e.memset(S.consts[:, C_35:C_35 + 1], 3.5))
        P.op("vector", lambda e: e.memset(S.consts[:, C_NI1024:C_NI1024 + 1], -1.0 / 1024.0))
        P.op("vector", lambda e: e.memset(S.carry[:, :], 0.0))
        P.op("vector", lambda e: e.memset(S.carryK[:, :], 0.0))
        P.op("sync", lambda e: e.dma_start(out=av(16384, 128, KC * 8), in_=dr.wg),
             dn=16, tag="ini.wg", chain=False, sem="sI")
        P.after("ini.gbias", "ini.wg")
        P.op("vector", lambda e: e.tensor_copy(S.wqk_hi[:, :], S.wqk[:, :]))
        P.op("vector", lambda e: e.tensor_sub(S.wqk_lo[:, :], S.wqk[:, :], S.wqk_hi[:, :]))
        P.op("vector", lambda e: e.tensor_copy(S.wg_hi[:, :], av(16384, 128, KC * 8)))
        P.op("vector", lambda e: e.tensor_sub(S.wg_lo[:, :], av(16384, 128, KC * 8), S.wg_hi[:, :]))
        for k in range(KC):
            P.op("tensor", lambda e, k=k: e.matmul(
                S.ps_srow[:, :], S.ones[0:128, 0:128],
                S.wqk[:, k * 32 + 16:k * 32 + 32],
                start=(k == 0), stop=(k == KC - 1)))
        P.op("scalar", lambda e: e.copy(S.srow[:, :], S.ps_srow[:, :]))

    def ph_p0(P):
        vprev = [None]
        sprev = [None]

        def vop(builder, waits=(), tag=None):
            w = list(waits) + ([vprev[0]] if vprev[0] else [])
            t = P.op("vector", builder, waits=w, chain=False, tag=tag)
            vprev[0] = t
            return t

        def sop(builder, waits=(), tag=None):
            w = list(waits) + ([sprev[0]] if sprev[0] else [])
            t = P.op("scalar", builder, waits=w, chain=False, tag=tag)
            sprev[0] = t
            return t

        for i in range(min(2, NPT)):
            P.op("sync", lambda e, i=i: e.dma_start(
                out=av(OFF.p0_x[i % 2], 128, 1024),
                in_=dr.xp[i * 128:(i + 1) * 128, :]), dn=16,
                tag=f"p0.x.{i}", chain=False, sem=("sA", "sB")[i % 2])
            P.op("sync", lambda e, i=i: e.dma_start(
                out=av(OFF.p0_xt[i % 2], 128, 1024),
                in_=dr.xpt[:, i * 1024:(i + 1) * 1024]), dn=16,
                tag=f"p0.xt.{i}", chain=False, sem=("sA", "sB")[i % 2])
            P.op("sync", lambda e, i=i: e.dma_start(
                out=S.pmask[:, i % 2:i % 2 + 1],
                in_=dr.pm[i * 128:(i + 1) * 128, :]), dn=16, tag=f"p0.pm.{i}",
                chain=False, sem=("sA", "sB")[i % 2])
        for i in range(NPT):
            q2 = i % 2
            xs = lambda i=i: av(OFF.p0_x[i % 2], 128, 1024)
            vop(lambda e, i=i: e.tensor_mul(av(OFF.p0_sq, 128, 1024), xs(i), xs(i)),
                waits=[f"p0.x.{i}", f"p0.xt.{i}", f"p0.pm.{i}"])
            vop(lambda e, i=i, q2=q2: e.reduce_sum(S.red1[:, q2:q2 + 1], xs(i), axis=AX),
                tag=f"p0.vr1.{i}")
            vop(lambda e, q2=q2: e.reduce_sum(S.red2[:, q2:q2 + 1],
                                              av(OFF.p0_sq, 128, 1024), axis=AX),
                tag=f"p0.vr2.{i}")
            sop(lambda e, q2=q2: e.activation(
                S.negm[:, q2:q2 + 1], S.red1[:, q2:q2 + 1], ACT.Copy,
                scale=S.consts[:, C_NI1024:C_NI1024 + 1]), waits=[f"p0.vr1.{i}"])
            sop(lambda e, q2=q2: e.activation(S.ln_m[:, q2:q2 + 1],
                                              S.negm[:, q2:q2 + 1], ACT.Square),
                tag=f"p0.a1.{i}")
            vop(lambda e, q2=q2: e.tensor_scalar(
                S.red2[:, q2:q2 + 1], S.red2[:, q2:q2 + 1],
                S.consts[:, C_I1024:C_I1024 + 1], S.ln_m[:, q2:q2 + 1],
                ALU.mult, ALU.subtract), waits=[f"p0.a1.{i}"], tag=f"p0.vv.{i}")
            sop(lambda e, q2=q2: e.activation(S.ln_r[:, q2:q2 + 1], S.red2[:, q2:q2 + 1],
                                              ACT.Sqrt,
                                              bias=S.consts[:, C_EPSLN:C_EPSLN + 1]),
                waits=[f"p0.vv.{i}"], tag=f"p0.a2.{i}")
            vop(lambda e, q2=q2: e.reciprocal(S.ln_r[:, q2:q2 + 1], S.ln_r[:, q2:q2 + 1]),
                waits=[f"p0.a2.{i}"])
            kw = [f"p0.x.{i}", f"p0.xt.{i}", f"p0.pm.{i}"]
            if i >= 2:
                kw.append(f"p0.kc.{i - 2}")
            for k in range(KC):
                P.op("tensor", lambda e, k=k, i=i, q2=q2: e.matmul(
                    S.ps_k[q2][:, :], av(OFF.p0_xt[i % 2] + k * 128, 128, 128),
                    S.wqk[:, k * 32 + 16:k * 32 + 32],
                    start=(k == 0), stop=(k == KC - 1)),
                    waits=(kw if k == 0 else ()), chain=False,
                    tag=(f"p0.kraw.{i}" if k == KC - 1 else None))
            vop(lambda e, q2=q2: e.scalar_tensor_tensor(
                S.cs_k[:, q2 * 32:q2 * 32 + 16], S.srow[:, :], S.negm[:, q2:q2 + 1],
                S.ps_k[q2][:, :], ALU.mult, ALU.add), waits=[f"p0.kraw.{i}"])
            vop(lambda e, q2=q2: e.tensor_scalar(
                S.cs_k[:, q2 * 32:q2 * 32 + 16], S.cs_k[:, q2 * 32:q2 * 32 + 16],
                S.ln_r[:, q2:q2 + 1], None, ALU.mult), tag=f"p0.kc.{i}")
            sop(lambda e, q2=q2: e.activation(S.cs_k[:, q2 * 32:q2 * 32 + 16],
                                              S.cs_k[:, q2 * 32:q2 * 32 + 16], ACT.Tanh),
                waits=[f"p0.kc.{i}"])
            sop(lambda e, q2=q2: e.activation(
                S.cs_k[:, q2 * 32 + 16:q2 * 32 + 32], S.cs_k[:, q2 * 32:q2 * 32 + 16],
                ACT.Sin, scale=S.consts[:, C_PI4:C_PI4 + 1]))
            sop(lambda e, q2=q2: e.activation(
                S.cs_k[:, q2 * 32:q2 * 32 + 16], S.cs_k[:, q2 * 32:q2 * 32 + 16],
                ACT.Sin, bias=S.consts[:, C_PI2:C_PI2 + 1],
                scale=S.consts[:, C_PI4:C_PI4 + 1]), tag=f"p0.trig.{i}")
            vop(lambda e, i=i, q2=q2: e.tensor_scalar(
                S.cs_k[:, q2 * 32:q2 * 32 + 32], S.cs_k[:, q2 * 32:q2 * 32 + 32],
                S.pmask[:, i % 2:i % 2 + 1], None, ALU.mult),
                waits=[f"p0.trig.{i}"], tag=f"p0.msk.{i}")
            vop(lambda e, q2=q2: e.tensor_scalar(
                S.cs_a[:, q2 * 32:q2 * 32 + 32], S.cs_k[:, q2 * 32:q2 * 32 + 32],
                S.ln_r[:, q2:q2 + 1], None, ALU.mult), tag=f"p0.a.{i}")
            for n in range(2):
                P.op("tensor", lambda e, n=n, i=i, q2=q2: e.matmul(
                    S.ps_wsc[:, n * 512:(n + 1) * 512], S.cs_a[:, q2 * 32:q2 * 32 + 16],
                    av(OFF.p0_x[i % 2] + n * 512, 128, 512),
                    start=(i == 0), stop=(i == NPT - 1)),
                    waits=([f"p0.a.{i}"] if n == 0 else ()), chain=False)
                P.op("tensor", lambda e, n=n, i=i, q2=q2: e.matmul(
                    S.ps_wss[:, n * 512:(n + 1) * 512],
                    S.cs_a[:, q2 * 32 + 16:q2 * 32 + 32],
                    av(OFF.p0_x[i % 2] + n * 512, 128, 512),
                    start=(i == 0), stop=(i == NPT - 1)), chain=False)
            P.op("tensor", lambda e, i=i, q2=q2: e.matmul(
                S.ps_kk[0:16, 0:1], S.cs_k[:, q2 * 32:q2 * 32 + 16], S.onecol[:, :],
                start=(i == 0), stop=False), chain=False)
            P.op("tensor", lambda e, i=i, q2=q2: e.matmul(
                S.ps_kk[0:16, 1:2], S.cs_k[:, q2 * 32 + 16:q2 * 32 + 32], S.onecol[:, :],
                start=False, stop=(i == NPT - 1)), chain=False)
            P.op("tensor", lambda e, i=i, q2=q2: e.matmul(
                S.ps_cc[0:16, 0:1], S.cs_a[:, q2 * 32:q2 * 32 + 16], S.negm[:, q2:q2 + 1],
                start=(i == 0), stop=False), chain=False)
            P.op("tensor", lambda e, i=i, q2=q2: e.matmul(
                S.ps_cc[0:16, 1:2], S.cs_a[:, q2 * 32 + 16:q2 * 32 + 32],
                S.negm[:, q2:q2 + 1],
                start=False, stop=(i == NPT - 1)), chain=False, tag=f"p0.wsum.{i}")
            if i + 2 < NPT:
                P.op("sync", lambda e, i=i: e.dma_start(
                    out=av(OFF.p0_x[i % 2], 128, 1024),
                    in_=dr.xp[(i + 2) * 128:(i + 3) * 128, :]), dn=16,
                    tag=f"p0.x.{i + 2}", chain=False, waits=[f"p0.wsum.{i}"],
                    sem=("sA", "sB")[i % 2])
                P.op("sync", lambda e, i=i: e.dma_start(
                    out=av(OFF.p0_xt[i % 2], 128, 1024),
                    in_=dr.xpt[:, (i + 2) * 1024:(i + 3) * 1024]), dn=16,
                    tag=f"p0.xt.{i + 2}", chain=False, waits=[f"p0.kraw.{i}"],
                    sem=("sA", "sB")[i % 2])
                P.op("sync", lambda e, i=i: e.dma_start(
                    out=S.pmask[:, i % 2:i % 2 + 1],
                    in_=dr.pm[(i + 2) * 128:(i + 3) * 128, :]), dn=16,
                    tag=f"p0.pm.{i + 2}", chain=False, waits=[f"p0.msk.{i}"],
                    sem=("sA", "sB")[i % 2])
        P.reset_chain()
        P.after(f"p0.wsum.{NPT - 1}", f"p0.a.{NPT - 1}")
        P.op("scalar", lambda e: e.copy(S.carryK[:, 0:1], S.ps_kk[0:16, 0:1]))
        P.op("scalar", lambda e: e.copy(S.carryK[:, 1:2], S.ps_kk[0:16, 1:2]))
        P.op("scalar", lambda e: e.copy(S.cs_k[0:16, 0:2], S.ps_cc[0:16, 0:2]))
        P.op("scalar", lambda e: e.copy(av(OFF.p0_wsc, NH, D), S.ps_wsc[:, :]))
        P.op("scalar", lambda e: e.copy(av(OFF.p0_wss, NH, D), S.ps_wss[:, :]))
        P.op("vector", lambda e: e.tensor_scalar(
            av(OFF.p0_wsc, NH, D), av(OFF.p0_wsc, NH, D), S.cs_k[0:16, 0:1], None,
            ALU.add))
        P.op("vector", lambda e: e.tensor_scalar(
            av(OFF.p0_wss, NH, D), av(OFF.p0_wss, NH, D), S.cs_k[0:16, 1:2], None,
            ALU.add))
        for k in range(KC):
            P.op("tensor", lambda e, k=k: e.transpose(
                S.ps_trc[:, :], av(OFF.p0_wsc + k * 128, NH, 128), S.idn[0:16, 0:16]))
            P.op("scalar", lambda e, k=k: e.copy(
                S.arena[0:128, OFF.wsumT - 16384 + k * 32:OFF.wsumT - 16384 + k * 32 + 32:2],
                S.ps_trc[:, 0:16]))
            P.op("tensor", lambda e, k=k: e.transpose(
                S.ps_trs[:, :], av(OFF.p0_wss + k * 128, NH, 128), S.idn[0:16, 0:16]))
            P.op("scalar", lambda e, k=k: e.copy(
                S.arena[0:128, OFF.wsumT - 16384 + k * 32 + 1:OFF.wsumT - 16384 + k * 32 + 33:2],
                S.ps_trs[:, 0:16]))
        P.op("sync", lambda e: e.dma_start(
            out=av(OFF.p0_wvst[0], 128, 1024),
            in_=dr.wvf[:, 0:1024]), dn=16, tag="p0.wv.0", chain=False, sem="sWO")
        for k in range(KC):
            P.after(f"p0.wv.{k}")
            for h in range(NH):
                P.op("tensor", lambda e, k=k, h=h: e.matmul(
                    S.ps_car2[0:2, h * 64:(h + 1) * 64],
                    av(OFF.wsumT + k * 32 + 2 * h, 128, 2),
                    av(OFF.p0_wvst[0] + h * 64, 128, 64),
                    start=(k == 0 and h % 8 == 0), stop=(k == KC - 1 and h % 8 == 7)),
                    tag=(f"p0.carmm.{k}" if h == NH - 1 else None))
            if k + 1 < KC:
                P.op("sync", lambda e, k=k: e.dma_start(
                    out=av(OFF.p0_wvst[0], 128, 1024),
                    in_=dr.wvf[:, (k + 1) * 1024:(k + 2) * 1024]), dn=16,
                    tag=f"p0.wv.{k + 1}", chain=False, waits=[f"p0.carmm.{k}"],
                    sem="sWO")
        for h in range(NH):
            P.op("scalar", lambda e, h=h: e.copy(av(16384, 2, 64),
                                                 S.ps_car2[0:2, h * 64:(h + 1) * 64]))
            P.op("tensor", lambda e: e.transpose(
                S.ps_ctr[:, :], av(16384, 2, 64), S.idn[0:2, 0:2]))
            P.op("scalar", lambda e: e.copy(av(16448, 64, 2), S.ps_ctr[:, :]))
            hec, off = h // 2, (h % 2) * 64
            P.op("gpsimd", lambda e, hec=hec, off=off: e.dma_start(
                out=S.carry[off:off + 64, hec:hec + 1], in_=av(16448, 64, 2)[:, 0:1]), dn=16,
                tag=f"p0.cdma.{h}.c")
            P.op("gpsimd", lambda e, hec=hec, off=off: e.dma_start(
                out=S.carry[off:off + 64, HEC + hec:HEC + hec + 1],
                in_=av(16448, 64, 2)[:, 1:2]), dn=16, tag=f"p0.cdma.{h}.s")
        P.wait("gpsimd", f"p0.cdma.{NH - 1}.s")
        if "dbg_wsumT" in dbg:
            P.op("gpsimd", lambda e: e.dma_start(
                out=dbg_dr["dbg_wsumT"], in_=av(OFF.wsumT, 128, KC * 32)), dn=16)
        if "dbg_carry" in dbg:
            P.op("gpsimd", lambda e: e.dma_start(
                out=dbg_dr["dbg_carry"], in_=S.carry[:, :]), dn=16)
        if "dbg_carryK" in dbg:
            P.op("gpsimd", lambda e: e.dma_start(
                out=dbg_dr["dbg_carryK"], in_=S.carryK[:, :]), dn=16)

    def ph_p1a(P, hh):
        pfx = f"a{hh}"
        vprev = [None]
        sprev = [None]

        def vop(builder, waits=(), tag=None):
            w = list(waits) + ([vprev[0]] if vprev[0] else [])
            t = P.op("vector", builder, waits=w, chain=False, tag=tag)
            vprev[0] = t
            return t

        def sop(builder, waits=(), tag=None):
            w = list(waits) + ([sprev[0]] if sprev[0] else [])
            t = P.op("scalar", builder, waits=w, chain=False, tag=tag)
            sprev[0] = t
            return t

        for t in range(min(2, NTAU)):
            P.op("sync", lambda e, t=t: e.dma_start(
                out=av(OFF.p1_x[t % 2], 128, 1024),
                in_=dr.x[(hh * NTAU + t) * 128:(hh * NTAU + t + 1) * 128, :]),
                dn=16, tag=f"{pfx}.x.{t}", chain=False, sem=("sA", "sB")[t % 2])
        def emit_ln(t):
            q2 = t % 2
            xs = lambda t=t: av(OFF.p1_x[t % 2], 128, 1024)
            tm = lambda t=t: av(OFF.p1_tm[t % 2], 128, 1024)
            vop(lambda e, t=t: e.tensor_mul(av(OFF.ln_sq, 128, 1024), xs(t), xs(t)),
                waits=[f"{pfx}.x.{t}"])
            vop(lambda e, t=t, q2=q2: e.reduce_sum(S.red1[:, q2:q2 + 1], xs(t), axis=AX),
                tag=f"{pfx}.vr1.{t}")
            vop(lambda e, q2=q2: e.reduce_sum(S.red2[:, q2:q2 + 1],
                                              av(OFF.ln_sq, 128, 1024), axis=AX),
                tag=f"{pfx}.vr2.{t}")
            sop(lambda e, q2=q2: e.activation(
                S.negm[:, q2:q2 + 1], S.red1[:, q2:q2 + 1], ACT.Copy,
                scale=S.consts[:, C_NI1024:C_NI1024 + 1]), waits=[f"{pfx}.vr1.{t}"])
            sop(lambda e, q2=q2: e.activation(S.ln_m[:, q2:q2 + 1],
                                              S.negm[:, q2:q2 + 1], ACT.Square),
                tag=f"{pfx}.a1.{t}")
            vop(lambda e, q2=q2: e.tensor_scalar(
                S.red2[:, q2:q2 + 1], S.red2[:, q2:q2 + 1],
                S.consts[:, C_I1024:C_I1024 + 1], S.ln_m[:, q2:q2 + 1],
                ALU.mult, ALU.subtract), waits=[f"{pfx}.a1.{t}"],
                tag=f"{pfx}.vv.{t}")
            sop(lambda e, q2=q2: e.activation(S.ln_r[:, q2:q2 + 1], S.red2[:, q2:q2 + 1],
                                              ACT.Sqrt,
                                              bias=S.consts[:, C_EPSLN:C_EPSLN + 1]),
                waits=[f"{pfx}.vv.{t}"], tag=f"{pfx}.a2.{t}")
            vop(lambda e, q2=q2: e.reciprocal(S.ln_r[:, q2:q2 + 1], S.ln_r[:, q2:q2 + 1]),
                waits=[f"{pfx}.a2.{t}"])
            nw = [f"{pfx}.trg.{t - 2}"] if t >= 2 else []
            vop(lambda e, t=t, q2=q2: e.tensor_scalar(
                tm(t), xs(t), S.negm[:, q2:q2 + 1], S.ln_r[:, q2:q2 + 1],
                ALU.add, ALU.mult), waits=nw, tag=f"{pfx}.nrm.{t}")
            if t + 2 < NTAU:
                P.op("sync", lambda e, t=t: e.dma_start(
                    out=av(OFF.p1_x[(t + 2) % 2], 128, 1024),
                    in_=dr.x[(hh * NTAU + t + 2) * 128:(hh * NTAU + t + 3) * 128, :]),
                    dn=16, tag=f"{pfx}.x.{t + 2}", chain=False,
                    waits=[f"{pfx}.nrm.{t}"], sem=("sA", "sB")[t % 2])

        def emit_gate(t):
            gw = [f"{pfx}.lo.{t}"]
            if t >= 1:
                gw.append(f"{pfx}.gsev.{t - 1}")
            first = True
            for k in range(KC):
                for pi_, (lh, rh) in enumerate((
                        (S.xn_hi, S.wg_hi), (S.xn_hi, S.wg_lo), (S.xn_lo, S.wg_hi))):
                    P.op("tensor", lambda e, k=k, t=t, lh=lh, rh=rh,
                         st=(k == 0 and pi_ == 0): e.matmul(
                        S.ps_g[:, :],
                        lh[:, k * 1024 + t * 128:k * 1024 + t * 128 + 128],
                        rh[:, k * 8:(k + 1) * 8], start=st, stop=False),
                        waits=(gw if first else ()), chain=False)
                    first = False
            P.op("tensor", lambda e: e.matmul(
                S.ps_g[:, :], S.ones[0:1, 0:128], S.gbias[:, :], start=False, stop=True),
                chain=False, tag=f"{pfx}.gmm.{t}")
            sop(lambda e: e.copy(S.g_s[:, :], S.ps_g[:, :]),
                waits=[f"{pfx}.gmm.{t}"], tag=f"{pfx}.gsev.{t}")
            emit_routing(t)

        def emit_routing(t):
            for eix in range(8):
                vop(lambda e, eix=eix: e.scalar_tensor_tensor(
                    S.tmp8b[:, :], S.g_s[:, :], S.g_s[:, eix:eix + 1],
                    S.lt[:, eix * 8:eix * 8 + 8], ALU.is_equal, ALU.mult),
                    waits=([f"{pfx}.gsev.{t}"] if eix == 0 else ()))
                vop(lambda e, eix=eix: e.scalar_tensor_tensor(
                    S.tmp8a[:, :], S.g_s[:, :], S.g_s[:, eix:eix + 1],
                    S.tmp8b[:, :], ALU.is_gt, ALU.add))
                vop(lambda e, eix=eix: e.reduce_sum(
                    S.rank[:, eix:eix + 1], S.tmp8a[:, :], axis=AX))
            vop(lambda e: e.tensor_scalar(
                S.rmask[:, :], S.rank[:, :], S.consts[:, C_35:C_35 + 1], None, ALU.is_le))
            vop(lambda e: e.reduce_max(S.rt_m[:, :], S.g_s[:, :], axis=AX, negate=True),
                tag=f"{pfx}.rt1.{t}")
            sop(lambda e: e.activation(S.eg[:, :], S.g_s[:, :], ACT.Exp,
                                       bias=S.rt_m[:, 0:1]),
                waits=[f"{pfx}.rt1.{t}"], tag=f"{pfx}.exp.{t}")
            vop(lambda e: e.tensor_mul(S.eg[:, :], S.eg[:, :], S.rmask[:, :]),
                waits=[f"{pfx}.exp.{t}"])
            vop(lambda e: e.reduce_sum(S.rt_s[:, :], S.eg[:, :], axis=AX))
            vop(lambda e: e.reciprocal(S.rt_r[:, 0:1], S.rt_s[:, 0:1]))
            vop(lambda e, t=t: e.tensor_scalar(
                S.w_all[:, t * 8:t * 8 + 8], S.eg[:, :], S.rt_r[:, 0:1], None, ALU.mult),
                tag=f"{pfx}.w.{t}")

        emit_ln(0)
        for t in range(NTAU):
            q2 = t % 2
            xs = lambda t=t: av(OFF.p1_x[t % 2], 128, 1024)
            tm = lambda t=t: av(OFF.p1_tm[t % 2], 128, 1024)
            tw = [f"{pfx}.nrm.{t}"]
            if t >= 1:
                tw.append(f"{pfx}.lo.{t - 1}")
            for k in range(KC):
                P.op("tensor", lambda e, k=k, t=t: e.transpose(
                    S.ps_tr8[:, k * 128:(k + 1) * 128],
                    av(OFF.p1_tm[t % 2] + k * 128, 128, 128), S.idn[:, :]),
                    waits=(tw if k == 0 else ()), chain=False,
                    tag=(f"{pfx}.trg.{t}" if k == KC - 1 else None))
            if t >= 1:
                emit_gate(t - 1)
            for k in range(KC):
                sop(lambda e, k=k, t=t: e.copy(
                    S.xn_hi[:, k * 1024 + t * 128:k * 1024 + t * 128 + 128],
                    S.ps_tr8[:, k * 128:(k + 1) * 128]),
                    waits=([f"{pfx}.trg.{t}"] if k == 0 else ()),
                    tag=(f"{pfx}.heg.{t}" if k == KC - 1 else None))
            for k in range(KC):
                vop(lambda e, k=k, t=t: e.tensor_sub(
                    S.xn_lo[:, k * 1024 + t * 128:k * 1024 + t * 128 + 128],
                    S.ps_tr8[:, k * 128:(k + 1) * 128],
                    S.xn_hi[:, k * 1024 + t * 128:k * 1024 + t * 128 + 128]),
                    waits=([f"{pfx}.heg.{t}"] if k == 0 else ()),
                    tag=(f"{pfx}.lo.{t}" if k == KC - 1 else None))
            if t + 1 < NTAU:
                emit_ln(t + 1)

        emit_gate(NTAU - 1)
        P.reset_chain()
        P.after(f"{pfx}.w.{NTAU - 1}", f"{pfx}.lo.{NTAU - 1}")
        if not with_attn:
            P.op("vector", lambda e: e.memset(S.acc[:, :], 0.0))
        if "dbg_xnhi" in dbg and hh == 0:
            P.op("gpsimd", lambda e: e.dma_start(
                out=dbg_dr["dbg_xnhi"], in_=S.xn_hi[:, :].bitcast(F32)), dn=16)
            P.op("gpsimd", lambda e: e.dma_start(
                out=dbg_dr["dbg_xnlo"], in_=S.xn_lo[:, :].bitcast(F32)), dn=16)
        if "dbg_w" in dbg and hh == 0:
            P.op("gpsimd", lambda e: e.dma_start(
                out=dbg_dr["dbg_w"], in_=S.w_all[:, :]), dn=16)

    def ph_atn(P, q):
        hh, qh = divmod(q, 2)
        tb = qh * QT
        pfx = f"q{q}"
        for hec in range(min(2, HEC)):
            P.op("sync", lambda e, hec=hec: e.dma_start(
                out=av(OFF.wv_st[hec % 2], 128, 1024), in_=dr.wv[hec, :, :]),
                dn=16, tag=f"{pfx}.wv.{hec}", chain=False, sem=("sWV0", "sWV1")[hec % 2])
        for (psn, col) in (("ps_q", 0), ("ps_k16", 16)):
            for k in range(KC):
                for pi_ in range(3):
                    lh = S.wqk_hi if pi_ != 2 else S.wqk_lo
                    rh = S.xn_hi if pi_ != 1 else S.xn_lo
                    P.op("tensor", lambda e, k=k, psn=psn, col=col, lh=lh, rh=rh,
                         st=(k == 0 and pi_ == 0), sp=(k == KC - 1 and pi_ == 2):
                         e.matmul(getattr(S, psn)[:, :],
                                  lh[:, k * 32 + col:k * 32 + col + 16],
                                  rh[:, k * 1024 + tb:k * 1024 + tb + QT],
                                  start=st, stop=sp))
        P.op("scalar", lambda e: e.activation(av(OFF.t_den, NH, QT), S.ps_q[:, :], ACT.Tanh))
        P.op("scalar", lambda e: e.activation(S.t_cq[:, :], av(OFF.t_den, NH, QT), ACT.Sin,
                                              bias=S.consts[0:16, C_PI2:C_PI2 + 1],
                                              scale=S.consts[0:16, C_PI4:C_PI4 + 1]))
        P.op("scalar", lambda e: e.activation(S.t_sq[:, :], av(OFF.t_den, NH, QT), ACT.Sin,
                                              scale=S.consts[0:16, C_PI4:C_PI4 + 1]))
        P.op("scalar", lambda e: e.activation(av(OFF.t_den, NH, QT), S.ps_k16[:, :], ACT.Tanh))
        P.op("scalar", lambda e: e.activation(S.t_ck[:, :], av(OFF.t_den, NH, QT), ACT.Sin,
                                              bias=S.consts[0:16, C_PI2:C_PI2 + 1],
                                              scale=S.consts[0:16, C_PI4:C_PI4 + 1]))
        P.op("scalar", lambda e: e.activation(S.t_sk[:, :], av(OFF.t_den, NH, QT), ACT.Sin,
                                              scale=S.consts[0:16, C_PI4:C_PI4 + 1]))
        P.op("vector", lambda e: e.tensor_tensor_scan(
            av(OFF.cum_c, NH, QT), S.ones[0:NH, 0:QT], S.t_ck[:, :], S.carryK[:, 0:1],
            ALU.mult, ALU.add))
        P.op("vector", lambda e: e.tensor_copy(S.carryK[:, 0:1], av(OFF.cum_c + QT - 1, NH, 1)))
        P.op("vector", lambda e: e.tensor_tensor_scan(
            av(OFF.cum_s, NH, QT), S.ones[0:NH, 0:QT], S.t_sk[:, :], S.carryK[:, 1:2],
            ALU.mult, ALU.add))
        P.op("vector", lambda e: e.tensor_copy(S.carryK[:, 1:2], av(OFF.cum_s + QT - 1, NH, 1)))
        P.op("vector", lambda e: e.tensor_mul(av(OFF.t_den, NH, QT), S.t_cq[:, :], av(OFF.cum_c, NH, QT)))
        P.op("vector", lambda e: e.tensor_mul(av(OFF.cum_c, NH, QT), S.t_sq[:, :],
                                              av(OFF.cum_s, NH, QT)))
        P.op("vector", lambda e: e.tensor_add(av(OFF.t_den, NH, QT), av(OFF.t_den, NH, QT),
                                              av(OFF.cum_c, NH, QT)))
        P.op("vector", lambda e: e.tensor_scalar(
            av(OFF.t_den, NH, QT), av(OFF.t_den, NH, QT), S.consts[0:16, C_EPSAT:C_EPSAT + 1], None,
            ALU.add))
        P.op("vector", lambda e: e.reciprocal(av(OFF.t_den, NH, QT), av(OFF.t_den, NH, QT)))
        P.op("vector", lambda e: e.tensor_mul(S.t_cq[:, :], S.t_cq[:, :], av(OFF.t_den, NH, QT)))
        P.op("vector", lambda e: e.tensor_mul(S.t_sq[:, :], S.t_sq[:, :], av(OFF.t_den, NH, QT)),
             tag=f"{pfx}.trig")
        if "dbg_trig" in dbg and q == 0:
            for j, srct in enumerate((S.t_cq, S.t_sq, S.t_ck, S.t_sk)):
                P.op("gpsimd", lambda e, j=j, srct=srct: e.dma_start(
                    out=dbg_dr["dbg_trig"][:, j * QT:(j + 1) * QT], in_=srct[:, :]), dn=16)
        P.reset_chain()
        vprev = [f"{pfx}.trig"]

        def vop(builder, waits=(), tag=None):
            t = P.op("vector", builder, waits=list(waits) + vprev, chain=False, tag=tag)
            vprev[0] = t
            return t

        for hec in range(HEC):
            st_off = OFF.wv_st[hec % 2]
            hi_off = OFF.wv_hi[hec % 2]
            lo_off = OFF.wv_lo[hec % 2]
            sw = [f"{pfx}.wv.{hec}"]
            if hec >= 2:
                sw.append(f"{pfx}.vp.{hec - 2}")
            P.op("scalar", lambda e, st_off=st_off, hi_off=hi_off: e.copy(
                av(hi_off, 128, 1024, F32R), av(st_off, 128, 1024)),
                waits=sw, chain=False, tag=f"{pfx}.wvh.{hec}")
            vop(lambda e, st_off=st_off, hi_off=hi_off, lo_off=lo_off:
                e.tensor_sub(av(lo_off, 128, 1024, F32R), av(st_off, 128, 1024),
                             av(hi_off, 128, 1024, F32R)),
                waits=[f"{pfx}.wvh.{hec}"], tag=f"{pfx}.wvs.{hec}")
            vw = [f"{pfx}.wvs.{hec}"]
            if hec >= 2:
                vw.append(f"{pfx}.kv.{hec - 2}")
            first = True
            for k in range(KC):
                for pi_ in range(3):
                    lh = av((hi_off if pi_ < 2 else lo_off) + k * 128, 128, 128, F32R)
                    rh = (S.xn_hi if pi_ != 1 else S.xn_lo)
                    P.op("tensor", lambda e, k=k, lh=lh, rh=rh, hec=hec,
                         st=(k == 0 and pi_ == 0), sp=(k == KC - 1 and pi_ == 2):
                         e.matmul(S.ps_v[hec % 2][:, :], lh,
                                  rh[:, k * 1024 + tb:k * 1024 + tb + QT],
                                  start=st, stop=sp),
                         waits=(vw if first else ()), chain=False,
                         tag=(f"{pfx}.vp.{hec}" if (k == KC - 1 and pi_ == 2) else None))
                    first = False
            if hec + 2 < HEC:
                P.op("sync", lambda e, hec=hec: e.dma_start(
                    out=av(OFF.wv_st[hec % 2], 128, 1024), in_=dr.wv[hec + 2, :, :]),
                    dn=16, tag=f"{pfx}.wv.{hec + 2}", chain=False,
                    waits=[f"{pfx}.vp.{hec}"], sem=("sWV0", "sWV1")[hec % 2])
            for j, (trg, evdst) in enumerate(((S.t_ck, OFF.bc_c), (S.t_sk, OFF.bc_s),
                                              (S.t_cq, OFF.bc_cq), (S.t_sq, OFF.bc_sq))):
                bw = [f"{pfx}.trig"] if hec == 0 else [f"{pfx}.ev.{hec - 1}.{j}"]
                P.op("tensor", lambda e, hec=hec, trg=trg, j=j: e.matmul(
                    S.ps_bc4[j][:, :], S.ind[:, hec * 128:(hec + 1) * 128], trg[:, :],
                    start=True, stop=True),
                    waits=bw, chain=False, tag=f"{pfx}.bc.{hec}.{j}")
                ew = [f"{pfx}.bc.{hec}.{j}"]
                if hec >= 1:
                    ew.append(f"{pfx}.bcr.{hec - 1}.{j}")
                P.op("scalar", lambda e, evdst=evdst, j=j: e.copy(
                    av(evdst, 128, QT), S.ps_bc4[j][:, :]),
                    waits=ew, chain=False, tag=f"{pfx}.ev.{hec}.{j}")
            vop(lambda e, hec=hec: e.tensor_mul(
                av(OFF.kv_c, 128, QT), S.ps_v[hec % 2][:, :], av(OFF.bc_c, 128, QT)),
                waits=[f"{pfx}.vp.{hec}", f"{pfx}.ev.{hec}.0"],
                tag=f"{pfx}.bcr.{hec}.0")
            vop(lambda e, hec=hec: e.tensor_mul(
                av(OFF.kv_s, 128, QT), S.ps_v[hec % 2][:, :], av(OFF.bc_s, 128, QT)),
                waits=[f"{pfx}.ev.{hec}.1"], tag=f"{pfx}.kv.{hec}")
            vop(lambda e, hec=hec: e.tensor_tensor_scan(
                av(OFF.cum_c, 128, QT), S.ones[:, :], av(OFF.kv_c, 128, QT),
                S.carry[:, hec:hec + 1], ALU.mult, ALU.add),
                tag=f"{pfx}.bcr.{hec}.1")
            vop(lambda e, hec=hec: e.tensor_copy(
                S.carry[:, hec:hec + 1], av(OFF.cum_c + QT - 1, 128, 1)))
            vop(lambda e, hec=hec: e.tensor_tensor_scan(
                av(OFF.cum_s, 128, QT), S.ones[:, :], av(OFF.kv_s, 128, QT),
                S.carry[:, HEC + hec:HEC + hec + 1], ALU.mult, ALU.add))
            vop(lambda e, hec=hec: e.tensor_copy(
                S.carry[:, HEC + hec:HEC + hec + 1], av(OFF.cum_s + QT - 1, 128, 1)))
            vop(lambda e: e.tensor_mul(av(OFF.kv_c, 128, QT),
                                       av(OFF.bc_cq, 128, QT),
                                       av(OFF.cum_c, 128, QT)),
                waits=[f"{pfx}.ev.{hec}.2"], tag=f"{pfx}.bcr.{hec}.2")
            vop(lambda e: e.tensor_mul(av(OFF.kv_s, 128, QT),
                                       av(OFF.bc_sq, 128, QT),
                                       av(OFF.cum_s, 128, QT)),
                waits=[f"{pfx}.ev.{hec}.3"], tag=f"{pfx}.bcr.{hec}.3")
            vop(lambda e: e.tensor_add(av(OFF.kv_c, 128, QT),
                                       av(OFF.kv_c, 128, QT),
                                       av(OFF.kv_s, 128, QT)))
            vop(lambda e, hec=hec: e.tensor_copy(
                av(OFF.heads_hi + hec * QT, 128, QT, F32R), av(OFF.kv_c, 128, QT)),
                tag=f"{pfx}.hh.{hec}")
            vop(lambda e, hec=hec: e.tensor_sub(
                av(OFF.heads_lo + hec * QT, 128, QT, F32R), av(OFF.kv_c, 128, QT),
                av(OFF.heads_hi + hec * QT, 128, QT, F32R)),
                tag=f"{pfx}.hd.{hec}")
        P.reset_chain()
        if "dbg_headshi" in dbg and q == 0:
            P.op("gpsimd", lambda e: e.dma_start(
                out=dbg_dr["dbg_headshi"], in_=av(OFF.heads_hi, 128, HEC * QT)), dn=16,
                waits=[f"{pfx}.hd.{HEC - 1}"])
        for hec in range(min(2, HEC)):
            P.op("sync", lambda e, hec=hec: e.dma_start(
                out=av(OFF.wv_st[hec % 2], 128, 1024), in_=dr.wo[hec, :, :]),
                dn=16, tag=f"{pfx}.wo.{hec}", chain=False,
                waits=[f"{pfx}.vp.{HEC - 1}"], sem=("sWV0", "sWV1")[hec % 2])
        for hec in range(HEC):
            st_off = OFF.wv_st[hec % 2]
            hi_off = OFF.wv_hi[hec % 2]
            lo_off = OFF.wv_lo[hec % 2]
            sw = [f"{pfx}.wo.{hec}"]
            if hec < 2:
                sw.append(f"{pfx}.vp.{HEC - 1}")
            else:
                sw.append(f"{pfx}.op.{hec - 2}")
            P.op("scalar", lambda e, st_off=st_off, hi_off=hi_off: e.copy(
                av(hi_off, 128, 1024, F32R), av(st_off, 128, 1024)),
                waits=sw, chain=False, tag=f"{pfx}.woh.{hec}")
            P.op("vector", lambda e, st_off=st_off, hi_off=hi_off, lo_off=lo_off:
                 e.tensor_sub(av(lo_off, 128, 1024, F32R), av(st_off, 128, 1024),
                              av(hi_off, 128, 1024, F32R)),
                 waits=[f"{pfx}.woh.{hec}"], chain=False, tag=f"{pfx}.wos.{hec}")
            first = True
            for tau in range(4):
                for dh in range(2):
                    for pi_ in range(3):
                        lh = av((OFF.heads_hi if pi_ != 2 else OFF.heads_lo)
                                + hec * QT + tau * 128, 128, 128, F32R)
                        rh = av((hi_off if pi_ != 1 else lo_off) + dh * 512,
                                128, 512, F32R)
                        fw = [f"{pfx}.wos.{hec}"]
                        if hec == 0:
                            fw.append(f"{pfx}.hd.{HEC - 1}")
                        P.op("tensor", lambda e, lh=lh, rh=rh, tau=tau, dh=dh,
                             st=(hec == 0 and pi_ == 0),
                             sp=(hec == HEC - 1 and pi_ == 2):
                             e.matmul(S.ps_out[tau][:, dh * 512:(dh + 1) * 512],
                                      lh, rh, start=st, stop=sp),
                             waits=(fw if first else ()), chain=False,
                             tag=(f"{pfx}.op.{hec}"
                                  if (tau == 3 and dh == 1 and pi_ == 2) else None))
                        first = False
            if hec + 2 < HEC:
                P.op("sync", lambda e, hec=hec: e.dma_start(
                    out=av(OFF.wv_st[hec % 2], 128, 1024), in_=dr.wo[hec + 2, :, :]),
                    dn=16, tag=f"{pfx}.wo.{hec + 2}", chain=False,
                    waits=[f"{pfx}.op.{hec}"], sem=("sWV0", "sWV1")[hec % 2])
        P.reset_chain()
        P.after(f"{pfx}.op.{HEC - 1}")
        for tau in range(4):
            P.op("scalar", lambda e, tau=tau: e.copy(
                S.acc[:, (qh * 4 + tau) * D:(qh * 4 + tau + 1) * D],
                S.ps_out[tau][:, :]))

    def ffn_sync(P, hh):
        pfx = f"f{hh}"
        for i in range(NI):
            ee, hc = divmod(i, NHC)
            waits = [f"{pfx}.eo.{i - 2}.{NTAU - 1}"] if i >= 2 else []
            for j, drw in enumerate((dr.w1, dr.w2, dr.w3)):
                P.op("sync", lambda e, j=j, drw=drw, ee=ee, hc=hc, i=i: e.dma_start(
                    out=av(OFF.wst[j][i % 2], 128, 1024), in_=drw[ee, hc, :, :]),
                    dn=16, tag=f"{pfx}.w.{i}.{j}", chain=False,
                    waits=(waits if j == 0 else ()), sem=("sW0", "sW1")[i % 2])

    def ffn_scalar(P, hh):
        pfx = f"f{hh}"

        def hi_copies(i):
            waits = [f"{pfx}.w.{i}.0", f"{pfx}.w.{i}.1", f"{pfx}.w.{i}.2"]
            if i >= 2:
                waits.append(f"{pfx}.eo.{i - 2}.{NTAU - 1}")
            for j in range(3):
                P.op("scalar", lambda e, j=j, i=i: e.copy(
                    av(OFF.whi[j][i % 2], 128, 1024, F32R),
                    av(OFF.wst[j][i % 2], 128, 1024)),
                    waits=(waits if j == 0 else ()), chain=False,
                    tag=(f"{pfx}.hi.{i}" if j == 2 else None))

        hi_copies(0)
        if NI > 1:
            hi_copies(1)
        for i in range(NI):
            P.op("scalar", lambda e, i=i: e.activation(
                av(OFF.t2, 128, 1024), S.ps_h1[:, :], ACT.Silu),
                waits=([f"{pfx}.h1.{i}"] + ([f"{pfx}.t.{i - 1}"] if i >= 1 else [])),
                chain=False, tag=f"{pfx}.sig.{i}")
            if i + 2 < NI:
                hi_copies(i + 2)

    def acc_adds(P, pfx, i):
        ee = i // NHC
        for tau in range(NTAU):
            if ee == 0:
                P.op("vector", lambda e, tau=tau: e.tensor_add(
                    S.acc[:, tau * D:(tau + 1) * D], S.ps_eo[tau % 2][:, :],
                    S.acc[:, tau * D:(tau + 1) * D]),
                    waits=[f"{pfx}.eo.{i}.{tau}"], chain=False,
                    tag=f"{pfx}.acc.{i}.{tau}")
            else:
                P.op("vector", lambda e, tau=tau, ee=ee: e.scalar_tensor_tensor(
                    S.acc[:, tau * D:(tau + 1) * D], S.ps_eo[tau % 2][:, :],
                    S.w_all[:, tau * 8 + ee - 1:tau * 8 + ee],
                    S.acc[:, tau * D:(tau + 1) * D], ALU.mult, ALU.add),
                    waits=[f"{pfx}.eo.{i}.{tau}"], chain=False,
                    tag=f"{pfx}.acc.{i}.{tau}")

    def ffn_vector(P, hh):
        pfx = f"f{hh}"

        def lo_subs(i):
            waits = [f"{pfx}.hi.{i}"]
            for j in range(3):
                P.op("vector", lambda e, j=j, i=i: e.tensor_sub(
                    av(OFF.wlo[j][i % 2], 128, 1024, F32R),
                    av(OFF.wst[j][i % 2], 128, 1024),
                    av(OFF.whi[j][i % 2], 128, 1024, F32R)),
                    waits=(waits if j == 0 else ()), chain=False,
                    tag=(f"{pfx}.lo.{i}" if j == 2 else None))

        lo_subs(0)
        if NI > 1:
            lo_subs(1)
        for i in range(NI):
            if i >= 1:
                acc_adds(P, pfx, i - 1)
            if i >= 1 and i + 1 < NI:
                lo_subs(i + 1)
            P.op("vector", lambda e, i=i: e.tensor_mul(
                av(OFF.t2, 128, 1024), av(OFF.t2, 128, 1024),
                S.ps_h2[:, :]),
                waits=[f"{pfx}.sig.{i}", f"{pfx}.h2.{i}"], chain=False,
                tag=f"{pfx}.t2m.{i}")
            P.op("vector", lambda e, i=i: e.tensor_copy(
                av(OFF.thi[i % 2], 128, 1024, F32R), av(OFF.t2, 128, 1024)),
                waits=[f"{pfx}.t2m.{i}"], chain=False, tag=f"{pfx}.thi.{i}")
            P.op("vector", lambda e, i=i: e.tensor_sub(
                av(OFF.tlo[i % 2], 128, 1024, F32R), av(OFF.t2, 128, 1024),
                av(OFF.thi[i % 2], 128, 1024, F32R)), chain=False,
                waits=[f"{pfx}.thi.{i}"], tag=f"{pfx}.t.{i}")
        acc_adds(P, pfx, NI - 1)
        for tau in range(NTAU):
            w = [f"{pfx}.acc.{NI - 1}.{tau}", f"{pfx}.xr.{tau}", f"{pfx}.hi.{NI - 1}"]
            if tau >= 2:
                w.append(f"{pfx}.y.{tau - 2}")
            P.op("vector", lambda e, tau=tau: e.tensor_add(
                av(OFF.ob[tau % 2], 128, 1024), S.acc[:, tau * D:(tau + 1) * D],
                av(OFF.xr[tau % 2], 128, 1024)),
                waits=w, chain=False, tag=f"{pfx}.add.{tau}")

    def eo_mms(P, pfx, i):
        for tau in range(NTAU):
            waits = []
            if tau == 0:
                waits.append(f"{pfx}.t.{i}")
            if tau >= 2:
                waits.append(f"{pfx}.acc.{i}.{tau - 2}")
            elif i >= 1:
                waits.append(f"{pfx}.acc.{i - 1}.{tau - 2 + NTAU}")
            first = True
            for dh in range(2):
                for pi_ in range(3):
                    thof = OFF.thi[i % 2] if pi_ != 2 else OFF.tlo[i % 2]
                    w3of = OFF.whi[2][i % 2] if pi_ != 1 else OFF.wlo[2][i % 2]
                    P.op("tensor", lambda e, tau=tau, dh=dh, thof=thof, w3of=w3of,
                         st=(pi_ == 0), sp=(pi_ == 2): e.matmul(
                        S.ps_eo[tau % 2][:, dh * 512:(dh + 1) * 512],
                        av(thof + tau * 128, 128, 128, F32R),
                        av(w3of + dh * 512, 128, 512, F32R),
                        start=st, stop=sp),
                        waits=(waits if first else ()), chain=False,
                        tag=(f"{pfx}.eo.{i}.{tau}" if (dh == 1 and pi_ == 2) else None))
                    first = False

    def ffn_h_mms(P, pfx, i, j, psn):
        waits = [f"{pfx}.lo.{i}"]
        if i >= 1:
            waits.append(f"{pfx}.sig.{i - 1}" if j == 0 else f"{pfx}.t2m.{i - 1}")
        first = True
        for nh in range(2):
            for k in range(KC):
                for pi_ in range(3):
                    lhof = OFF.whi[j][i % 2] if pi_ < 2 else OFF.wlo[j][i % 2]
                    rht = S.xn_hi if pi_ != 1 else S.xn_lo
                    P.op("tensor", lambda e, k=k, nh=nh, lhof=lhof, rht=rht,
                         psn=psn, st=(k == 0 and pi_ == 0),
                         sp=(k == KC - 1 and pi_ == 2): e.matmul(
                        getattr(S, psn)[:, nh * 512:(nh + 1) * 512],
                        av(lhof + k * 128, 128, 128, F32R),
                        rht[:, k * 1024 + nh * 512:k * 1024 + (nh + 1) * 512],
                        start=st, stop=sp),
                        waits=(waits if first else ()), chain=False,
                        tag=(f"{pfx}.h{j + 1}.{i}"
                             if (nh == 1 and k == KC - 1 and pi_ == 2) else None))
                    first = False

    def ffn_tensor(P, hh):
        pfx = f"f{hh}"
        for i in range(NI):
            ffn_h_mms(P, pfx, i, 0, "ps_h1")
            if i >= 1:
                eo_mms(P, pfx, i - 1)
            ffn_h_mms(P, pfx, i, 1, "ps_h2")
        eo_mms(P, pfx, NI - 1)

    def ph_p1d(P, hh):
        pfx = f"d{hh}"
        for tau in range(min(2, NTAU)):
            tg = hh * NTAU + tau
            P.op("gpsimd", lambda e, tau=tau, tg=tg: e.dma_start(
                out=av(OFF.xr[tau % 2], 128, 1024),
                in_=dr.x[tg * 128:(tg + 1) * 128, :]), dn=16, tag=f"{pfx}.xr.{tau}",
                chain=False, sem=("sXR0", "sXR1")[tau % 2])
        for tau in range(NTAU):
            tg = hh * NTAU + tau
            P.after(f"{pfx}.xr.{tau}")
            P.op("vector", lambda e, tau=tau: e.tensor_add(
                av(OFF.ob[tau % 2], 128, 1024), S.acc[:, tau * D:(tau + 1) * D],
                av(OFF.xr[tau % 2], 128, 1024)), tag=f"{pfx}.add.{tau}")
            if tau + 2 < NTAU:
                tg2 = hh * NTAU + tau + 2
                P.op("gpsimd", lambda e, tau=tau, tg2=tg2: e.dma_start(
                    out=av(OFF.xr[tau % 2], 128, 1024),
                    in_=dr.x[tg2 * 128:(tg2 + 1) * 128, :]), dn=16,
                    tag=f"{pfx}.xr.{tau + 2}", chain=False,
                    waits=[f"{pfx}.add.{tau}"], sem=("sXR0", "sXR1")[tau % 2])
            P.op("gpsimd", lambda e, tau=tau, tg=tg: e.dma_start(
                out=dr.y[tg * 128:(tg + 1) * 128, :], in_=av(OFF.ob[tau % 2], 128, 1024)),
                dn=16, tag=f"{pfx}.y.{tau}")
        P.wait("gpsimd", f"{pfx}.y.{NTAU - 1}")
        if "dbg_acc" in dbg and hh == 0:
            P.op("gpsimd", lambda e: e.dma_start(
                out=dbg_dr["dbg_acc"], in_=S.acc[:, :]), dn=16)

    def ph_final(P):
        if P.mode == 0:
            return
        if P.mode == 1:
            for s in ("sG", "sY0", "sY1"):
                P.tags.pop(f"__{s}_total", None)
                P.tags[f"__{s}_total"] = (s, P.counters.get(s, 0), "gpsimd", 16)
        elif P.filter == "gpsimd":
            for s in ("sG", "sY0", "sY1"):
                P.eng.wait_ge(P.sems[s], P.tags[f"__{s}_total"][1])

    def ffn_gpsimd(P, hh):
        pfx = f"f{hh}"
        for tau in range(NTAU):
            tg = hh * NTAU + tau
            w = [f"{pfx}.lo.{NI - 1}"] if tau < 2 else [f"{pfx}.add.{tau - 2}"]
            P.op("gpsimd", lambda e, tau=tau, tg=tg: e.dma_start(
                out=av(OFF.xr[tau % 2], 128, 1024),
                in_=dr.x[tg * 128:(tg + 1) * 128, :]), dn=16, tag=f"{pfx}.xr.{tau}",
                chain=False, waits=w, sem=("sXR0", "sXR1")[tau % 2])
            P.op("gpsimd", lambda e, tau=tau, tg=tg: e.dma_start(
                out=dr.y[tg * 128:(tg + 1) * 128, :],
                in_=av(OFF.ob[tau % 2], 128, 1024)),
                dn=16, tag=f"{pfx}.y.{tau}", chain=False,
                waits=[f"{pfx}.add.{tau}"], sem=("sY0", "sY1")[tau % 2])
        P.wait("gpsimd", f"{pfx}.y.{NTAU - 2}", f"{pfx}.y.{NTAU - 1}")

    def ffn_phase(hh):
        return {"sync": lambda P: ffn_sync(P, hh),
                "tensor": lambda P: ffn_tensor(P, hh),
                "vector": lambda P: ffn_vector(P, hh),
                "scalar": lambda P: ffn_scalar(P, hh),
                "gpsimd": lambda P: ffn_gpsimd(P, hh)}

    PH = [("init", "seq", ph_init)]
    if with_attn:
        PH.append(("p0", "seq", ph_p0))
    PH.append(("p1a0", "seq", lambda P: ph_p1a(P, 0)))
    if with_attn:
        PH.append(("atn0", "seq", lambda P: ph_atn(P, 0)))
        PH.append(("atn1", "seq", lambda P: ph_atn(P, 1)))
    PH.append(("ffn0", "streams", ffn_phase(0)))
    PH.append(("p1a1", "seq", lambda P: ph_p1a(P, 1)))
    if with_attn:
        PH.append(("atn2", "seq", lambda P: ph_atn(P, 2)))
        PH.append(("atn3", "seq", lambda P: ph_atn(P, 3)))
    PH.append(("ffn1", "streams", ffn_phase(1)))
    PH.append(("fin", "seq", ph_final))
    if only is not None:
        keep = set(only) | {"init", "fin"}
        PH = [p for p in PH if p[0] in keep]

    STREAM_ORDER = ("sync", "tensor", "vector", "scalar", "gpsimd")

    for mode in (0, 1):
        P.mode = mode
        P.anon = 0
        P.anon_marks = {}
        for name, style, body in PH:
            P.anon_marks[name] = P.anon
            P.reset_chain()
            if style == "seq":
                body(P)
            else:
                for engname in STREAM_ORDER:
                    if engname in body:
                        body[engname](P)

    P.mode = 2
    P.sems = sems

    for name, style, body in PH:
        with ExitStack() as pst:
            if name == "init":
                S.ps_srow = pst.enter_context(nc.psum_tensor("ini_srow", [128, 16], F32))
            elif name == "p0":
                S.ps_wsc = pst.enter_context(nc.psum_tensor(name + "_wsc", [NH, D], F32))
                S.ps_wss = pst.enter_context(nc.psum_tensor(name + "_wss", [NH, D], F32))
                kkb = pst.enter_context(nc.psum_tensor(name + "_kk", [NH, 512], F32))
                ccb = pst.enter_context(nc.psum_tensor(name + "_cc", [NH, 512], F32))
                sh1 = pst.enter_context(nc.psum_tensor(name + "_sh1", [128, 512], F32))
                sh2 = pst.enter_context(nc.psum_tensor(name + "_sh2", [128, 512], F32))
                S.ps_kk = kkb[:, 0:2]
                S.ps_cc = ccb[:, 0:2]
                S.ps_k = [sh1[:, 128:144], sh2[:, 160:176]]
                S.ps_trc = sh1[:, 144:160]
                S.ps_trs = sh2[:, 144:160]
                S.ps_car2 = S.ps_wsc
                S.ps_ctr = sh2[0:64, 128:130]
            elif name.startswith("p1a"):
                S.ps_tr8 = pst.enter_context(nc.psum_tensor(name + "_tr8", [128, 1024], F32))
                S.ps_g = pst.enter_context(nc.psum_tensor(name + "_g", [128, 8], F32))
            elif name.startswith("atn"):
                big1 = pst.enter_context(nc.psum_tensor(name + "_b1", [128, 2048], F32))
                big2 = pst.enter_context(nc.psum_tensor(name + "_b2", [128, 2048], F32))
                S.ps_q = big2[0:NH, 0:QT]
                S.ps_k16 = big2[0:NH, 512:512 + QT]
                S.ps_v = [big2[:, 0:512], big2[:, 512:1024]]
                S.ps_bc4 = [big1[:, j * 512:(j + 1) * 512] for j in range(4)]
                S.ps_out = [big1[:, 0:1024], big1[:, 1024:2048],
                            big2[:, 0:1024], big2[:, 1024:2048]]
            elif name.startswith("ffn"):
                S.ps_h1 = pst.enter_context(nc.psum_tensor(name + "_h1", [128, 1024], F32))
                S.ps_h2 = pst.enter_context(nc.psum_tensor(name + "_h2", [128, 1024], F32))
                S.ps_eo = [pst.enter_context(nc.psum_tensor(name + "_e0", [128, 1024], F32)),
                           pst.enter_context(nc.psum_tensor(name + "_e1", [128, 1024], F32))]
            with nc.Block() as blk:
                if style == "seq":
                    for engname in ENGINES:
                        def run(engobj, engname=engname):
                            P.filter = engname
                            P.eng = engobj
                            P.anon = P.anon_marks[name]
                            P.reset_chain()
                            body(P)
                        getattr(blk, engname)(run)
                else:
                    P.anon = P.anon_marks[name]
                    for engname in STREAM_ORDER:
                        def run(engobj, engname=engname, fn=body[engname]):
                            P.filter = engname
                            P.eng = engobj
                            fn(P)
                        getattr(blk, engname)(run)

    ctx.close()
    return nc, list(dbg_dr.keys())


def kernel_run(inputs, with_attn=True, dbg=(), core_ids=None, **kw):
    shared, per_core = host_prepare(inputs)
    nc, dbg_names = make_kernel(with_attn=with_attn, dbg=dbg)
    in_maps = []
    for c in range(NCORES):
        m = dict(shared)
        m.update(per_core[c])
        in_maps.append(m)
    res = run_bass_kernel_spmd(nc, in_maps, core_ids=list(range(NCORES)), **kw)
    out = np.empty((B, T, D), dtype=np.float32)
    for c in range(NCORES):
        b, s = divmod(c, 4)
        out[b, s * TPC:(s + 1) * TPC, :] = res.results[c]["y"]
    return out, res


def kernel(**inputs):
    out, _ = kernel_run(inputs, with_attn=True)
    return out


# version 08
# speedup vs baseline: 1.0057x; 1.0057x over previous
import numpy as np
from contextlib import ExitStack

import concourse.bass as bass
import concourse.mybir as mybir
from concourse.bass_utils import run_bass_kernel_spmd

F32 = mybir.dt.float32
F32R = mybir.dt.float32r
ALU = mybir.AluOpType
ACT = mybir.ActivationFunctionType
AX = mybir.AxisListType.X

B, T, D = 2, 8192, 1024
NCORES = 8
TPC = 2048
HALFT = 1024
NQ = 4
QT = 512
NTAU = 8
KC = 8
NH = 16
DH = 64
HEC = 8
NE = 9
NHC = 22
HFF = 2730
HPAD = NHC * 128
PRIOR = 6144
NPT = PRIOR // 128
PI = float(np.pi)
NI = NE * NHC

C_PI2, C_PI4, C_I1024, C_EPSLN, C_EPSAT, C_35, C_NI1024 = 0, 1, 2, 3, 4, 5, 6

ENGINES = ("sync", "tensor", "vector", "scalar", "gpsimd")
ESEM = {"sync": "sL", "tensor": "sT", "vector": "sV", "scalar": "sS", "gpsimd": "sG"}


class Prog:
    def __init__(self):
        self.mode = 0
        self.counters = {}
        self.tags = {}
        self.used = set()
        self.filter = None
        self.eng = None
        self.sems = None
        self.chain = None
        self.pending = []
        self.anon = 0
        self.anon_marks = {}

    @property
    def counting(self):
        return self.mode != 2

    def op(self, engine, builder, waits=(), tag=None, dn=1, chain=True, sem=None):
        sem_name = sem or ESEM[engine]
        if tag is None:
            tag = f"_a{self.anon}"
            self.anon += 1
        if self.mode == 0:
            self.used.update(waits)
            if chain:
                self.used.update(self.pending)
                if self.chain is not None:
                    self.used.add(self.chain)
                self.pending = []
                self.chain = tag
            return tag
        live = (tag in self.used) or dn != 1
        if self.mode == 1:
            if live:
                self.counters[sem_name] = self.counters.get(sem_name, 0) + dn
            assert tag not in self.tags, f"dup tag {tag}"
            self.tags[tag] = (sem_name, self.counters.get(sem_name, 0), engine, dn)
            if chain:
                self.chain = tag
                self.pending = []
            return tag
        allw = list(waits)
        if chain:
            allw += self.pending
            if self.chain is not None:
                allw.append(self.chain)
            self.pending = []
            self.chain = tag
        if engine == self.filter:
            need = {}
            for lab in allw:
                s, v, prod_eng, pdn = self.tags[lab]
                if v > need.get(s, -1):
                    need[s] = v
            for s, v in need.items():
                self.eng.wait_ge(self.sems[s], v)
            ins = builder(self.eng)
            if live:
                ins.then_inc(self.sems[sem_name], dn)
        return tag

    def wait(self, engine, *labels):
        if self.mode == 0:
            self.used.update(labels)
            return
        if self.mode == 1 or engine != self.filter:
            return
        need = {}
        for lab in labels:
            s, v, prod_eng, pdn = self.tags[lab]
            if v > need.get(s, -1):
                need[s] = v
        for s, v in need.items():
            self.eng.wait_ge(self.sems[s], v)

    def after(self, *labels):
        if self.mode == 0:
            self.used.update(labels)
        self.pending += list(labels)

    def reset_chain(self):
        self.chain = None
        self.pending = []


def host_prepare(inp):
    out = {}
    q = np.ascontiguousarray(np.asarray(inp["L0_attn_query"], dtype=np.float32))

    WQ = np.asarray(inp["L0_attn_W_Q"], dtype=np.float32)[:, :, 0]
    WK = np.asarray(inp["L0_attn_W_K"], dtype=np.float32)[:, :, 0]
    WQK = np.concatenate([WQ.T, WK.T], axis=1)
    out["wqk"] = np.ascontiguousarray(
        WQK.reshape(KC, 128, 32).transpose(1, 0, 2).reshape(128, KC * 32))

    WV = np.asarray(inp["L0_attn_W_V"], dtype=np.float32)
    WVp = WV.transpose(1, 0, 2).reshape(D, NH * DH)
    out["wv"] = np.ascontiguousarray(
        WVp.reshape(KC, 128, HEC, 128).transpose(2, 1, 0, 3).reshape(HEC, 128, KC * 128))
    out["wvf"] = np.ascontiguousarray(
        WVp.reshape(KC, 128, NH * DH).transpose(1, 0, 2).reshape(128, KC * NH * DH))

    WO = np.asarray(inp["L0_attn_W_O"], dtype=np.float32)
    WOp = WO.transpose(2, 1, 0).reshape(NH * DH, D)
    out["wo"] = np.ascontiguousarray(WOp.reshape(HEC, 128, D))

    Wg = np.asarray(inp["L0_ffn_Wg"], dtype=np.float32)
    out["wg"] = np.ascontiguousarray(
        Wg.T.reshape(KC, 128, 8).transpose(1, 0, 2).reshape(128, KC * 8))
    out["gbias"] = np.ascontiguousarray(
        np.asarray(inp["L0_ffn_routing_bias"], dtype=np.float32).reshape(1, 8))

    def prep_w12(w):
        e = w.shape[0]
        wp = np.zeros((e, HPAD, D), dtype=np.float32)
        wp[:, :HFF, :] = w
        return np.ascontiguousarray(
            wp.reshape(e, NHC, 128, KC, 128).transpose(0, 1, 4, 3, 2)
            .reshape(e, NHC, 128, KC * 128))

    def prep_w3(w):
        e = w.shape[0]
        wp = np.zeros((e, D, HPAD), dtype=np.float32)
        wp[:, :, :HFF] = w
        return np.ascontiguousarray(wp.transpose(0, 2, 1).reshape(e, NHC, 128, D))

    out["w1"] = prep_w12(np.concatenate(
        [np.asarray(inp["L0_ffn_Ws1"], dtype=np.float32)[None],
         np.asarray(inp["L0_ffn_W1"], dtype=np.float32)], axis=0))
    out["w2"] = prep_w12(np.concatenate(
        [np.asarray(inp["L0_ffn_Ws2"], dtype=np.float32)[None],
         np.asarray(inp["L0_ffn_W2"], dtype=np.float32)], axis=0))
    out["w3"] = prep_w3(np.concatenate(
        [np.asarray(inp["L0_ffn_Ws3"], dtype=np.float32)[None],
         np.asarray(inp["L0_ffn_W3"], dtype=np.float32)], axis=0))

    out["idn"] = np.eye(128, dtype=np.float32)
    ind = np.zeros((NH, HEC * 128), dtype=np.float32)
    for j in range(HEC * 128):
        ind[j >> 6, j] = 1.0
    out["ind"] = np.ascontiguousarray(ind)
    lt = np.zeros((128, 64), dtype=np.float32)
    for e in range(8):
        lt[:, e * 8:e * 8 + e] = 1.0
    out["lt"] = lt

    per_core = []
    for c in range(NCORES):
        b, s = divmod(c, 4)
        xo = q[b, s * TPC:(s + 1) * TPC, :]
        xp = np.zeros((PRIOR, D), dtype=np.float32)
        pm = np.zeros((PRIOR, 1), dtype=np.float32)
        n_pr = s * TPC
        if n_pr:
            xp[PRIOR - n_pr:, :] = q[b, :n_pr, :]
            pm[PRIOR - n_pr:, :] = 1.0
        xpt = np.ascontiguousarray(
            xp.reshape(NPT, 128, KC, 128).transpose(3, 0, 2, 1).reshape(128, NPT * 1024))
        per_core.append({"x": np.ascontiguousarray(xo), "xp": xp, "xpt": xpt, "pm": pm})
    return out, per_core


class NS:
    pass


def make_kernel(with_attn=True, dbg=(), only=None):
    nc = bass.Bass("TRN2", target_bir_lowering=False)
    S = NS()
    dbg = set(dbg)

    dr = NS()
    dr.x = nc.dram_tensor("x", [TPC, D], F32, kind="ExternalInput").ap()
    dr.xp = nc.dram_tensor("xp", [PRIOR, D], F32, kind="ExternalInput").ap()
    dr.pm = nc.dram_tensor("pm", [PRIOR, 1], F32, kind="ExternalInput").ap()
    dr.xpt = nc.dram_tensor("xpt", [128, NPT * 1024], F32, kind="ExternalInput").ap()
    dr.wqk = nc.dram_tensor("wqk", [128, KC * 32], F32, kind="ExternalInput").ap()
    dr.wv = nc.dram_tensor("wv", [HEC, 128, KC * 128], F32, kind="ExternalInput").ap()
    dr.wvf = nc.dram_tensor("wvf", [128, KC * 1024], F32, kind="ExternalInput").ap()
    dr.wo = nc.dram_tensor("wo", [HEC, 128, D], F32, kind="ExternalInput").ap()
    dr.wg = nc.dram_tensor("wg", [128, KC * 8], F32, kind="ExternalInput").ap()
    dr.gbias = nc.dram_tensor("gbias", [1, 8], F32, kind="ExternalInput").ap()
    dr.w1 = nc.dram_tensor("w1", [NE, NHC, 128, KC * 128], F32, kind="ExternalInput").ap()
    dr.w2 = nc.dram_tensor("w2", [NE, NHC, 128, KC * 128], F32, kind="ExternalInput").ap()
    dr.w3 = nc.dram_tensor("w3", [NE, NHC, 128, D], F32, kind="ExternalInput").ap()
    dr.idn = nc.dram_tensor("idn", [128, 128], F32, kind="ExternalInput").ap()
    dr.ind = nc.dram_tensor("ind", [NH, HEC * 128], F32, kind="ExternalInput").ap()
    dr.lt = nc.dram_tensor("lt", [128, 64], F32, kind="ExternalInput").ap()
    dr.y = nc.dram_tensor("y", [TPC, D], F32, kind="ExternalOutput").ap()
    dbg_dr = {}

    def dbg_out(name, shape):
        if name in dbg:
            dbg_dr[name] = nc.dram_tensor(name, shape, F32, kind="ExternalOutput").ap()

    dbg_out("dbg_xnhi", [128, KC * HALFT])
    dbg_out("dbg_xnlo", [128, KC * HALFT])
    dbg_out("dbg_w", [128, NTAU * 8])
    dbg_out("dbg_acc", [128, NTAU * D])
    dbg_out("dbg_headshi", [128, HEC * QT])
    dbg_out("dbg_carry", [128, 2 * HEC])
    dbg_out("dbg_carryK", [NH, 2])
    dbg_out("dbg_trig", [NH, 4 * QT])
    dbg_out("dbg_wsumT", [128, KC * 32])

    ctx = ExitStack()
    sb = lambda name, shape, dt=F32: ctx.enter_context(
        nc.sbuf_tensor("sb_" + name, shape, dt))

    S.xn_hi = sb("xn_hi", [128, KC * HALFT], F32R)
    S.xn_lo = sb("xn_lo", [128, KC * HALFT], F32R)
    S.acc = sb("acc", [128, NTAU * D])
    S.arena = sb("arena", [128, 7168])
    S.arenb = sb("arenb", [128, 16384], F32R)
    S.ones = sb("ones", [128, QT])
    S.onecol = sb("onecol", [128, 1])
    S.consts = sb("consts", [128, 8])
    S.idn = sb("idn", [128, 128])
    S.lt = sb("lt", [128, 64])
    S.ind = sb("ind", [NH, HEC * 128])
    S.wqk = sb("wqk", [128, KC * 32])
    S.wqk_hi = sb("wqk_hi", [128, KC * 32], F32R)
    S.wqk_lo = sb("wqk_lo", [128, KC * 32], F32R)
    S.wg_hi = sb("wg_hi", [128, KC * 8], F32R)
    S.wg_lo = sb("wg_lo", [128, KC * 8], F32R)
    S.gbias = sb("gbias", [1, 8])
    S.w_all = sb("w_all", [128, NTAU * 8])
    S.carry = sb("carry", [128, 2 * HEC])
    S.carryK = sb("carryK", [NH, 2])
    S.t_cq = sb("t_cq", [NH, QT])
    S.t_sq = sb("t_sq", [NH, QT])
    S.t_ck = sb("t_ck", [NH, QT])
    S.t_sk = sb("t_sk", [NH, QT])
    S.g_s = sb("g_s", [128, 8])
    S.tmp8a = sb("tmp8a", [128, 8])
    S.tmp8b = sb("tmp8b", [128, 8])
    S.rank = sb("rank", [128, 8])
    S.rmask = sb("rmask", [128, 8])
    S.eg = sb("eg", [128, 8])
    S.red1 = sb("red1", [128, 2])
    S.red2 = sb("red2", [128, 2])
    S.ln_m = sb("ln_m", [128, 2])
    S.ln_r = sb("ln_r", [128, 2])
    S.rt_m = sb("rt_m", [128, 1])
    S.rt_s = sb("rt_s", [128, 1])
    S.rt_r = sb("rt_r", [128, 1])
    S.cs_k = sb("cs_k", [128, 64])
    S.cs_a = sb("cs_a", [128, 64])
    S.srow = sb("srow", [128, 16])
    S.negm = sb("negm", [128, 2])
    S.pmask = sb("pmask", [128, 2])

    A = S.arena

    def av(off, p, n, dt=F32):
        if off < 16384:
            ap = S.arenb[0:p, off:off + n]
            if dt is F32:
                ap = ap.bitcast(F32)
        else:
            ap = S.arena[0:p, off - 16384:off - 16384 + n]
            if dt is F32R:
                ap = ap.bitcast(F32R)
        return ap

    OFF = NS()
    OFF.whi = [[0, 1024], [2048, 3072], [4096, 5120]]
    OFF.wlo = [[6144, 7168], [8192, 9216], [10240, 11264]]
    OFF.thi = [12288, 13312]
    OFF.tlo = [14336, 15360]
    OFF.heads_hi = 0
    OFF.heads_lo = 4096
    OFF.wv_hi = [8192, 9216]
    OFF.wv_lo = [10240, 11264]
    OFF.wo_hi = [12288]
    OFF.wo_lo = [13312]
    OFF.wst = [[16384, 17408], [18432, 19456], [20480, 21504]]
    OFF.t2 = 22528
    OFF.xr = [16384, 17408]
    OFF.ob = [18432, 19456]
    OFF.wv_st = [16384, 17408]
    OFF.wo_st = [18432]
    OFF.bc_c = 19456
    OFF.bc_s = 19968
    OFF.bc_cq = 18432
    OFF.bc_sq = 18944
    OFF.kv_c = 20480
    OFF.kv_s = 20992
    OFF.cum_c = 21504
    OFF.cum_s = 22016
    OFF.p1_x = [16384, 17408]
    OFF.p1_tm = [18432, 19456]
    OFF.p0_x = [16384, 17408]
    OFF.p0_xt = [18432, 19456]
    OFF.p0_wvst = [20480]
    OFF.wsumT = 18432
    OFF.p0_wsc = 21504
    OFF.p0_wss = 22528
    OFF.p0_sq = 22528
    OFF.ln_sq = 22528
    OFF.t_den = 22528

    sems = {}
    for sname in ("sL", "sT", "sV", "sS", "sG", "sI", "sW0", "sW1", "sA", "sB",
                  "sWV0", "sWV1", "sWO", "sXR0", "sXR1", "sY0", "sY1"):
        sems[sname] = ctx.enter_context(nc.semaphore(sname))

    P = Prog()

    def ph_init(P):
        for nm in ("idn", "ind", "lt", "wqk", "gbias"):
            P.op("sync", lambda e, nm=nm: e.dma_start(out=getattr(S, nm)[:, :],
                                                      in_=getattr(dr, nm)),
                 dn=16, tag=f"ini.{nm}", chain=False, sem="sI")
        P.op("vector", lambda e: e.memset(S.ones[:, :], 1.0))
        P.op("vector", lambda e: e.memset(S.onecol[:, :], 1.0))
        P.op("vector", lambda e: e.memset(S.consts[:, C_PI2:C_PI2 + 1], PI / 2))
        P.op("vector", lambda e: e.memset(S.consts[:, C_PI4:C_PI4 + 1], PI / 4))
        P.op("vector", lambda e: e.memset(S.consts[:, C_I1024:C_I1024 + 1], 1.0 / 1024.0))
        P.op("vector", lambda e: e.memset(S.consts[:, C_EPSLN:C_EPSLN + 1], 1e-5))
        P.op("vector", lambda e: e.memset(S.consts[:, C_EPSAT:C_EPSAT + 1], 1e-6))
        P.op("vector", lambda e: e.memset(S.consts[:, C_35:C_35 + 1], 3.5))
        P.op("vector", lambda e: e.memset(S.consts[:, C_NI1024:C_NI1024 + 1], -1.0 / 1024.0))
        P.op("vector", lambda e: e.memset(S.carry[:, :], 0.0))
        P.op("vector", lambda e: e.memset(S.carryK[:, :], 0.0))
        P.op("sync", lambda e: e.dma_start(out=av(16384, 128, KC * 8), in_=dr.wg),
             dn=16, tag="ini.wg", chain=False, sem="sI")
        P.after("ini.gbias", "ini.wg")
        P.op("vector", lambda e: e.tensor_copy(S.wqk_hi[:, :], S.wqk[:, :]))
        P.op("vector", lambda e: e.tensor_sub(S.wqk_lo[:, :], S.wqk[:, :], S.wqk_hi[:, :]))
        P.op("vector", lambda e: e.tensor_copy(S.wg_hi[:, :], av(16384, 128, KC * 8)))
        P.op("vector", lambda e: e.tensor_sub(S.wg_lo[:, :], av(16384, 128, KC * 8), S.wg_hi[:, :]))
        for k in range(KC):
            P.op("tensor", lambda e, k=k: e.matmul(
                S.ps_srow[:, :], S.ones[0:128, 0:128],
                S.wqk[:, k * 32 + 16:k * 32 + 32],
                start=(k == 0), stop=(k == KC - 1)))
        P.op("scalar", lambda e: e.copy(S.srow[:, :], S.ps_srow[:, :]))

    def ph_p0(P):
        vprev = [None]
        sprev = [None]

        def vop(builder, waits=(), tag=None):
            w = list(waits) + ([vprev[0]] if vprev[0] else [])
            t = P.op("vector", builder, waits=w, chain=False, tag=tag)
            vprev[0] = t
            return t

        def sop(builder, waits=(), tag=None):
            w = list(waits) + ([sprev[0]] if sprev[0] else [])
            t = P.op("scalar", builder, waits=w, chain=False, tag=tag)
            sprev[0] = t
            return t

        for i in range(min(2, NPT)):
            P.op("sync", lambda e, i=i: e.dma_start(
                out=av(OFF.p0_x[i % 2], 128, 1024),
                in_=dr.xp[i * 128:(i + 1) * 128, :]), dn=16,
                tag=f"p0.x.{i}", chain=False, sem=("sA", "sB")[i % 2])
            P.op("sync", lambda e, i=i: e.dma_start(
                out=av(OFF.p0_xt[i % 2], 128, 1024),
                in_=dr.xpt[:, i * 1024:(i + 1) * 1024]), dn=16,
                tag=f"p0.xt.{i}", chain=False, sem=("sA", "sB")[i % 2])
            P.op("sync", lambda e, i=i: e.dma_start(
                out=S.pmask[:, i % 2:i % 2 + 1],
                in_=dr.pm[i * 128:(i + 1) * 128, :]), dn=16, tag=f"p0.pm.{i}",
                chain=False, sem=("sA", "sB")[i % 2])
        for i in range(NPT):
            q2 = i % 2
            xs = lambda i=i: av(OFF.p0_x[i % 2], 128, 1024)
            vop(lambda e, i=i: e.tensor_mul(av(OFF.p0_sq, 128, 1024), xs(i), xs(i)),
                waits=[f"p0.x.{i}", f"p0.xt.{i}", f"p0.pm.{i}"])
            vop(lambda e, i=i, q2=q2: e.reduce_sum(S.red1[:, q2:q2 + 1], xs(i), axis=AX),
                tag=f"p0.vr1.{i}")
            vop(lambda e, q2=q2: e.reduce_sum(S.red2[:, q2:q2 + 1],
                                              av(OFF.p0_sq, 128, 1024), axis=AX),
                tag=f"p0.vr2.{i}")
            sop(lambda e, q2=q2: e.activation(
                S.negm[:, q2:q2 + 1], S.red1[:, q2:q2 + 1], ACT.Copy,
                scale=S.consts[:, C_NI1024:C_NI1024 + 1]), waits=[f"p0.vr1.{i}"])
            sop(lambda e, q2=q2: e.activation(S.ln_m[:, q2:q2 + 1],
                                              S.negm[:, q2:q2 + 1], ACT.Square),
                tag=f"p0.a1.{i}")
            vop(lambda e, q2=q2: e.tensor_scalar(
                S.red2[:, q2:q2 + 1], S.red2[:, q2:q2 + 1],
                S.consts[:, C_I1024:C_I1024 + 1], S.ln_m[:, q2:q2 + 1],
                ALU.mult, ALU.subtract), waits=[f"p0.a1.{i}"], tag=f"p0.vv.{i}")
            sop(lambda e, q2=q2: e.activation(S.ln_r[:, q2:q2 + 1], S.red2[:, q2:q2 + 1],
                                              ACT.Sqrt,
                                              bias=S.consts[:, C_EPSLN:C_EPSLN + 1]),
                waits=[f"p0.vv.{i}"], tag=f"p0.a2.{i}")
            vop(lambda e, q2=q2: e.reciprocal(S.ln_r[:, q2:q2 + 1], S.ln_r[:, q2:q2 + 1]),
                waits=[f"p0.a2.{i}"])
            kw = [f"p0.x.{i}", f"p0.xt.{i}", f"p0.pm.{i}"]
            if i >= 2:
                kw.append(f"p0.kc.{i - 2}")
            for k in range(KC):
                P.op("tensor", lambda e, k=k, i=i, q2=q2: e.matmul(
                    S.ps_k[q2][:, :], av(OFF.p0_xt[i % 2] + k * 128, 128, 128),
                    S.wqk[:, k * 32 + 16:k * 32 + 32],
                    start=(k == 0), stop=(k == KC - 1)),
                    waits=(kw if k == 0 else ()), chain=False,
                    tag=(f"p0.kraw.{i}" if k == KC - 1 else None))
            vop(lambda e, q2=q2: e.scalar_tensor_tensor(
                S.cs_k[:, q2 * 32:q2 * 32 + 16], S.srow[:, :], S.negm[:, q2:q2 + 1],
                S.ps_k[q2][:, :], ALU.mult, ALU.add), waits=[f"p0.kraw.{i}"])
            vop(lambda e, q2=q2: e.tensor_scalar(
                S.cs_k[:, q2 * 32:q2 * 32 + 16], S.cs_k[:, q2 * 32:q2 * 32 + 16],
                S.ln_r[:, q2:q2 + 1], None, ALU.mult), tag=f"p0.kc.{i}")
            sop(lambda e, q2=q2: e.activation(S.cs_k[:, q2 * 32:q2 * 32 + 16],
                                              S.cs_k[:, q2 * 32:q2 * 32 + 16], ACT.Tanh),
                waits=[f"p0.kc.{i}"])
            sop(lambda e, q2=q2: e.activation(
                S.cs_k[:, q2 * 32 + 16:q2 * 32 + 32], S.cs_k[:, q2 * 32:q2 * 32 + 16],
                ACT.Sin, scale=S.consts[:, C_PI4:C_PI4 + 1]))
            sop(lambda e, q2=q2: e.activation(
                S.cs_k[:, q2 * 32:q2 * 32 + 16], S.cs_k[:, q2 * 32:q2 * 32 + 16],
                ACT.Sin, bias=S.consts[:, C_PI2:C_PI2 + 1],
                scale=S.consts[:, C_PI4:C_PI4 + 1]), tag=f"p0.trig.{i}")
            vop(lambda e, i=i, q2=q2: e.tensor_scalar(
                S.cs_k[:, q2 * 32:q2 * 32 + 32], S.cs_k[:, q2 * 32:q2 * 32 + 32],
                S.pmask[:, i % 2:i % 2 + 1], None, ALU.mult),
                waits=[f"p0.trig.{i}"], tag=f"p0.msk.{i}")
            vop(lambda e, q2=q2: e.tensor_scalar(
                S.cs_a[:, q2 * 32:q2 * 32 + 32], S.cs_k[:, q2 * 32:q2 * 32 + 32],
                S.ln_r[:, q2:q2 + 1], None, ALU.mult), tag=f"p0.a.{i}")
            for n in range(2):
                P.op("tensor", lambda e, n=n, i=i, q2=q2: e.matmul(
                    S.ps_wsc[:, n * 512:(n + 1) * 512], S.cs_a[:, q2 * 32:q2 * 32 + 16],
                    av(OFF.p0_x[i % 2] + n * 512, 128, 512),
                    start=(i == 0), stop=(i == NPT - 1)),
                    waits=([f"p0.a.{i}"] if n == 0 else ()), chain=False)
                P.op("tensor", lambda e, n=n, i=i, q2=q2: e.matmul(
                    S.ps_wss[:, n * 512:(n + 1) * 512],
                    S.cs_a[:, q2 * 32 + 16:q2 * 32 + 32],
                    av(OFF.p0_x[i % 2] + n * 512, 128, 512),
                    start=(i == 0), stop=(i == NPT - 1)), chain=False)
            P.op("tensor", lambda e, i=i, q2=q2: e.matmul(
                S.ps_kk[0:16, 0:1], S.cs_k[:, q2 * 32:q2 * 32 + 16], S.onecol[:, :],
                start=(i == 0), stop=False), chain=False)
            P.op("tensor", lambda e, i=i, q2=q2: e.matmul(
                S.ps_kk[0:16, 1:2], S.cs_k[:, q2 * 32 + 16:q2 * 32 + 32], S.onecol[:, :],
                start=False, stop=(i == NPT - 1)), chain=False)
            P.op("tensor", lambda e, i=i, q2=q2: e.matmul(
                S.ps_cc[0:16, 0:1], S.cs_a[:, q2 * 32:q2 * 32 + 16], S.negm[:, q2:q2 + 1],
                start=(i == 0), stop=False), chain=False)
            P.op("tensor", lambda e, i=i, q2=q2: e.matmul(
                S.ps_cc[0:16, 1:2], S.cs_a[:, q2 * 32 + 16:q2 * 32 + 32],
                S.negm[:, q2:q2 + 1],
                start=False, stop=(i == NPT - 1)), chain=False, tag=f"p0.wsum.{i}")
            if i + 2 < NPT:
                P.op("sync", lambda e, i=i: e.dma_start(
                    out=av(OFF.p0_x[i % 2], 128, 1024),
                    in_=dr.xp[(i + 2) * 128:(i + 3) * 128, :]), dn=16,
                    tag=f"p0.x.{i + 2}", chain=False, waits=[f"p0.wsum.{i}"],
                    sem=("sA", "sB")[i % 2])
                P.op("sync", lambda e, i=i: e.dma_start(
                    out=av(OFF.p0_xt[i % 2], 128, 1024),
                    in_=dr.xpt[:, (i + 2) * 1024:(i + 3) * 1024]), dn=16,
                    tag=f"p0.xt.{i + 2}", chain=False, waits=[f"p0.kraw.{i}"],
                    sem=("sA", "sB")[i % 2])
                P.op("sync", lambda e, i=i: e.dma_start(
                    out=S.pmask[:, i % 2:i % 2 + 1],
                    in_=dr.pm[(i + 2) * 128:(i + 3) * 128, :]), dn=16,
                    tag=f"p0.pm.{i + 2}", chain=False, waits=[f"p0.msk.{i}"],
                    sem=("sA", "sB")[i % 2])
        P.reset_chain()
        P.after(f"p0.wsum.{NPT - 1}", f"p0.a.{NPT - 1}")
        P.op("scalar", lambda e: e.copy(S.carryK[:, 0:1], S.ps_kk[0:16, 0:1]))
        P.op("scalar", lambda e: e.copy(S.carryK[:, 1:2], S.ps_kk[0:16, 1:2]))
        P.op("scalar", lambda e: e.copy(S.cs_k[0:16, 0:2], S.ps_cc[0:16, 0:2]))
        P.op("scalar", lambda e: e.copy(av(OFF.p0_wsc, NH, D), S.ps_wsc[:, :]))
        P.op("scalar", lambda e: e.copy(av(OFF.p0_wss, NH, D), S.ps_wss[:, :]))
        P.op("vector", lambda e: e.tensor_scalar(
            av(OFF.p0_wsc, NH, D), av(OFF.p0_wsc, NH, D), S.cs_k[0:16, 0:1], None,
            ALU.add))
        P.op("vector", lambda e: e.tensor_scalar(
            av(OFF.p0_wss, NH, D), av(OFF.p0_wss, NH, D), S.cs_k[0:16, 1:2], None,
            ALU.add))
        for k in range(KC):
            P.op("tensor", lambda e, k=k: e.transpose(
                S.ps_trc[:, :], av(OFF.p0_wsc + k * 128, NH, 128), S.idn[0:16, 0:16]))
            P.op("scalar", lambda e, k=k: e.copy(
                S.arena[0:128, OFF.wsumT - 16384 + k * 32:OFF.wsumT - 16384 + k * 32 + 32:2],
                S.ps_trc[:, 0:16]))
            P.op("tensor", lambda e, k=k: e.transpose(
                S.ps_trs[:, :], av(OFF.p0_wss + k * 128, NH, 128), S.idn[0:16, 0:16]))
            P.op("scalar", lambda e, k=k: e.copy(
                S.arena[0:128, OFF.wsumT - 16384 + k * 32 + 1:OFF.wsumT - 16384 + k * 32 + 33:2],
                S.ps_trs[:, 0:16]))
        P.op("sync", lambda e: e.dma_start(
            out=av(OFF.p0_wvst[0], 128, 1024),
            in_=dr.wvf[:, 0:1024]), dn=16, tag="p0.wv.0", chain=False, sem="sWO")
        for k in range(KC):
            P.after(f"p0.wv.{k}")
            for h in range(NH):
                P.op("tensor", lambda e, k=k, h=h: e.matmul(
                    S.ps_car2[0:2, h * 64:(h + 1) * 64],
                    av(OFF.wsumT + k * 32 + 2 * h, 128, 2),
                    av(OFF.p0_wvst[0] + h * 64, 128, 64),
                    start=(k == 0 and h % 8 == 0), stop=(k == KC - 1 and h % 8 == 7)),
                    tag=(f"p0.carmm.{k}" if h == NH - 1 else None))
            if k + 1 < KC:
                P.op("sync", lambda e, k=k: e.dma_start(
                    out=av(OFF.p0_wvst[0], 128, 1024),
                    in_=dr.wvf[:, (k + 1) * 1024:(k + 2) * 1024]), dn=16,
                    tag=f"p0.wv.{k + 1}", chain=False, waits=[f"p0.carmm.{k}"],
                    sem="sWO")
        for h in range(NH):
            P.op("scalar", lambda e, h=h: e.copy(av(16384, 2, 64),
                                                 S.ps_car2[0:2, h * 64:(h + 1) * 64]))
            P.op("tensor", lambda e: e.transpose(
                S.ps_ctr[:, :], av(16384, 2, 64), S.idn[0:2, 0:2]))
            P.op("scalar", lambda e: e.copy(av(16448, 64, 2), S.ps_ctr[:, :]))
            hec, off = h // 2, (h % 2) * 64
            P.op("gpsimd", lambda e, hec=hec, off=off: e.dma_start(
                out=S.carry[off:off + 64, hec:hec + 1], in_=av(16448, 64, 2)[:, 0:1]), dn=16,
                tag=f"p0.cdma.{h}.c")
            P.op("gpsimd", lambda e, hec=hec, off=off: e.dma_start(
                out=S.carry[off:off + 64, HEC + hec:HEC + hec + 1],
                in_=av(16448, 64, 2)[:, 1:2]), dn=16, tag=f"p0.cdma.{h}.s")
        P.wait("gpsimd", f"p0.cdma.{NH - 1}.s")
        if "dbg_wsumT" in dbg:
            P.op("gpsimd", lambda e: e.dma_start(
                out=dbg_dr["dbg_wsumT"], in_=av(OFF.wsumT, 128, KC * 32)), dn=16)
        if "dbg_carry" in dbg:
            P.op("gpsimd", lambda e: e.dma_start(
                out=dbg_dr["dbg_carry"], in_=S.carry[:, :]), dn=16)
        if "dbg_carryK" in dbg:
            P.op("gpsimd", lambda e: e.dma_start(
                out=dbg_dr["dbg_carryK"], in_=S.carryK[:, :]), dn=16)

    def ph_p1a(P, hh):
        pfx = f"a{hh}"
        vprev = [None]
        sprev = [None]

        def vop(builder, waits=(), tag=None):
            w = list(waits) + ([vprev[0]] if vprev[0] else [])
            t = P.op("vector", builder, waits=w, chain=False, tag=tag)
            vprev[0] = t
            return t

        def sop(builder, waits=(), tag=None):
            w = list(waits) + ([sprev[0]] if sprev[0] else [])
            t = P.op("scalar", builder, waits=w, chain=False, tag=tag)
            sprev[0] = t
            return t

        for t in range(min(2, NTAU)):
            P.op("sync", lambda e, t=t: e.dma_start(
                out=av(OFF.p1_x[t % 2], 128, 1024),
                in_=dr.x[(hh * NTAU + t) * 128:(hh * NTAU + t + 1) * 128, :]),
                dn=16, tag=f"{pfx}.x.{t}", chain=False, sem=("sA", "sB")[t % 2])
        def emit_ln(t):
            q2 = t % 2
            xs = lambda t=t: av(OFF.p1_x[t % 2], 128, 1024)
            tm = lambda t=t: av(OFF.p1_tm[t % 2], 128, 1024)
            vop(lambda e, t=t: e.tensor_mul(av(OFF.ln_sq, 128, 1024), xs(t), xs(t)),
                waits=[f"{pfx}.x.{t}"])
            vop(lambda e, t=t, q2=q2: e.reduce_sum(S.red1[:, q2:q2 + 1], xs(t), axis=AX),
                tag=f"{pfx}.vr1.{t}")
            vop(lambda e, q2=q2: e.reduce_sum(S.red2[:, q2:q2 + 1],
                                              av(OFF.ln_sq, 128, 1024), axis=AX),
                tag=f"{pfx}.vr2.{t}")
            sop(lambda e, q2=q2: e.activation(
                S.negm[:, q2:q2 + 1], S.red1[:, q2:q2 + 1], ACT.Copy,
                scale=S.consts[:, C_NI1024:C_NI1024 + 1]), waits=[f"{pfx}.vr1.{t}"])
            sop(lambda e, q2=q2: e.activation(S.ln_m[:, q2:q2 + 1],
                                              S.negm[:, q2:q2 + 1], ACT.Square),
                tag=f"{pfx}.a1.{t}")
            vop(lambda e, q2=q2: e.tensor_scalar(
                S.red2[:, q2:q2 + 1], S.red2[:, q2:q2 + 1],
                S.consts[:, C_I1024:C_I1024 + 1], S.ln_m[:, q2:q2 + 1],
                ALU.mult, ALU.subtract), waits=[f"{pfx}.a1.{t}"],
                tag=f"{pfx}.vv.{t}")
            sop(lambda e, q2=q2: e.activation(S.ln_r[:, q2:q2 + 1], S.red2[:, q2:q2 + 1],
                                              ACT.Sqrt,
                                              bias=S.consts[:, C_EPSLN:C_EPSLN + 1]),
                waits=[f"{pfx}.vv.{t}"], tag=f"{pfx}.a2.{t}")
            vop(lambda e, q2=q2: e.reciprocal(S.ln_r[:, q2:q2 + 1], S.ln_r[:, q2:q2 + 1]),
                waits=[f"{pfx}.a2.{t}"])
            nw = [f"{pfx}.trg.{t - 2}"] if t >= 2 else []
            vop(lambda e, t=t, q2=q2: e.tensor_scalar(
                tm(t), xs(t), S.negm[:, q2:q2 + 1], S.ln_r[:, q2:q2 + 1],
                ALU.add, ALU.mult), waits=nw, tag=f"{pfx}.nrm.{t}")
            if t + 2 < NTAU:
                P.op("sync", lambda e, t=t: e.dma_start(
                    out=av(OFF.p1_x[(t + 2) % 2], 128, 1024),
                    in_=dr.x[(hh * NTAU + t + 2) * 128:(hh * NTAU + t + 3) * 128, :]),
                    dn=16, tag=f"{pfx}.x.{t + 2}", chain=False,
                    waits=[f"{pfx}.nrm.{t}"], sem=("sA", "sB")[t % 2])

        def emit_gate(t):
            gw = [f"{pfx}.lo.{t}"]
            if t >= 1:
                gw.append(f"{pfx}.gsev.{t - 1}")
            first = True
            for k in range(KC):
                for pi_, (lh, rh) in enumerate((
                        (S.xn_hi, S.wg_hi), (S.xn_hi, S.wg_lo), (S.xn_lo, S.wg_hi))):
                    P.op("tensor", lambda e, k=k, t=t, lh=lh, rh=rh,
                         st=(k == 0 and pi_ == 0): e.matmul(
                        S.ps_g[:, :],
                        lh[:, k * 1024 + t * 128:k * 1024 + t * 128 + 128],
                        rh[:, k * 8:(k + 1) * 8], start=st, stop=False),
                        waits=(gw if first else ()), chain=False)
                    first = False
            P.op("tensor", lambda e: e.matmul(
                S.ps_g[:, :], S.ones[0:1, 0:128], S.gbias[:, :], start=False, stop=True),
                chain=False, tag=f"{pfx}.gmm.{t}")
            sop(lambda e: e.copy(S.g_s[:, :], S.ps_g[:, :]),
                waits=[f"{pfx}.gmm.{t}"], tag=f"{pfx}.gsev.{t}")

        def emit_routing(t):
            for eix in range(8):
                vop(lambda e, eix=eix: e.scalar_tensor_tensor(
                    S.tmp8b[:, :], S.g_s[:, :], S.g_s[:, eix:eix + 1],
                    S.lt[:, eix * 8:eix * 8 + 8], ALU.is_equal, ALU.mult),
                    waits=([f"{pfx}.gsev.{t}"] if eix == 0 else ()))
                vop(lambda e, eix=eix: e.scalar_tensor_tensor(
                    S.tmp8a[:, :], S.g_s[:, :], S.g_s[:, eix:eix + 1],
                    S.tmp8b[:, :], ALU.is_gt, ALU.add))
                vop(lambda e, eix=eix: e.reduce_sum(
                    S.rank[:, eix:eix + 1], S.tmp8a[:, :], axis=AX))
            vop(lambda e: e.tensor_scalar(
                S.rmask[:, :], S.rank[:, :], S.consts[:, C_35:C_35 + 1], None, ALU.is_le))
            vop(lambda e: e.reduce_max(S.rt_m[:, :], S.g_s[:, :], axis=AX, negate=True),
                tag=f"{pfx}.rt1.{t}")
            sop(lambda e: e.activation(S.eg[:, :], S.g_s[:, :], ACT.Exp,
                                       bias=S.rt_m[:, 0:1]),
                waits=[f"{pfx}.rt1.{t}"], tag=f"{pfx}.exp.{t}")
            vop(lambda e: e.tensor_mul(S.eg[:, :], S.eg[:, :], S.rmask[:, :]),
                waits=[f"{pfx}.exp.{t}"])
            vop(lambda e: e.reduce_sum(S.rt_s[:, :], S.eg[:, :], axis=AX))
            vop(lambda e: e.reciprocal(S.rt_r[:, 0:1], S.rt_s[:, 0:1]))
            vop(lambda e, t=t: e.tensor_scalar(
                S.w_all[:, t * 8:t * 8 + 8], S.eg[:, :], S.rt_r[:, 0:1], None, ALU.mult),
                tag=f"{pfx}.w.{t}")

        emit_ln(0)
        for t in range(NTAU):
            q2 = t % 2
            xs = lambda t=t: av(OFF.p1_x[t % 2], 128, 1024)
            tm = lambda t=t: av(OFF.p1_tm[t % 2], 128, 1024)
            tw = [f"{pfx}.nrm.{t}"]
            if t >= 1:
                tw.append(f"{pfx}.lo.{t - 1}")
            for k in range(KC):
                P.op("tensor", lambda e, k=k, t=t: e.transpose(
                    S.ps_tr8[:, k * 128:(k + 1) * 128],
                    av(OFF.p1_tm[t % 2] + k * 128, 128, 128), S.idn[:, :]),
                    waits=(tw if k == 0 else ()), chain=False,
                    tag=(f"{pfx}.trg.{t}" if k == KC - 1 else None))
            if t >= 1:
                emit_gate(t - 1)
            for k in range(KC):
                sop(lambda e, k=k, t=t: e.copy(
                    S.xn_hi[:, k * 1024 + t * 128:k * 1024 + t * 128 + 128],
                    S.ps_tr8[:, k * 128:(k + 1) * 128]),
                    waits=([f"{pfx}.trg.{t}"] if k == 0 else ()),
                    tag=(f"{pfx}.heg.{t}" if k == KC - 1 else None))
            for k in range(KC):
                vop(lambda e, k=k, t=t: e.tensor_sub(
                    S.xn_lo[:, k * 1024 + t * 128:k * 1024 + t * 128 + 128],
                    S.ps_tr8[:, k * 128:(k + 1) * 128],
                    S.xn_hi[:, k * 1024 + t * 128:k * 1024 + t * 128 + 128]),
                    waits=([f"{pfx}.heg.{t}"] if k == 0 else ()),
                    tag=(f"{pfx}.lo.{t}" if k == KC - 1 else None))
            if t + 1 < NTAU:
                emit_ln(t + 1)
            if t >= 1:
                emit_routing(t - 1)

        emit_gate(NTAU - 1)
        emit_routing(NTAU - 1)
        P.reset_chain()
        P.after(f"{pfx}.w.{NTAU - 1}", f"{pfx}.lo.{NTAU - 1}")
        if not with_attn:
            P.op("vector", lambda e: e.memset(S.acc[:, :], 0.0))
        if "dbg_xnhi" in dbg and hh == 0:
            P.op("gpsimd", lambda e: e.dma_start(
                out=dbg_dr["dbg_xnhi"], in_=S.xn_hi[:, :].bitcast(F32)), dn=16)
            P.op("gpsimd", lambda e: e.dma_start(
                out=dbg_dr["dbg_xnlo"], in_=S.xn_lo[:, :].bitcast(F32)), dn=16)
        if "dbg_w" in dbg and hh == 0:
            P.op("gpsimd", lambda e: e.dma_start(
                out=dbg_dr["dbg_w"], in_=S.w_all[:, :]), dn=16)

    def ph_atn(P, q):
        hh, qh = divmod(q, 2)
        tb = qh * QT
        pfx = f"q{q}"
        for hec in range(min(2, HEC)):
            P.op("sync", lambda e, hec=hec: e.dma_start(
                out=av(OFF.wv_st[hec % 2], 128, 1024), in_=dr.wv[hec, :, :]),
                dn=16, tag=f"{pfx}.wv.{hec}", chain=False, sem=("sWV0", "sWV1")[hec % 2])
        for (psn, col) in (("ps_q", 0), ("ps_k16", 16)):
            for k in range(KC):
                for pi_ in range(3):
                    lh = S.wqk_hi if pi_ != 2 else S.wqk_lo
                    rh = S.xn_hi if pi_ != 1 else S.xn_lo
                    P.op("tensor", lambda e, k=k, psn=psn, col=col, lh=lh, rh=rh,
                         st=(k == 0 and pi_ == 0), sp=(k == KC - 1 and pi_ == 2):
                         e.matmul(getattr(S, psn)[:, :],
                                  lh[:, k * 32 + col:k * 32 + col + 16],
                                  rh[:, k * 1024 + tb:k * 1024 + tb + QT],
                                  start=st, stop=sp))
        P.op("scalar", lambda e: e.activation(av(OFF.t_den, NH, QT), S.ps_q[:, :], ACT.Tanh))
        P.op("scalar", lambda e: e.activation(S.t_cq[:, :], av(OFF.t_den, NH, QT), ACT.Sin,
                                              bias=S.consts[0:16, C_PI2:C_PI2 + 1],
                                              scale=S.consts[0:16, C_PI4:C_PI4 + 1]))
        P.op("scalar", lambda e: e.activation(S.t_sq[:, :], av(OFF.t_den, NH, QT), ACT.Sin,
                                              scale=S.consts[0:16, C_PI4:C_PI4 + 1]))
        P.op("scalar", lambda e: e.activation(av(OFF.t_den, NH, QT), S.ps_k16[:, :], ACT.Tanh))
        P.op("scalar", lambda e: e.activation(S.t_ck[:, :], av(OFF.t_den, NH, QT), ACT.Sin,
                                              bias=S.consts[0:16, C_PI2:C_PI2 + 1],
                                              scale=S.consts[0:16, C_PI4:C_PI4 + 1]))
        P.op("scalar", lambda e: e.activation(S.t_sk[:, :], av(OFF.t_den, NH, QT), ACT.Sin,
                                              scale=S.consts[0:16, C_PI4:C_PI4 + 1]))
        P.op("vector", lambda e: e.tensor_tensor_scan(
            av(OFF.cum_c, NH, QT), S.ones[0:NH, 0:QT], S.t_ck[:, :], S.carryK[:, 0:1],
            ALU.mult, ALU.add))
        P.op("vector", lambda e: e.tensor_copy(S.carryK[:, 0:1], av(OFF.cum_c + QT - 1, NH, 1)))
        P.op("vector", lambda e: e.tensor_tensor_scan(
            av(OFF.cum_s, NH, QT), S.ones[0:NH, 0:QT], S.t_sk[:, :], S.carryK[:, 1:2],
            ALU.mult, ALU.add))
        P.op("vector", lambda e: e.tensor_copy(S.carryK[:, 1:2], av(OFF.cum_s + QT - 1, NH, 1)))
        P.op("vector", lambda e: e.tensor_mul(av(OFF.t_den, NH, QT), S.t_cq[:, :], av(OFF.cum_c, NH, QT)))
        P.op("vector", lambda e: e.tensor_mul(av(OFF.cum_c, NH, QT), S.t_sq[:, :],
                                              av(OFF.cum_s, NH, QT)))
        P.op("vector", lambda e: e.tensor_add(av(OFF.t_den, NH, QT), av(OFF.t_den, NH, QT),
                                              av(OFF.cum_c, NH, QT)))
        P.op("vector", lambda e: e.tensor_scalar(
            av(OFF.t_den, NH, QT), av(OFF.t_den, NH, QT), S.consts[0:16, C_EPSAT:C_EPSAT + 1], None,
            ALU.add))
        P.op("vector", lambda e: e.reciprocal(av(OFF.t_den, NH, QT), av(OFF.t_den, NH, QT)))
        P.op("vector", lambda e: e.tensor_mul(S.t_cq[:, :], S.t_cq[:, :], av(OFF.t_den, NH, QT)))
        P.op("vector", lambda e: e.tensor_mul(S.t_sq[:, :], S.t_sq[:, :], av(OFF.t_den, NH, QT)),
             tag=f"{pfx}.trig")
        if "dbg_trig" in dbg and q == 0:
            for j, srct in enumerate((S.t_cq, S.t_sq, S.t_ck, S.t_sk)):
                P.op("gpsimd", lambda e, j=j, srct=srct: e.dma_start(
                    out=dbg_dr["dbg_trig"][:, j * QT:(j + 1) * QT], in_=srct[:, :]), dn=16)
        P.reset_chain()
        vprev = [f"{pfx}.trig"]

        def vop(builder, waits=(), tag=None):
            t = P.op("vector", builder, waits=list(waits) + vprev, chain=False, tag=tag)
            vprev[0] = t
            return t

        for hec in range(HEC):
            st_off = OFF.wv_st[hec % 2]
            hi_off = OFF.wv_hi[hec % 2]
            lo_off = OFF.wv_lo[hec % 2]
            sw = [f"{pfx}.wv.{hec}"]
            if hec >= 2:
                sw.append(f"{pfx}.vp.{hec - 2}")
            P.op("scalar", lambda e, st_off=st_off, hi_off=hi_off: e.copy(
                av(hi_off, 128, 1024, F32R), av(st_off, 128, 1024)),
                waits=sw, chain=False, tag=f"{pfx}.wvh.{hec}")
            vop(lambda e, st_off=st_off, hi_off=hi_off, lo_off=lo_off:
                e.tensor_sub(av(lo_off, 128, 1024, F32R), av(st_off, 128, 1024),
                             av(hi_off, 128, 1024, F32R)),
                waits=[f"{pfx}.wvh.{hec}"], tag=f"{pfx}.wvs.{hec}")
            vw = [f"{pfx}.wvs.{hec}"]
            if hec >= 2:
                vw.append(f"{pfx}.kv.{hec - 2}")
            first = True
            for k in range(KC):
                for pi_ in range(3):
                    lh = av((hi_off if pi_ < 2 else lo_off) + k * 128, 128, 128, F32R)
                    rh = (S.xn_hi if pi_ != 1 else S.xn_lo)
                    P.op("tensor", lambda e, k=k, lh=lh, rh=rh, hec=hec,
                         st=(k == 0 and pi_ == 0), sp=(k == KC - 1 and pi_ == 2):
                         e.matmul(S.ps_v[hec % 2][:, :], lh,
                                  rh[:, k * 1024 + tb:k * 1024 + tb + QT],
                                  start=st, stop=sp),
                         waits=(vw if first else ()), chain=False,
                         tag=(f"{pfx}.vp.{hec}" if (k == KC - 1 and pi_ == 2) else None))
                    first = False
            if hec + 2 < HEC:
                P.op("sync", lambda e, hec=hec: e.dma_start(
                    out=av(OFF.wv_st[hec % 2], 128, 1024), in_=dr.wv[hec + 2, :, :]),
                    dn=16, tag=f"{pfx}.wv.{hec + 2}", chain=False,
                    waits=[f"{pfx}.vp.{hec}"], sem=("sWV0", "sWV1")[hec % 2])
            for j, (trg, evdst) in enumerate(((S.t_ck, OFF.bc_c), (S.t_sk, OFF.bc_s),
                                              (S.t_cq, OFF.bc_cq), (S.t_sq, OFF.bc_sq))):
                bw = [f"{pfx}.trig"] if hec == 0 else [f"{pfx}.ev.{hec - 1}.{j}"]
                P.op("tensor", lambda e, hec=hec, trg=trg, j=j: e.matmul(
                    S.ps_bc4[j][:, :], S.ind[:, hec * 128:(hec + 1) * 128], trg[:, :],
                    start=True, stop=True),
                    waits=bw, chain=False, tag=f"{pfx}.bc.{hec}.{j}")
                ew = [f"{pfx}.bc.{hec}.{j}"]
                if hec >= 1:
                    ew.append(f"{pfx}.bcr.{hec - 1}.{j}")
                P.op("scalar", lambda e, evdst=evdst, j=j: e.copy(
                    av(evdst, 128, QT), S.ps_bc4[j][:, :]),
                    waits=ew, chain=False, tag=f"{pfx}.ev.{hec}.{j}")
            vop(lambda e, hec=hec: e.tensor_mul(
                av(OFF.kv_c, 128, QT), S.ps_v[hec % 2][:, :], av(OFF.bc_c, 128, QT)),
                waits=[f"{pfx}.vp.{hec}", f"{pfx}.ev.{hec}.0"],
                tag=f"{pfx}.bcr.{hec}.0")
            vop(lambda e, hec=hec: e.tensor_mul(
                av(OFF.kv_s, 128, QT), S.ps_v[hec % 2][:, :], av(OFF.bc_s, 128, QT)),
                waits=[f"{pfx}.ev.{hec}.1"], tag=f"{pfx}.kv.{hec}")
            vop(lambda e, hec=hec: e.tensor_tensor_scan(
                av(OFF.cum_c, 128, QT), S.ones[:, :], av(OFF.kv_c, 128, QT),
                S.carry[:, hec:hec + 1], ALU.mult, ALU.add),
                tag=f"{pfx}.bcr.{hec}.1")
            vop(lambda e, hec=hec: e.tensor_copy(
                S.carry[:, hec:hec + 1], av(OFF.cum_c + QT - 1, 128, 1)))
            vop(lambda e, hec=hec: e.tensor_tensor_scan(
                av(OFF.cum_s, 128, QT), S.ones[:, :], av(OFF.kv_s, 128, QT),
                S.carry[:, HEC + hec:HEC + hec + 1], ALU.mult, ALU.add))
            vop(lambda e, hec=hec: e.tensor_copy(
                S.carry[:, HEC + hec:HEC + hec + 1], av(OFF.cum_s + QT - 1, 128, 1)))
            vop(lambda e: e.tensor_mul(av(OFF.kv_c, 128, QT),
                                       av(OFF.bc_cq, 128, QT),
                                       av(OFF.cum_c, 128, QT)),
                waits=[f"{pfx}.ev.{hec}.2"], tag=f"{pfx}.bcr.{hec}.2")
            vop(lambda e: e.tensor_mul(av(OFF.kv_s, 128, QT),
                                       av(OFF.bc_sq, 128, QT),
                                       av(OFF.cum_s, 128, QT)),
                waits=[f"{pfx}.ev.{hec}.3"], tag=f"{pfx}.bcr.{hec}.3")
            vop(lambda e: e.tensor_add(av(OFF.kv_c, 128, QT),
                                       av(OFF.kv_c, 128, QT),
                                       av(OFF.kv_s, 128, QT)))
            vop(lambda e, hec=hec: e.tensor_copy(
                av(OFF.heads_hi + hec * QT, 128, QT, F32R), av(OFF.kv_c, 128, QT)),
                tag=f"{pfx}.hh.{hec}")
            vop(lambda e, hec=hec: e.tensor_sub(
                av(OFF.heads_lo + hec * QT, 128, QT, F32R), av(OFF.kv_c, 128, QT),
                av(OFF.heads_hi + hec * QT, 128, QT, F32R)),
                tag=f"{pfx}.hd.{hec}")
        P.reset_chain()
        if "dbg_headshi" in dbg and q == 0:
            P.op("gpsimd", lambda e: e.dma_start(
                out=dbg_dr["dbg_headshi"], in_=av(OFF.heads_hi, 128, HEC * QT)), dn=16,
                waits=[f"{pfx}.hd.{HEC - 1}"])
        for hec in range(min(2, HEC)):
            P.op("sync", lambda e, hec=hec: e.dma_start(
                out=av(OFF.wv_st[hec % 2], 128, 1024), in_=dr.wo[hec, :, :]),
                dn=16, tag=f"{pfx}.wo.{hec}", chain=False,
                waits=[f"{pfx}.vp.{HEC - 1}"], sem=("sWV0", "sWV1")[hec % 2])
        for hec in range(HEC):
            st_off = OFF.wv_st[hec % 2]
            hi_off = OFF.wv_hi[hec % 2]
            lo_off = OFF.wv_lo[hec % 2]
            sw = [f"{pfx}.wo.{hec}"]
            if hec < 2:
                sw.append(f"{pfx}.vp.{HEC - 1}")
            else:
                sw.append(f"{pfx}.op.{hec - 2}")
            P.op("scalar", lambda e, st_off=st_off, hi_off=hi_off: e.copy(
                av(hi_off, 128, 1024, F32R), av(st_off, 128, 1024)),
                waits=sw, chain=False, tag=f"{pfx}.woh.{hec}")
            P.op("vector", lambda e, st_off=st_off, hi_off=hi_off, lo_off=lo_off:
                 e.tensor_sub(av(lo_off, 128, 1024, F32R), av(st_off, 128, 1024),
                              av(hi_off, 128, 1024, F32R)),
                 waits=[f"{pfx}.woh.{hec}"], chain=False, tag=f"{pfx}.wos.{hec}")
            first = True
            for tau in range(4):
                for dh in range(2):
                    for pi_ in range(3):
                        lh = av((OFF.heads_hi if pi_ != 2 else OFF.heads_lo)
                                + hec * QT + tau * 128, 128, 128, F32R)
                        rh = av((hi_off if pi_ != 1 else lo_off) + dh * 512,
                                128, 512, F32R)
                        fw = [f"{pfx}.wos.{hec}"]
                        if hec == 0:
                            fw.append(f"{pfx}.hd.{HEC - 1}")
                        P.op("tensor", lambda e, lh=lh, rh=rh, tau=tau, dh=dh,
                             st=(hec == 0 and pi_ == 0),
                             sp=(hec == HEC - 1 and pi_ == 2):
                             e.matmul(S.ps_out[tau][:, dh * 512:(dh + 1) * 512],
                                      lh, rh, start=st, stop=sp),
                             waits=(fw if first else ()), chain=False,
                             tag=(f"{pfx}.op.{hec}"
                                  if (tau == 3 and dh == 1 and pi_ == 2) else None))
                        first = False
            if hec + 2 < HEC:
                P.op("sync", lambda e, hec=hec: e.dma_start(
                    out=av(OFF.wv_st[hec % 2], 128, 1024), in_=dr.wo[hec + 2, :, :]),
                    dn=16, tag=f"{pfx}.wo.{hec + 2}", chain=False,
                    waits=[f"{pfx}.op.{hec}"], sem=("sWV0", "sWV1")[hec % 2])
        P.reset_chain()
        P.after(f"{pfx}.op.{HEC - 1}")
        for tau in range(4):
            P.op("scalar", lambda e, tau=tau: e.copy(
                S.acc[:, (qh * 4 + tau) * D:(qh * 4 + tau + 1) * D],
                S.ps_out[tau][:, :]))

    def ffn_sync(P, hh):
        pfx = f"f{hh}"
        for i in range(NI):
            ee, hc = divmod(i, NHC)
            waits = [f"{pfx}.eo.{i - 2}.{NTAU - 1}"] if i >= 2 else []
            for j, drw in enumerate((dr.w1, dr.w2, dr.w3)):
                P.op("sync", lambda e, j=j, drw=drw, ee=ee, hc=hc, i=i: e.dma_start(
                    out=av(OFF.wst[j][i % 2], 128, 1024), in_=drw[ee, hc, :, :]),
                    dn=16, tag=f"{pfx}.w.{i}.{j}", chain=False,
                    waits=(waits if j == 0 else ()), sem=("sW0", "sW1")[i % 2])

    def ffn_scalar(P, hh):
        pfx = f"f{hh}"

        def hi_copies(i):
            waits = [f"{pfx}.w.{i}.0", f"{pfx}.w.{i}.1", f"{pfx}.w.{i}.2"]
            if i >= 2:
                waits.append(f"{pfx}.eo.{i - 2}.{NTAU - 1}")
            for j in range(3):
                P.op("scalar", lambda e, j=j, i=i: e.copy(
                    av(OFF.whi[j][i % 2], 128, 1024, F32R),
                    av(OFF.wst[j][i % 2], 128, 1024)),
                    waits=(waits if j == 0 else ()), chain=False,
                    tag=(f"{pfx}.hi.{i}" if j == 2 else None))

        hi_copies(0)
        if NI > 1:
            hi_copies(1)
        for i in range(NI):
            P.op("scalar", lambda e, i=i: e.activation(
                av(OFF.t2, 128, 1024), S.ps_h1[:, :], ACT.Silu),
                waits=([f"{pfx}.h1.{i}"] + ([f"{pfx}.t.{i - 1}"] if i >= 1 else [])),
                chain=False, tag=f"{pfx}.sig.{i}")
            if i + 2 < NI:
                hi_copies(i + 2)

    def acc_adds(P, pfx, i):
        ee = i // NHC
        for tau in range(NTAU):
            if ee == 0:
                P.op("vector", lambda e, tau=tau: e.tensor_add(
                    S.acc[:, tau * D:(tau + 1) * D], S.ps_eo[tau % 2][:, :],
                    S.acc[:, tau * D:(tau + 1) * D]),
                    waits=[f"{pfx}.eo.{i}.{tau}"], chain=False,
                    tag=f"{pfx}.acc.{i}.{tau}")
            else:
                P.op("vector", lambda e, tau=tau, ee=ee: e.scalar_tensor_tensor(
                    S.acc[:, tau * D:(tau + 1) * D], S.ps_eo[tau % 2][:, :],
                    S.w_all[:, tau * 8 + ee - 1:tau * 8 + ee],
                    S.acc[:, tau * D:(tau + 1) * D], ALU.mult, ALU.add),
                    waits=[f"{pfx}.eo.{i}.{tau}"], chain=False,
                    tag=f"{pfx}.acc.{i}.{tau}")

    def ffn_vector(P, hh):
        pfx = f"f{hh}"

        def lo_subs(i):
            waits = [f"{pfx}.hi.{i}"]
            for j in range(3):
                P.op("vector", lambda e, j=j, i=i: e.tensor_sub(
                    av(OFF.wlo[j][i % 2], 128, 1024, F32R),
                    av(OFF.wst[j][i % 2], 128, 1024),
                    av(OFF.whi[j][i % 2], 128, 1024, F32R)),
                    waits=(waits if j == 0 else ()), chain=False,
                    tag=(f"{pfx}.lo.{i}" if j == 2 else None))

        lo_subs(0)
        if NI > 1:
            lo_subs(1)
        for i in range(NI):
            if i >= 1:
                acc_adds(P, pfx, i - 1)
            if i >= 1 and i + 1 < NI:
                lo_subs(i + 1)
            P.op("vector", lambda e, i=i: e.tensor_mul(
                av(OFF.t2, 128, 1024), av(OFF.t2, 128, 1024),
                S.ps_h2[:, :]),
                waits=[f"{pfx}.sig.{i}", f"{pfx}.h2.{i}"], chain=False,
                tag=f"{pfx}.t2m.{i}")
            P.op("vector", lambda e, i=i: e.tensor_copy(
                av(OFF.thi[i % 2], 128, 1024, F32R), av(OFF.t2, 128, 1024)),
                waits=[f"{pfx}.t2m.{i}"], chain=False, tag=f"{pfx}.thi.{i}")
            P.op("vector", lambda e, i=i: e.tensor_sub(
                av(OFF.tlo[i % 2], 128, 1024, F32R), av(OFF.t2, 128, 1024),
                av(OFF.thi[i % 2], 128, 1024, F32R)), chain=False,
                waits=[f"{pfx}.thi.{i}"], tag=f"{pfx}.t.{i}")
        acc_adds(P, pfx, NI - 1)
        for tau in range(NTAU):
            w = [f"{pfx}.acc.{NI - 1}.{tau}", f"{pfx}.xr.{tau}", f"{pfx}.hi.{NI - 1}"]
            if tau >= 2:
                w.append(f"{pfx}.y.{tau - 2}")
            P.op("vector", lambda e, tau=tau: e.tensor_add(
                av(OFF.ob[tau % 2], 128, 1024), S.acc[:, tau * D:(tau + 1) * D],
                av(OFF.xr[tau % 2], 128, 1024)),
                waits=w, chain=False, tag=f"{pfx}.add.{tau}")

    def eo_mms(P, pfx, i):
        for tau in range(NTAU):
            waits = []
            if tau == 0:
                waits.append(f"{pfx}.t.{i}")
            if tau >= 2:
                waits.append(f"{pfx}.acc.{i}.{tau - 2}")
            elif i >= 1:
                waits.append(f"{pfx}.acc.{i - 1}.{tau - 2 + NTAU}")
            first = True
            for dh in range(2):
                for pi_ in range(3):
                    thof = OFF.thi[i % 2] if pi_ != 2 else OFF.tlo[i % 2]
                    w3of = OFF.whi[2][i % 2] if pi_ != 1 else OFF.wlo[2][i % 2]
                    P.op("tensor", lambda e, tau=tau, dh=dh, thof=thof, w3of=w3of,
                         st=(pi_ == 0), sp=(pi_ == 2): e.matmul(
                        S.ps_eo[tau % 2][:, dh * 512:(dh + 1) * 512],
                        av(thof + tau * 128, 128, 128, F32R),
                        av(w3of + dh * 512, 128, 512, F32R),
                        start=st, stop=sp),
                        waits=(waits if first else ()), chain=False,
                        tag=(f"{pfx}.eo.{i}.{tau}" if (dh == 1 and pi_ == 2) else None))
                    first = False

    def ffn_h_mms(P, pfx, i, j, psn):
        waits = [f"{pfx}.lo.{i}"]
        if i >= 1:
            waits.append(f"{pfx}.sig.{i - 1}" if j == 0 else f"{pfx}.t2m.{i - 1}")
        first = True
        for nh in range(2):
            for k in range(KC):
                for pi_ in range(3):
                    lhof = OFF.whi[j][i % 2] if pi_ < 2 else OFF.wlo[j][i % 2]
                    rht = S.xn_hi if pi_ != 1 else S.xn_lo
                    P.op("tensor", lambda e, k=k, nh=nh, lhof=lhof, rht=rht,
                         psn=psn, st=(k == 0 and pi_ == 0),
                         sp=(k == KC - 1 and pi_ == 2): e.matmul(
                        getattr(S, psn)[:, nh * 512:(nh + 1) * 512],
                        av(lhof + k * 128, 128, 128, F32R),
                        rht[:, k * 1024 + nh * 512:k * 1024 + (nh + 1) * 512],
                        start=st, stop=sp),
                        waits=(waits if first else ()), chain=False,
                        tag=(f"{pfx}.h{j + 1}.{i}"
                             if (nh == 1 and k == KC - 1 and pi_ == 2) else None))
                    first = False

    def ffn_tensor(P, hh):
        pfx = f"f{hh}"
        for i in range(NI):
            ffn_h_mms(P, pfx, i, 0, "ps_h1")
            if i >= 1:
                eo_mms(P, pfx, i - 1)
            ffn_h_mms(P, pfx, i, 1, "ps_h2")
        eo_mms(P, pfx, NI - 1)

    def ph_p1d(P, hh):
        pfx = f"d{hh}"
        for tau in range(min(2, NTAU)):
            tg = hh * NTAU + tau
            P.op("gpsimd", lambda e, tau=tau, tg=tg: e.dma_start(
                out=av(OFF.xr[tau % 2], 128, 1024),
                in_=dr.x[tg * 128:(tg + 1) * 128, :]), dn=16, tag=f"{pfx}.xr.{tau}",
                chain=False, sem=("sXR0", "sXR1")[tau % 2])
        for tau in range(NTAU):
            tg = hh * NTAU + tau
            P.after(f"{pfx}.xr.{tau}")
            P.op("vector", lambda e, tau=tau: e.tensor_add(
                av(OFF.ob[tau % 2], 128, 1024), S.acc[:, tau * D:(tau + 1) * D],
                av(OFF.xr[tau % 2], 128, 1024)), tag=f"{pfx}.add.{tau}")
            if tau + 2 < NTAU:
                tg2 = hh * NTAU + tau + 2
                P.op("gpsimd", lambda e, tau=tau, tg2=tg2: e.dma_start(
                    out=av(OFF.xr[tau % 2], 128, 1024),
                    in_=dr.x[tg2 * 128:(tg2 + 1) * 128, :]), dn=16,
                    tag=f"{pfx}.xr.{tau + 2}", chain=False,
                    waits=[f"{pfx}.add.{tau}"], sem=("sXR0", "sXR1")[tau % 2])
            P.op("gpsimd", lambda e, tau=tau, tg=tg: e.dma_start(
                out=dr.y[tg * 128:(tg + 1) * 128, :], in_=av(OFF.ob[tau % 2], 128, 1024)),
                dn=16, tag=f"{pfx}.y.{tau}")
        P.wait("gpsimd", f"{pfx}.y.{NTAU - 1}")
        if "dbg_acc" in dbg and hh == 0:
            P.op("gpsimd", lambda e: e.dma_start(
                out=dbg_dr["dbg_acc"], in_=S.acc[:, :]), dn=16)

    def ph_final(P):
        if P.mode == 0:
            return
        if P.mode == 1:
            for s in ("sG", "sY0", "sY1"):
                P.tags.pop(f"__{s}_total", None)
                P.tags[f"__{s}_total"] = (s, P.counters.get(s, 0), "gpsimd", 16)
        elif P.filter == "gpsimd":
            for s in ("sG", "sY0", "sY1"):
                P.eng.wait_ge(P.sems[s], P.tags[f"__{s}_total"][1])

    def ffn_gpsimd(P, hh):
        pfx = f"f{hh}"
        for tau in range(NTAU):
            tg = hh * NTAU + tau
            w = [f"{pfx}.lo.{NI - 1}"] if tau < 2 else [f"{pfx}.add.{tau - 2}"]
            P.op("gpsimd", lambda e, tau=tau, tg=tg: e.dma_start(
                out=av(OFF.xr[tau % 2], 128, 1024),
                in_=dr.x[tg * 128:(tg + 1) * 128, :]), dn=16, tag=f"{pfx}.xr.{tau}",
                chain=False, waits=w, sem=("sXR0", "sXR1")[tau % 2])
            P.op("gpsimd", lambda e, tau=tau, tg=tg: e.dma_start(
                out=dr.y[tg * 128:(tg + 1) * 128, :],
                in_=av(OFF.ob[tau % 2], 128, 1024)),
                dn=16, tag=f"{pfx}.y.{tau}", chain=False,
                waits=[f"{pfx}.add.{tau}"], sem=("sY0", "sY1")[tau % 2])
        P.wait("gpsimd", f"{pfx}.y.{NTAU - 2}", f"{pfx}.y.{NTAU - 1}")

    def ffn_phase(hh):
        return {"sync": lambda P: ffn_sync(P, hh),
                "tensor": lambda P: ffn_tensor(P, hh),
                "vector": lambda P: ffn_vector(P, hh),
                "scalar": lambda P: ffn_scalar(P, hh),
                "gpsimd": lambda P: ffn_gpsimd(P, hh)}

    PH = [("init", "seq", ph_init)]
    if with_attn:
        PH.append(("p0", "seq", ph_p0))
    PH.append(("p1a0", "seq", lambda P: ph_p1a(P, 0)))
    if with_attn:
        PH.append(("atn0", "seq", lambda P: ph_atn(P, 0)))
        PH.append(("atn1", "seq", lambda P: ph_atn(P, 1)))
    PH.append(("ffn0", "streams", ffn_phase(0)))
    PH.append(("p1a1", "seq", lambda P: ph_p1a(P, 1)))
    if with_attn:
        PH.append(("atn2", "seq", lambda P: ph_atn(P, 2)))
        PH.append(("atn3", "seq", lambda P: ph_atn(P, 3)))
    PH.append(("ffn1", "streams", ffn_phase(1)))
    PH.append(("fin", "seq", ph_final))
    if only is not None:
        keep = set(only) | {"init", "fin"}
        PH = [p for p in PH if p[0] in keep]

    STREAM_ORDER = ("sync", "tensor", "vector", "scalar", "gpsimd")

    for mode in (0, 1):
        P.mode = mode
        P.anon = 0
        P.anon_marks = {}
        for name, style, body in PH:
            P.anon_marks[name] = P.anon
            P.reset_chain()
            if style == "seq":
                body(P)
            else:
                for engname in STREAM_ORDER:
                    if engname in body:
                        body[engname](P)

    P.mode = 2
    P.sems = sems

    for name, style, body in PH:
        with ExitStack() as pst:
            if name == "init":
                S.ps_srow = pst.enter_context(nc.psum_tensor("ini_srow", [128, 16], F32))
            elif name == "p0":
                S.ps_wsc = pst.enter_context(nc.psum_tensor(name + "_wsc", [NH, D], F32))
                S.ps_wss = pst.enter_context(nc.psum_tensor(name + "_wss", [NH, D], F32))
                kkb = pst.enter_context(nc.psum_tensor(name + "_kk", [NH, 512], F32))
                ccb = pst.enter_context(nc.psum_tensor(name + "_cc", [NH, 512], F32))
                sh1 = pst.enter_context(nc.psum_tensor(name + "_sh1", [128, 512], F32))
                sh2 = pst.enter_context(nc.psum_tensor(name + "_sh2", [128, 512], F32))
                S.ps_kk = kkb[:, 0:2]
                S.ps_cc = ccb[:, 0:2]
                S.ps_k = [sh1[:, 128:144], sh2[:, 160:176]]
                S.ps_trc = sh1[:, 144:160]
                S.ps_trs = sh2[:, 144:160]
                S.ps_car2 = S.ps_wsc
                S.ps_ctr = sh2[0:64, 128:130]
            elif name.startswith("p1a"):
                S.ps_tr8 = pst.enter_context(nc.psum_tensor(name + "_tr8", [128, 1024], F32))
                S.ps_g = pst.enter_context(nc.psum_tensor(name + "_g", [128, 8], F32))
            elif name.startswith("atn"):
                big1 = pst.enter_context(nc.psum_tensor(name + "_b1", [128, 2048], F32))
                big2 = pst.enter_context(nc.psum_tensor(name + "_b2", [128, 2048], F32))
                S.ps_q = big2[0:NH, 0:QT]
                S.ps_k16 = big2[0:NH, 512:512 + QT]
                S.ps_v = [big2[:, 0:512], big2[:, 512:1024]]
                S.ps_bc4 = [big1[:, j * 512:(j + 1) * 512] for j in range(4)]
                S.ps_out = [big1[:, 0:1024], big1[:, 1024:2048],
                            big2[:, 0:1024], big2[:, 1024:2048]]
            elif name.startswith("ffn"):
                S.ps_h1 = pst.enter_context(nc.psum_tensor(name + "_h1", [128, 1024], F32))
                S.ps_h2 = pst.enter_context(nc.psum_tensor(name + "_h2", [128, 1024], F32))
                S.ps_eo = [pst.enter_context(nc.psum_tensor(name + "_e0", [128, 1024], F32)),
                           pst.enter_context(nc.psum_tensor(name + "_e1", [128, 1024], F32))]
            with nc.Block() as blk:
                if style == "seq":
                    for engname in ENGINES:
                        def run(engobj, engname=engname):
                            P.filter = engname
                            P.eng = engobj
                            P.anon = P.anon_marks[name]
                            P.reset_chain()
                            body(P)
                        getattr(blk, engname)(run)
                else:
                    P.anon = P.anon_marks[name]
                    for engname in STREAM_ORDER:
                        def run(engobj, engname=engname, fn=body[engname]):
                            P.filter = engname
                            P.eng = engobj
                            fn(P)
                        getattr(blk, engname)(run)

    ctx.close()
    return nc, list(dbg_dr.keys())


def kernel_run(inputs, with_attn=True, dbg=(), core_ids=None, **kw):
    shared, per_core = host_prepare(inputs)
    nc, dbg_names = make_kernel(with_attn=with_attn, dbg=dbg)
    in_maps = []
    for c in range(NCORES):
        m = dict(shared)
        m.update(per_core[c])
        in_maps.append(m)
    res = run_bass_kernel_spmd(nc, in_maps, core_ids=list(range(NCORES)), **kw)
    out = np.empty((B, T, D), dtype=np.float32)
    for c in range(NCORES):
        b, s = divmod(c, 4)
        out[b, s * TPC:(s + 1) * TPC, :] = res.results[c]["y"]
    return out, res


def kernel(**inputs):
    out, _ = kernel_run(inputs, with_attn=True)
    return out


# version 09 (final)
# speedup vs baseline: 1.0087x; 1.0030x over previous
import numpy as np
from contextlib import ExitStack

import concourse.bass as bass
import concourse.mybir as mybir
from concourse.bass_utils import run_bass_kernel_spmd

F32 = mybir.dt.float32
F32R = mybir.dt.float32r
ALU = mybir.AluOpType
ACT = mybir.ActivationFunctionType
AX = mybir.AxisListType.X

B, T, D = 2, 8192, 1024
NCORES = 8
TPC = 2048
HALFT = 1024
NQ = 4
QT = 512
NTAU = 8
KC = 8
NH = 16
DH = 64
HEC = 8
NE = 9
NHC = 22
HFF = 2730
HPAD = NHC * 128
PRIOR = 6144
NPT = PRIOR // 128
PI = float(np.pi)
NI = NE * NHC

C_PI2, C_PI4, C_I1024, C_EPSLN, C_EPSAT, C_35, C_NI1024 = 0, 1, 2, 3, 4, 5, 6

ENGINES = ("sync", "tensor", "vector", "scalar", "gpsimd")
ESEM = {"sync": "sL", "tensor": "sT", "vector": "sV", "scalar": "sS", "gpsimd": "sG"}


class Prog:
    def __init__(self):
        self.mode = 0
        self.counters = {}
        self.tags = {}
        self.used = set()
        self.filter = None
        self.eng = None
        self.sems = None
        self.chain = None
        self.pending = []
        self.anon = 0
        self.anon_marks = {}

    @property
    def counting(self):
        return self.mode != 2

    def op(self, engine, builder, waits=(), tag=None, dn=1, chain=True, sem=None):
        sem_name = sem or ESEM[engine]
        if tag is None:
            tag = f"_a{self.anon}"
            self.anon += 1
        if self.mode == 0:
            self.used.update(waits)
            if chain:
                self.used.update(self.pending)
                if self.chain is not None:
                    self.used.add(self.chain)
                self.pending = []
                self.chain = tag
            return tag
        live = (tag in self.used) or dn != 1
        if self.mode == 1:
            if live:
                self.counters[sem_name] = self.counters.get(sem_name, 0) + dn
            assert tag not in self.tags, f"dup tag {tag}"
            self.tags[tag] = (sem_name, self.counters.get(sem_name, 0), engine, dn)
            if chain:
                self.chain = tag
                self.pending = []
            return tag
        allw = list(waits)
        if chain:
            allw += self.pending
            if self.chain is not None:
                allw.append(self.chain)
            self.pending = []
            self.chain = tag
        if engine == self.filter:
            need = {}
            for lab in allw:
                s, v, prod_eng, pdn = self.tags[lab]
                if v > need.get(s, -1):
                    need[s] = v
            for s, v in need.items():
                self.eng.wait_ge(self.sems[s], v)
            ins = builder(self.eng)
            if live:
                ins.then_inc(self.sems[sem_name], dn)
        return tag

    def wait(self, engine, *labels):
        if self.mode == 0:
            self.used.update(labels)
            return
        if self.mode == 1 or engine != self.filter:
            return
        need = {}
        for lab in labels:
            s, v, prod_eng, pdn = self.tags[lab]
            if v > need.get(s, -1):
                need[s] = v
        for s, v in need.items():
            self.eng.wait_ge(self.sems[s], v)

    def after(self, *labels):
        if self.mode == 0:
            self.used.update(labels)
        self.pending += list(labels)

    def reset_chain(self):
        self.chain = None
        self.pending = []


def host_prepare(inp):
    out = {}
    q = np.ascontiguousarray(np.asarray(inp["L0_attn_query"], dtype=np.float32))

    WQ = np.asarray(inp["L0_attn_W_Q"], dtype=np.float32)[:, :, 0]
    WK = np.asarray(inp["L0_attn_W_K"], dtype=np.float32)[:, :, 0]
    WQK = np.concatenate([WQ.T, WK.T], axis=1)
    out["wqk"] = np.ascontiguousarray(
        WQK.reshape(KC, 128, 32).transpose(1, 0, 2).reshape(128, KC * 32))

    WV = np.asarray(inp["L0_attn_W_V"], dtype=np.float32)
    WVp = WV.transpose(1, 0, 2).reshape(D, NH * DH)
    out["wv"] = np.ascontiguousarray(
        WVp.reshape(KC, 128, HEC, 128).transpose(2, 1, 0, 3).reshape(HEC, 128, KC * 128))
    out["wvf"] = np.ascontiguousarray(
        WVp.reshape(KC, 128, NH * DH).transpose(1, 0, 2).reshape(128, KC * NH * DH))

    WO = np.asarray(inp["L0_attn_W_O"], dtype=np.float32)
    WOp = WO.transpose(2, 1, 0).reshape(NH * DH, D)
    out["wo"] = np.ascontiguousarray(WOp.reshape(HEC, 128, D))

    Wg = np.asarray(inp["L0_ffn_Wg"], dtype=np.float32)
    out["wg"] = np.ascontiguousarray(
        Wg.T.reshape(KC, 128, 8).transpose(1, 0, 2).reshape(128, KC * 8))
    out["gbias"] = np.ascontiguousarray(
        np.asarray(inp["L0_ffn_routing_bias"], dtype=np.float32).reshape(1, 8))

    def prep_w12(w):
        e = w.shape[0]
        wp = np.zeros((e, HPAD, D), dtype=np.float32)
        wp[:, :HFF, :] = w
        return np.ascontiguousarray(
            wp.reshape(e, NHC, 128, KC, 128).transpose(0, 1, 4, 3, 2)
            .reshape(e, NHC, 128, KC * 128))

    def prep_w3(w):
        e = w.shape[0]
        wp = np.zeros((e, D, HPAD), dtype=np.float32)
        wp[:, :, :HFF] = w
        return np.ascontiguousarray(wp.transpose(0, 2, 1).reshape(e, NHC, 128, D))

    out["w1"] = prep_w12(np.concatenate(
        [np.asarray(inp["L0_ffn_Ws1"], dtype=np.float32)[None],
         np.asarray(inp["L0_ffn_W1"], dtype=np.float32)], axis=0))
    out["w2"] = prep_w12(np.concatenate(
        [np.asarray(inp["L0_ffn_Ws2"], dtype=np.float32)[None],
         np.asarray(inp["L0_ffn_W2"], dtype=np.float32)], axis=0))
    out["w3"] = prep_w3(np.concatenate(
        [np.asarray(inp["L0_ffn_Ws3"], dtype=np.float32)[None],
         np.asarray(inp["L0_ffn_W3"], dtype=np.float32)], axis=0))

    out["idn"] = np.eye(128, dtype=np.float32)
    ind = np.zeros((NH, HEC * 128), dtype=np.float32)
    for j in range(HEC * 128):
        ind[j >> 6, j] = 1.0
    out["ind"] = np.ascontiguousarray(ind)
    lt = np.zeros((128, 64), dtype=np.float32)
    for e in range(8):
        lt[:, e * 8:e * 8 + e] = 1.0
    out["lt"] = lt

    per_core = []
    for c in range(NCORES):
        b, s = divmod(c, 4)
        xo = q[b, s * TPC:(s + 1) * TPC, :]
        xp = np.zeros((PRIOR, D), dtype=np.float32)
        pm = np.zeros((PRIOR, 1), dtype=np.float32)
        n_pr = s * TPC
        if n_pr:
            xp[PRIOR - n_pr:, :] = q[b, :n_pr, :]
            pm[PRIOR - n_pr:, :] = 1.0
        xpt = np.ascontiguousarray(
            xp.reshape(NPT, 128, KC, 128).transpose(3, 0, 2, 1).reshape(128, NPT * 1024))
        per_core.append({"x": np.ascontiguousarray(xo), "xp": xp, "xpt": xpt, "pm": pm})
    return out, per_core


class NS:
    pass


def make_kernel(with_attn=True, dbg=(), only=None):
    nc = bass.Bass("TRN2", target_bir_lowering=False)
    S = NS()
    dbg = set(dbg)

    dr = NS()
    dr.x = nc.dram_tensor("x", [TPC, D], F32, kind="ExternalInput").ap()
    dr.xp = nc.dram_tensor("xp", [PRIOR, D], F32, kind="ExternalInput").ap()
    dr.pm = nc.dram_tensor("pm", [PRIOR, 1], F32, kind="ExternalInput").ap()
    dr.xpt = nc.dram_tensor("xpt", [128, NPT * 1024], F32, kind="ExternalInput").ap()
    dr.wqk = nc.dram_tensor("wqk", [128, KC * 32], F32, kind="ExternalInput").ap()
    dr.wv = nc.dram_tensor("wv", [HEC, 128, KC * 128], F32, kind="ExternalInput").ap()
    dr.wvf = nc.dram_tensor("wvf", [128, KC * 1024], F32, kind="ExternalInput").ap()
    dr.wo = nc.dram_tensor("wo", [HEC, 128, D], F32, kind="ExternalInput").ap()
    dr.wg = nc.dram_tensor("wg", [128, KC * 8], F32, kind="ExternalInput").ap()
    dr.gbias = nc.dram_tensor("gbias", [1, 8], F32, kind="ExternalInput").ap()
    dr.w1 = nc.dram_tensor("w1", [NE, NHC, 128, KC * 128], F32, kind="ExternalInput").ap()
    dr.w2 = nc.dram_tensor("w2", [NE, NHC, 128, KC * 128], F32, kind="ExternalInput").ap()
    dr.w3 = nc.dram_tensor("w3", [NE, NHC, 128, D], F32, kind="ExternalInput").ap()
    dr.idn = nc.dram_tensor("idn", [128, 128], F32, kind="ExternalInput").ap()
    dr.ind = nc.dram_tensor("ind", [NH, HEC * 128], F32, kind="ExternalInput").ap()
    dr.lt = nc.dram_tensor("lt", [128, 64], F32, kind="ExternalInput").ap()
    dr.y = nc.dram_tensor("y", [TPC, D], F32, kind="ExternalOutput").ap()
    dbg_dr = {}

    def dbg_out(name, shape):
        if name in dbg:
            dbg_dr[name] = nc.dram_tensor(name, shape, F32, kind="ExternalOutput").ap()

    dbg_out("dbg_xnhi", [128, KC * HALFT])
    dbg_out("dbg_xnlo", [128, KC * HALFT])
    dbg_out("dbg_w", [128, NTAU * 8])
    dbg_out("dbg_acc", [128, NTAU * D])
    dbg_out("dbg_headshi", [128, HEC * QT])
    dbg_out("dbg_carry", [128, 2 * HEC])
    dbg_out("dbg_carryK", [NH, 2])
    dbg_out("dbg_trig", [NH, 4 * QT])
    dbg_out("dbg_wsumT", [128, KC * 32])

    ctx = ExitStack()
    sb = lambda name, shape, dt=F32: ctx.enter_context(
        nc.sbuf_tensor("sb_" + name, shape, dt))

    S.xn_hi = sb("xn_hi", [128, KC * HALFT], F32R)
    S.xn_lo = sb("xn_lo", [128, KC * HALFT], F32R)
    S.acc = sb("acc", [128, NTAU * D])
    S.arena = sb("arena", [128, 7168])
    S.arenb = sb("arenb", [128, 16384], F32R)
    S.ones = sb("ones", [128, QT])
    S.onecol = sb("onecol", [128, 1])
    S.consts = sb("consts", [128, 8])
    S.idn = sb("idn", [128, 128])
    S.lt = sb("lt", [128, 64])
    S.ind = sb("ind", [NH, HEC * 128])
    S.wqk = sb("wqk", [128, KC * 32])
    S.wqk_hi = sb("wqk_hi", [128, KC * 32], F32R)
    S.wqk_lo = sb("wqk_lo", [128, KC * 32], F32R)
    S.wg_hi = sb("wg_hi", [128, KC * 8], F32R)
    S.wg_lo = sb("wg_lo", [128, KC * 8], F32R)
    S.gbias = sb("gbias", [1, 8])
    S.w_all = sb("w_all", [128, NTAU * 8])
    S.carry = sb("carry", [128, 2 * HEC])
    S.carryK = sb("carryK", [NH, 2])
    S.t_cq = sb("t_cq", [NH, QT])
    S.t_sq = sb("t_sq", [NH, QT])
    S.t_ck = sb("t_ck", [NH, QT])
    S.t_sk = sb("t_sk", [NH, QT])
    S.g_s = sb("g_s", [128, 8])
    S.tmp8a = sb("tmp8a", [128, 8])
    S.tmp8b = sb("tmp8b", [128, 8])
    S.rank = sb("rank", [128, 8])
    S.rmask = sb("rmask", [128, 8])
    S.eg = sb("eg", [128, 8])
    S.red1 = sb("red1", [128, 2])
    S.red2 = sb("red2", [128, 2])
    S.ln_m = sb("ln_m", [128, 2])
    S.ln_r = sb("ln_r", [128, 2])
    S.rt_m = sb("rt_m", [128, 1])
    S.rt_s = sb("rt_s", [128, 1])
    S.rt_r = sb("rt_r", [128, 1])
    S.cs_k = sb("cs_k", [128, 64])
    S.cs_a = sb("cs_a", [128, 64])
    S.srow = sb("srow", [128, 16])
    S.negm = sb("negm", [128, 2])
    S.pmask = sb("pmask", [128, 2])

    A = S.arena

    def av(off, p, n, dt=F32):
        if off < 16384:
            ap = S.arenb[0:p, off:off + n]
            if dt is F32:
                ap = ap.bitcast(F32)
        else:
            ap = S.arena[0:p, off - 16384:off - 16384 + n]
            if dt is F32R:
                ap = ap.bitcast(F32R)
        return ap

    OFF = NS()
    OFF.whi = [[0, 1024], [2048, 3072], [4096, 5120]]
    OFF.wlo = [[6144, 7168], [8192, 9216], [10240, 11264]]
    OFF.thi = [12288, 13312]
    OFF.tlo = [14336, 15360]
    OFF.heads_hi = 0
    OFF.heads_lo = 4096
    OFF.wv_hi = [8192, 9216]
    OFF.wv_lo = [10240, 11264]
    OFF.wo_hi = [12288]
    OFF.wo_lo = [13312]
    OFF.wst = [[16384, 17408], [18432, 19456], [20480, 21504]]
    OFF.t2 = 22528
    OFF.xr = [16384, 17408]
    OFF.ob = [18432, 19456]
    OFF.wv_st = [16384, 17408]
    OFF.wo_st = [18432]
    OFF.bc_c = 19456
    OFF.bc_s = 19968
    OFF.bc_cq = 18432
    OFF.bc_sq = 18944
    OFF.kv_c = 20480
    OFF.kv_s = 20992
    OFF.cum_c = 21504
    OFF.cum_s = 22016
    OFF.p1_x = [16384, 17408]
    OFF.p1_tm = [18432, 19456]
    OFF.p0_x = [16384, 17408]
    OFF.p0_xt = [18432, 19456]
    OFF.p0_wvst = [20480]
    OFF.wsumT = 18432
    OFF.p0_wsc = 21504
    OFF.p0_wss = 22528
    OFF.p0_sq = 22528
    OFF.ln_sq = 22528
    OFF.t_den = 22528

    sems = {}
    for sname in ("sL", "sT", "sV", "sS", "sG", "sI", "sW0", "sW1", "sA", "sB",
                  "sWV0", "sWV1", "sWO", "sXR0", "sXR1", "sY0", "sY1"):
        sems[sname] = ctx.enter_context(nc.semaphore(sname))

    P = Prog()

    def ph_init(P):
        for nm in ("idn", "ind", "lt", "wqk", "gbias"):
            P.op("sync", lambda e, nm=nm: e.dma_start(out=getattr(S, nm)[:, :],
                                                      in_=getattr(dr, nm)),
                 dn=16, tag=f"ini.{nm}", chain=False, sem="sI")
        P.op("vector", lambda e: e.memset(S.ones[:, :], 1.0))
        P.op("vector", lambda e: e.memset(S.onecol[:, :], 1.0))
        P.op("vector", lambda e: e.memset(S.consts[:, C_PI2:C_PI2 + 1], PI / 2))
        P.op("vector", lambda e: e.memset(S.consts[:, C_PI4:C_PI4 + 1], PI / 4))
        P.op("vector", lambda e: e.memset(S.consts[:, C_I1024:C_I1024 + 1], 1.0 / 1024.0))
        P.op("vector", lambda e: e.memset(S.consts[:, C_EPSLN:C_EPSLN + 1], 1e-5))
        P.op("vector", lambda e: e.memset(S.consts[:, C_EPSAT:C_EPSAT + 1], 1e-6))
        P.op("vector", lambda e: e.memset(S.consts[:, C_35:C_35 + 1], 3.5))
        P.op("vector", lambda e: e.memset(S.consts[:, C_NI1024:C_NI1024 + 1], -1.0 / 1024.0))
        P.op("vector", lambda e: e.memset(S.carry[:, :], 0.0))
        P.op("vector", lambda e: e.memset(S.carryK[:, :], 0.0))
        P.op("sync", lambda e: e.dma_start(out=av(16384, 128, KC * 8), in_=dr.wg),
             dn=16, tag="ini.wg", chain=False, sem="sI")
        P.after("ini.gbias", "ini.wg")
        P.op("vector", lambda e: e.tensor_copy(S.wqk_hi[:, :], S.wqk[:, :]))
        P.op("vector", lambda e: e.tensor_sub(S.wqk_lo[:, :], S.wqk[:, :], S.wqk_hi[:, :]))
        P.op("vector", lambda e: e.tensor_copy(S.wg_hi[:, :], av(16384, 128, KC * 8)))
        P.op("vector", lambda e: e.tensor_sub(S.wg_lo[:, :], av(16384, 128, KC * 8), S.wg_hi[:, :]))
        for k in range(KC):
            P.op("tensor", lambda e, k=k: e.matmul(
                S.ps_srow[:, :], S.ones[0:128, 0:128],
                S.wqk[:, k * 32 + 16:k * 32 + 32],
                start=(k == 0), stop=(k == KC - 1)))
        P.op("scalar", lambda e: e.copy(S.srow[:, :], S.ps_srow[:, :]))

    def ph_p0(P):
        vprev = [None]
        sprev = [None]

        def vop(builder, waits=(), tag=None):
            w = list(waits) + ([vprev[0]] if vprev[0] else [])
            t = P.op("vector", builder, waits=w, chain=False, tag=tag)
            vprev[0] = t
            return t

        def sop(builder, waits=(), tag=None):
            w = list(waits) + ([sprev[0]] if sprev[0] else [])
            t = P.op("scalar", builder, waits=w, chain=False, tag=tag)
            sprev[0] = t
            return t

        for i in range(min(2, NPT)):
            P.op("sync", lambda e, i=i: e.dma_start(
                out=av(OFF.p0_x[i % 2], 128, 1024),
                in_=dr.xp[i * 128:(i + 1) * 128, :]), dn=16,
                tag=f"p0.x.{i}", chain=False, sem=("sA", "sB")[i % 2])
            P.op("sync", lambda e, i=i: e.dma_start(
                out=av(OFF.p0_xt[i % 2], 128, 1024),
                in_=dr.xpt[:, i * 1024:(i + 1) * 1024]), dn=16,
                tag=f"p0.xt.{i}", chain=False, sem=("sA", "sB")[i % 2])
            P.op("sync", lambda e, i=i: e.dma_start(
                out=S.pmask[:, i % 2:i % 2 + 1],
                in_=dr.pm[i * 128:(i + 1) * 128, :]), dn=16, tag=f"p0.pm.{i}",
                chain=False, sem=("sA", "sB")[i % 2])
        for i in range(NPT):
            q2 = i % 2
            xs = lambda i=i: av(OFF.p0_x[i % 2], 128, 1024)
            vop(lambda e, i=i: e.tensor_mul(av(OFF.p0_sq, 128, 1024), xs(i), xs(i)),
                waits=[f"p0.x.{i}", f"p0.xt.{i}", f"p0.pm.{i}"])
            vop(lambda e, i=i, q2=q2: e.reduce_sum(S.red1[:, q2:q2 + 1], xs(i), axis=AX),
                tag=f"p0.vr1.{i}")
            vop(lambda e, q2=q2: e.reduce_sum(S.red2[:, q2:q2 + 1],
                                              av(OFF.p0_sq, 128, 1024), axis=AX),
                tag=f"p0.vr2.{i}")
            sop(lambda e, q2=q2: e.activation(
                S.negm[:, q2:q2 + 1], S.red1[:, q2:q2 + 1], ACT.Copy,
                scale=S.consts[:, C_NI1024:C_NI1024 + 1]), waits=[f"p0.vr1.{i}"])
            sop(lambda e, q2=q2: e.activation(S.ln_m[:, q2:q2 + 1],
                                              S.negm[:, q2:q2 + 1], ACT.Square),
                tag=f"p0.a1.{i}")
            vop(lambda e, q2=q2: e.tensor_scalar(
                S.red2[:, q2:q2 + 1], S.red2[:, q2:q2 + 1],
                S.consts[:, C_I1024:C_I1024 + 1], S.ln_m[:, q2:q2 + 1],
                ALU.mult, ALU.subtract), waits=[f"p0.a1.{i}"], tag=f"p0.vv.{i}")
            sop(lambda e, q2=q2: e.activation(S.ln_r[:, q2:q2 + 1], S.red2[:, q2:q2 + 1],
                                              ACT.Sqrt,
                                              bias=S.consts[:, C_EPSLN:C_EPSLN + 1]),
                waits=[f"p0.vv.{i}"], tag=f"p0.a2.{i}")
            vop(lambda e, q2=q2: e.reciprocal(S.ln_r[:, q2:q2 + 1], S.ln_r[:, q2:q2 + 1]),
                waits=[f"p0.a2.{i}"])
            kw = [f"p0.x.{i}", f"p0.xt.{i}", f"p0.pm.{i}"]
            if i >= 2:
                kw.append(f"p0.kc.{i - 2}")
            for k in range(KC):
                P.op("tensor", lambda e, k=k, i=i, q2=q2: e.matmul(
                    S.ps_k[q2][:, :], av(OFF.p0_xt[i % 2] + k * 128, 128, 128),
                    S.wqk[:, k * 32 + 16:k * 32 + 32],
                    start=(k == 0), stop=(k == KC - 1)),
                    waits=(kw if k == 0 else ()), chain=False,
                    tag=(f"p0.kraw.{i}" if k == KC - 1 else None))
            vop(lambda e, q2=q2: e.scalar_tensor_tensor(
                S.cs_k[:, q2 * 32:q2 * 32 + 16], S.srow[:, :], S.negm[:, q2:q2 + 1],
                S.ps_k[q2][:, :], ALU.mult, ALU.add), waits=[f"p0.kraw.{i}"])
            vop(lambda e, q2=q2: e.tensor_scalar(
                S.cs_k[:, q2 * 32:q2 * 32 + 16], S.cs_k[:, q2 * 32:q2 * 32 + 16],
                S.ln_r[:, q2:q2 + 1], None, ALU.mult), tag=f"p0.kc.{i}")
            sop(lambda e, q2=q2: e.activation(S.cs_k[:, q2 * 32:q2 * 32 + 16],
                                              S.cs_k[:, q2 * 32:q2 * 32 + 16], ACT.Tanh),
                waits=[f"p0.kc.{i}"])
            sop(lambda e, q2=q2: e.activation(
                S.cs_k[:, q2 * 32 + 16:q2 * 32 + 32], S.cs_k[:, q2 * 32:q2 * 32 + 16],
                ACT.Sin, scale=S.consts[:, C_PI4:C_PI4 + 1]))
            sop(lambda e, q2=q2: e.activation(
                S.cs_k[:, q2 * 32:q2 * 32 + 16], S.cs_k[:, q2 * 32:q2 * 32 + 16],
                ACT.Sin, bias=S.consts[:, C_PI2:C_PI2 + 1],
                scale=S.consts[:, C_PI4:C_PI4 + 1]), tag=f"p0.trig.{i}")
            vop(lambda e, i=i, q2=q2: e.tensor_scalar(
                S.cs_k[:, q2 * 32:q2 * 32 + 32], S.cs_k[:, q2 * 32:q2 * 32 + 32],
                S.pmask[:, i % 2:i % 2 + 1], None, ALU.mult),
                waits=[f"p0.trig.{i}"], tag=f"p0.msk.{i}")
            vop(lambda e, q2=q2: e.tensor_scalar(
                S.cs_a[:, q2 * 32:q2 * 32 + 32], S.cs_k[:, q2 * 32:q2 * 32 + 32],
                S.ln_r[:, q2:q2 + 1], None, ALU.mult), tag=f"p0.a.{i}")
            for n in range(2):
                P.op("tensor", lambda e, n=n, i=i, q2=q2: e.matmul(
                    S.ps_wsc[:, n * 512:(n + 1) * 512], S.cs_a[:, q2 * 32:q2 * 32 + 16],
                    av(OFF.p0_x[i % 2] + n * 512, 128, 512),
                    start=(i == 0), stop=(i == NPT - 1)),
                    waits=([f"p0.a.{i}"] if n == 0 else ()), chain=False)
                P.op("tensor", lambda e, n=n, i=i, q2=q2: e.matmul(
                    S.ps_wss[:, n * 512:(n + 1) * 512],
                    S.cs_a[:, q2 * 32 + 16:q2 * 32 + 32],
                    av(OFF.p0_x[i % 2] + n * 512, 128, 512),
                    start=(i == 0), stop=(i == NPT - 1)), chain=False)
            P.op("tensor", lambda e, i=i, q2=q2: e.matmul(
                S.ps_kk[0:16, 0:1], S.cs_k[:, q2 * 32:q2 * 32 + 16], S.onecol[:, :],
                start=(i == 0), stop=False), chain=False)
            P.op("tensor", lambda e, i=i, q2=q2: e.matmul(
                S.ps_kk[0:16, 1:2], S.cs_k[:, q2 * 32 + 16:q2 * 32 + 32], S.onecol[:, :],
                start=False, stop=(i == NPT - 1)), chain=False)
            P.op("tensor", lambda e, i=i, q2=q2: e.matmul(
                S.ps_cc[0:16, 0:1], S.cs_a[:, q2 * 32:q2 * 32 + 16], S.negm[:, q2:q2 + 1],
                start=(i == 0), stop=False), chain=False)
            P.op("tensor", lambda e, i=i, q2=q2: e.matmul(
                S.ps_cc[0:16, 1:2], S.cs_a[:, q2 * 32 + 16:q2 * 32 + 32],
                S.negm[:, q2:q2 + 1],
                start=False, stop=(i == NPT - 1)), chain=False, tag=f"p0.wsum.{i}")
            if i + 2 < NPT:
                P.op("sync", lambda e, i=i: e.dma_start(
                    out=av(OFF.p0_x[i % 2], 128, 1024),
                    in_=dr.xp[(i + 2) * 128:(i + 3) * 128, :]), dn=16,
                    tag=f"p0.x.{i + 2}", chain=False, waits=[f"p0.wsum.{i}"],
                    sem=("sA", "sB")[i % 2])
                P.op("sync", lambda e, i=i: e.dma_start(
                    out=av(OFF.p0_xt[i % 2], 128, 1024),
                    in_=dr.xpt[:, (i + 2) * 1024:(i + 3) * 1024]), dn=16,
                    tag=f"p0.xt.{i + 2}", chain=False, waits=[f"p0.kraw.{i}"],
                    sem=("sA", "sB")[i % 2])
                P.op("sync", lambda e, i=i: e.dma_start(
                    out=S.pmask[:, i % 2:i % 2 + 1],
                    in_=dr.pm[(i + 2) * 128:(i + 3) * 128, :]), dn=16,
                    tag=f"p0.pm.{i + 2}", chain=False, waits=[f"p0.msk.{i}"],
                    sem=("sA", "sB")[i % 2])
        P.reset_chain()
        P.after(f"p0.wsum.{NPT - 1}", f"p0.a.{NPT - 1}")
        P.op("scalar", lambda e: e.copy(S.carryK[:, 0:1], S.ps_kk[0:16, 0:1]))
        P.op("scalar", lambda e: e.copy(S.carryK[:, 1:2], S.ps_kk[0:16, 1:2]))
        P.op("scalar", lambda e: e.copy(S.cs_k[0:16, 0:2], S.ps_cc[0:16, 0:2]))
        P.op("scalar", lambda e: e.copy(av(OFF.p0_wsc, NH, D), S.ps_wsc[:, :]))
        P.op("scalar", lambda e: e.copy(av(OFF.p0_wss, NH, D), S.ps_wss[:, :]))
        P.op("vector", lambda e: e.tensor_scalar(
            av(OFF.p0_wsc, NH, D), av(OFF.p0_wsc, NH, D), S.cs_k[0:16, 0:1], None,
            ALU.add))
        P.op("vector", lambda e: e.tensor_scalar(
            av(OFF.p0_wss, NH, D), av(OFF.p0_wss, NH, D), S.cs_k[0:16, 1:2], None,
            ALU.add))
        for k in range(KC):
            P.op("tensor", lambda e, k=k: e.transpose(
                S.ps_trc[:, :], av(OFF.p0_wsc + k * 128, NH, 128), S.idn[0:16, 0:16]))
            P.op("scalar", lambda e, k=k: e.copy(
                S.arena[0:128, OFF.wsumT - 16384 + k * 32:OFF.wsumT - 16384 + k * 32 + 32:2],
                S.ps_trc[:, 0:16]))
            P.op("tensor", lambda e, k=k: e.transpose(
                S.ps_trs[:, :], av(OFF.p0_wss + k * 128, NH, 128), S.idn[0:16, 0:16]))
            P.op("scalar", lambda e, k=k: e.copy(
                S.arena[0:128, OFF.wsumT - 16384 + k * 32 + 1:OFF.wsumT - 16384 + k * 32 + 33:2],
                S.ps_trs[:, 0:16]))
        P.op("sync", lambda e: e.dma_start(
            out=av(OFF.p0_wvst[0], 128, 1024),
            in_=dr.wvf[:, 0:1024]), dn=16, tag="p0.wv.0", chain=False, sem="sWO")
        for k in range(KC):
            P.after(f"p0.wv.{k}")
            for h in range(NH):
                P.op("tensor", lambda e, k=k, h=h: e.matmul(
                    S.ps_car2[0:2, h * 64:(h + 1) * 64],
                    av(OFF.wsumT + k * 32 + 2 * h, 128, 2),
                    av(OFF.p0_wvst[0] + h * 64, 128, 64),
                    start=(k == 0 and h % 8 == 0), stop=(k == KC - 1 and h % 8 == 7)),
                    tag=(f"p0.carmm.{k}" if h == NH - 1 else None))
            if k + 1 < KC:
                P.op("sync", lambda e, k=k: e.dma_start(
                    out=av(OFF.p0_wvst[0], 128, 1024),
                    in_=dr.wvf[:, (k + 1) * 1024:(k + 2) * 1024]), dn=16,
                    tag=f"p0.wv.{k + 1}", chain=False, waits=[f"p0.carmm.{k}"],
                    sem="sWO")
        for h in range(NH):
            P.op("scalar", lambda e, h=h: e.copy(av(16384, 2, 64),
                                                 S.ps_car2[0:2, h * 64:(h + 1) * 64]))
            P.op("tensor", lambda e: e.transpose(
                S.ps_ctr[:, :], av(16384, 2, 64), S.idn[0:2, 0:2]))
            P.op("scalar", lambda e: e.copy(av(16448, 64, 2), S.ps_ctr[:, :]))
            hec, off = h // 2, (h % 2) * 64
            P.op("gpsimd", lambda e, hec=hec, off=off: e.dma_start(
                out=S.carry[off:off + 64, hec:hec + 1], in_=av(16448, 64, 2)[:, 0:1]), dn=16,
                tag=f"p0.cdma.{h}.c")
            P.op("gpsimd", lambda e, hec=hec, off=off: e.dma_start(
                out=S.carry[off:off + 64, HEC + hec:HEC + hec + 1],
                in_=av(16448, 64, 2)[:, 1:2]), dn=16, tag=f"p0.cdma.{h}.s")
        P.wait("gpsimd", f"p0.cdma.{NH - 1}.s")
        if "dbg_wsumT" in dbg:
            P.op("gpsimd", lambda e: e.dma_start(
                out=dbg_dr["dbg_wsumT"], in_=av(OFF.wsumT, 128, KC * 32)), dn=16)
        if "dbg_carry" in dbg:
            P.op("gpsimd", lambda e: e.dma_start(
                out=dbg_dr["dbg_carry"], in_=S.carry[:, :]), dn=16)
        if "dbg_carryK" in dbg:
            P.op("gpsimd", lambda e: e.dma_start(
                out=dbg_dr["dbg_carryK"], in_=S.carryK[:, :]), dn=16)

    def ph_p1a(P, hh):
        pfx = f"a{hh}"
        vprev = [None]
        sprev = [None]

        def vop(builder, waits=(), tag=None):
            w = list(waits) + ([vprev[0]] if vprev[0] else [])
            t = P.op("vector", builder, waits=w, chain=False, tag=tag)
            vprev[0] = t
            return t

        def sop(builder, waits=(), tag=None):
            w = list(waits) + ([sprev[0]] if sprev[0] else [])
            t = P.op("scalar", builder, waits=w, chain=False, tag=tag)
            sprev[0] = t
            return t

        for t in range(min(2, NTAU)):
            P.op("sync", lambda e, t=t: e.dma_start(
                out=av(OFF.p1_x[t % 2], 128, 1024),
                in_=dr.x[(hh * NTAU + t) * 128:(hh * NTAU + t + 1) * 128, :]),
                dn=16, tag=f"{pfx}.x.{t}", chain=False, sem=("sA", "sB")[t % 2])
        def emit_ln(t):
            q2 = t % 2
            xs = lambda t=t: av(OFF.p1_x[t % 2], 128, 1024)
            tm = lambda t=t: av(OFF.p1_tm[t % 2], 128, 1024)
            vop(lambda e, t=t: e.tensor_mul(av(OFF.ln_sq, 128, 1024), xs(t), xs(t)),
                waits=[f"{pfx}.x.{t}"])
            vop(lambda e, t=t, q2=q2: e.reduce_sum(S.red1[:, q2:q2 + 1], xs(t), axis=AX),
                tag=f"{pfx}.vr1.{t}")
            vop(lambda e, q2=q2: e.reduce_sum(S.red2[:, q2:q2 + 1],
                                              av(OFF.ln_sq, 128, 1024), axis=AX),
                tag=f"{pfx}.vr2.{t}")
            sop(lambda e, q2=q2: e.activation(
                S.negm[:, q2:q2 + 1], S.red1[:, q2:q2 + 1], ACT.Copy,
                scale=S.consts[:, C_NI1024:C_NI1024 + 1]), waits=[f"{pfx}.vr1.{t}"])
            sop(lambda e, q2=q2: e.activation(S.ln_m[:, q2:q2 + 1],
                                              S.negm[:, q2:q2 + 1], ACT.Square),
                tag=f"{pfx}.a1.{t}")
            vop(lambda e, q2=q2: e.tensor_scalar(
                S.red2[:, q2:q2 + 1], S.red2[:, q2:q2 + 1],
                S.consts[:, C_I1024:C_I1024 + 1], S.ln_m[:, q2:q2 + 1],
                ALU.mult, ALU.subtract), waits=[f"{pfx}.a1.{t}"],
                tag=f"{pfx}.vv.{t}")
            sop(lambda e, q2=q2: e.activation(S.ln_r[:, q2:q2 + 1], S.red2[:, q2:q2 + 1],
                                              ACT.Sqrt,
                                              bias=S.consts[:, C_EPSLN:C_EPSLN + 1]),
                waits=[f"{pfx}.vv.{t}"], tag=f"{pfx}.a2.{t}")
            vop(lambda e, q2=q2: e.reciprocal(S.ln_r[:, q2:q2 + 1], S.ln_r[:, q2:q2 + 1]),
                waits=[f"{pfx}.a2.{t}"])
            nw = [f"{pfx}.trg.{t - 2}"] if t >= 2 else []
            vop(lambda e, t=t, q2=q2: e.tensor_scalar(
                tm(t), xs(t), S.negm[:, q2:q2 + 1], S.ln_r[:, q2:q2 + 1],
                ALU.add, ALU.mult), waits=nw, tag=f"{pfx}.nrm.{t}")
            if t + 2 < NTAU:
                P.op("sync", lambda e, t=t: e.dma_start(
                    out=av(OFF.p1_x[(t + 2) % 2], 128, 1024),
                    in_=dr.x[(hh * NTAU + t + 2) * 128:(hh * NTAU + t + 3) * 128, :]),
                    dn=16, tag=f"{pfx}.x.{t + 2}", chain=False,
                    waits=[f"{pfx}.nrm.{t}"], sem=("sA", "sB")[t % 2])

        def emit_gate(t):
            gw = [f"{pfx}.lo.{t}"]
            if t >= 1:
                gw.append(f"{pfx}.gsev.{t - 1}")
            first = True
            for k in range(KC):
                for pi_, (lh, rh) in enumerate((
                        (S.xn_hi, S.wg_hi), (S.xn_hi, S.wg_lo), (S.xn_lo, S.wg_hi))):
                    P.op("tensor", lambda e, k=k, t=t, lh=lh, rh=rh,
                         st=(k == 0 and pi_ == 0): e.matmul(
                        S.ps_g[:, :],
                        lh[:, k * 1024 + t * 128:k * 1024 + t * 128 + 128],
                        rh[:, k * 8:(k + 1) * 8], start=st, stop=False),
                        waits=(gw if first else ()), chain=False)
                    first = False
            P.op("tensor", lambda e: e.matmul(
                S.ps_g[:, :], S.ones[0:1, 0:128], S.gbias[:, :], start=False, stop=True),
                chain=False, tag=f"{pfx}.gmm.{t}")
            sop(lambda e: e.copy(S.g_s[:, :], S.ps_g[:, :]),
                waits=[f"{pfx}.gmm.{t}"], tag=f"{pfx}.gsev.{t}")

        def emit_routing(t):
            for eix in range(8):
                vop(lambda e, eix=eix: e.scalar_tensor_tensor(
                    S.tmp8b[:, :], S.g_s[:, :], S.g_s[:, eix:eix + 1],
                    S.lt[:, eix * 8:eix * 8 + 8], ALU.is_equal, ALU.mult),
                    waits=([f"{pfx}.gsev.{t}"] if eix == 0 else ()))
                vop(lambda e, eix=eix: e.scalar_tensor_tensor(
                    S.tmp8a[:, :], S.g_s[:, :], S.g_s[:, eix:eix + 1],
                    S.tmp8b[:, :], ALU.is_gt, ALU.add))
                vop(lambda e, eix=eix: e.reduce_sum(
                    S.rank[:, eix:eix + 1], S.tmp8a[:, :], axis=AX))
            vop(lambda e: e.tensor_scalar(
                S.rmask[:, :], S.rank[:, :], S.consts[:, C_35:C_35 + 1], None, ALU.is_le))
            vop(lambda e: e.reduce_max(S.rt_m[:, :], S.g_s[:, :], axis=AX, negate=True),
                tag=f"{pfx}.rt1.{t}")
            sop(lambda e: e.activation(S.eg[:, :], S.g_s[:, :], ACT.Exp,
                                       bias=S.rt_m[:, 0:1]),
                waits=[f"{pfx}.rt1.{t}"], tag=f"{pfx}.exp.{t}")
            vop(lambda e: e.tensor_mul(S.eg[:, :], S.eg[:, :], S.rmask[:, :]),
                waits=[f"{pfx}.exp.{t}"])
            vop(lambda e: e.reduce_sum(S.rt_s[:, :], S.eg[:, :], axis=AX))
            vop(lambda e: e.reciprocal(S.rt_r[:, 0:1], S.rt_s[:, 0:1]))
            vop(lambda e, t=t: e.tensor_scalar(
                S.w_all[:, t * 8:t * 8 + 8], S.eg[:, :], S.rt_r[:, 0:1], None, ALU.mult),
                tag=f"{pfx}.w.{t}")

        emit_ln(0)
        for t in range(NTAU):
            q2 = t % 2
            xs = lambda t=t: av(OFF.p1_x[t % 2], 128, 1024)
            tm = lambda t=t: av(OFF.p1_tm[t % 2], 128, 1024)
            tw = [f"{pfx}.nrm.{t}"]
            if t >= 1:
                tw.append(f"{pfx}.lo.{t - 1}")
            for k in range(KC):
                P.op("tensor", lambda e, k=k, t=t: e.transpose(
                    S.ps_tr8[:, k * 128:(k + 1) * 128],
                    av(OFF.p1_tm[t % 2] + k * 128, 128, 128), S.idn[:, :]),
                    waits=(tw if k == 0 else ()), chain=False,
                    tag=(f"{pfx}.trg.{t}" if k == KC - 1 else None))
            if t >= 1:
                emit_gate(t - 1)
            if t + 1 < NTAU:
                emit_ln(t + 1)
            for k in range(KC):
                sop(lambda e, k=k, t=t: e.copy(
                    S.xn_hi[:, k * 1024 + t * 128:k * 1024 + t * 128 + 128],
                    S.ps_tr8[:, k * 128:(k + 1) * 128]),
                    waits=([f"{pfx}.trg.{t}"] if k == 0 else ()),
                    tag=(f"{pfx}.heg.{t}" if k == KC - 1 else None))
            for k in range(KC):
                vop(lambda e, k=k, t=t: e.tensor_sub(
                    S.xn_lo[:, k * 1024 + t * 128:k * 1024 + t * 128 + 128],
                    S.ps_tr8[:, k * 128:(k + 1) * 128],
                    S.xn_hi[:, k * 1024 + t * 128:k * 1024 + t * 128 + 128]),
                    waits=([f"{pfx}.heg.{t}"] if k == 0 else ()),
                    tag=(f"{pfx}.lo.{t}" if k == KC - 1 else None))
            if t >= 1:
                emit_routing(t - 1)

        emit_gate(NTAU - 1)
        emit_routing(NTAU - 1)
        P.reset_chain()
        P.after(f"{pfx}.w.{NTAU - 1}", f"{pfx}.lo.{NTAU - 1}")
        if not with_attn:
            P.op("vector", lambda e: e.memset(S.acc[:, :], 0.0))
        if "dbg_xnhi" in dbg and hh == 0:
            P.op("gpsimd", lambda e: e.dma_start(
                out=dbg_dr["dbg_xnhi"], in_=S.xn_hi[:, :].bitcast(F32)), dn=16)
            P.op("gpsimd", lambda e: e.dma_start(
                out=dbg_dr["dbg_xnlo"], in_=S.xn_lo[:, :].bitcast(F32)), dn=16)
        if "dbg_w" in dbg and hh == 0:
            P.op("gpsimd", lambda e: e.dma_start(
                out=dbg_dr["dbg_w"], in_=S.w_all[:, :]), dn=16)

    def ph_atn(P, q):
        hh, qh = divmod(q, 2)
        tb = qh * QT
        pfx = f"q{q}"
        for hec in range(min(2, HEC)):
            P.op("sync", lambda e, hec=hec: e.dma_start(
                out=av(OFF.wv_st[hec % 2], 128, 1024), in_=dr.wv[hec, :, :]),
                dn=16, tag=f"{pfx}.wv.{hec}", chain=False, sem=("sWV0", "sWV1")[hec % 2])
        for (psn, col) in (("ps_q", 0), ("ps_k16", 16)):
            for k in range(KC):
                for pi_ in range(3):
                    lh = S.wqk_hi if pi_ != 2 else S.wqk_lo
                    rh = S.xn_hi if pi_ != 1 else S.xn_lo
                    P.op("tensor", lambda e, k=k, psn=psn, col=col, lh=lh, rh=rh,
                         st=(k == 0 and pi_ == 0), sp=(k == KC - 1 and pi_ == 2):
                         e.matmul(getattr(S, psn)[:, :],
                                  lh[:, k * 32 + col:k * 32 + col + 16],
                                  rh[:, k * 1024 + tb:k * 1024 + tb + QT],
                                  start=st, stop=sp))
        P.op("scalar", lambda e: e.activation(av(OFF.t_den, NH, QT), S.ps_q[:, :], ACT.Tanh))
        P.op("scalar", lambda e: e.activation(S.t_cq[:, :], av(OFF.t_den, NH, QT), ACT.Sin,
                                              bias=S.consts[0:16, C_PI2:C_PI2 + 1],
                                              scale=S.consts[0:16, C_PI4:C_PI4 + 1]))
        P.op("scalar", lambda e: e.activation(S.t_sq[:, :], av(OFF.t_den, NH, QT), ACT.Sin,
                                              scale=S.consts[0:16, C_PI4:C_PI4 + 1]))
        P.op("scalar", lambda e: e.activation(av(OFF.t_den, NH, QT), S.ps_k16[:, :], ACT.Tanh))
        P.op("scalar", lambda e: e.activation(S.t_ck[:, :], av(OFF.t_den, NH, QT), ACT.Sin,
                                              bias=S.consts[0:16, C_PI2:C_PI2 + 1],
                                              scale=S.consts[0:16, C_PI4:C_PI4 + 1]))
        P.op("scalar", lambda e: e.activation(S.t_sk[:, :], av(OFF.t_den, NH, QT), ACT.Sin,
                                              scale=S.consts[0:16, C_PI4:C_PI4 + 1]))
        P.op("vector", lambda e: e.tensor_tensor_scan(
            av(OFF.cum_c, NH, QT), S.ones[0:NH, 0:QT], S.t_ck[:, :], S.carryK[:, 0:1],
            ALU.mult, ALU.add))
        P.op("vector", lambda e: e.tensor_copy(S.carryK[:, 0:1], av(OFF.cum_c + QT - 1, NH, 1)))
        P.op("vector", lambda e: e.tensor_tensor_scan(
            av(OFF.cum_s, NH, QT), S.ones[0:NH, 0:QT], S.t_sk[:, :], S.carryK[:, 1:2],
            ALU.mult, ALU.add))
        P.op("vector", lambda e: e.tensor_copy(S.carryK[:, 1:2], av(OFF.cum_s + QT - 1, NH, 1)))
        P.op("vector", lambda e: e.tensor_mul(av(OFF.t_den, NH, QT), S.t_cq[:, :], av(OFF.cum_c, NH, QT)))
        P.op("vector", lambda e: e.tensor_mul(av(OFF.cum_c, NH, QT), S.t_sq[:, :],
                                              av(OFF.cum_s, NH, QT)))
        P.op("vector", lambda e: e.tensor_add(av(OFF.t_den, NH, QT), av(OFF.t_den, NH, QT),
                                              av(OFF.cum_c, NH, QT)))
        P.op("vector", lambda e: e.tensor_scalar(
            av(OFF.t_den, NH, QT), av(OFF.t_den, NH, QT), S.consts[0:16, C_EPSAT:C_EPSAT + 1], None,
            ALU.add))
        P.op("vector", lambda e: e.reciprocal(av(OFF.t_den, NH, QT), av(OFF.t_den, NH, QT)))
        P.op("vector", lambda e: e.tensor_mul(S.t_cq[:, :], S.t_cq[:, :], av(OFF.t_den, NH, QT)))
        P.op("vector", lambda e: e.tensor_mul(S.t_sq[:, :], S.t_sq[:, :], av(OFF.t_den, NH, QT)),
             tag=f"{pfx}.trig")
        if "dbg_trig" in dbg and q == 0:
            for j, srct in enumerate((S.t_cq, S.t_sq, S.t_ck, S.t_sk)):
                P.op("gpsimd", lambda e, j=j, srct=srct: e.dma_start(
                    out=dbg_dr["dbg_trig"][:, j * QT:(j + 1) * QT], in_=srct[:, :]), dn=16)
        P.reset_chain()
        vprev = [f"{pfx}.trig"]

        def vop(builder, waits=(), tag=None):
            t = P.op("vector", builder, waits=list(waits) + vprev, chain=False, tag=tag)
            vprev[0] = t
            return t

        for hec in range(HEC):
            st_off = OFF.wv_st[hec % 2]
            hi_off = OFF.wv_hi[hec % 2]
            lo_off = OFF.wv_lo[hec % 2]
            sw = [f"{pfx}.wv.{hec}"]
            if hec >= 2:
                sw.append(f"{pfx}.vp.{hec - 2}")
            P.op("scalar", lambda e, st_off=st_off, hi_off=hi_off: e.copy(
                av(hi_off, 128, 1024, F32R), av(st_off, 128, 1024)),
                waits=sw, chain=False, tag=f"{pfx}.wvh.{hec}")
            vop(lambda e, st_off=st_off, hi_off=hi_off, lo_off=lo_off:
                e.tensor_sub(av(lo_off, 128, 1024, F32R), av(st_off, 128, 1024),
                             av(hi_off, 128, 1024, F32R)),
                waits=[f"{pfx}.wvh.{hec}"], tag=f"{pfx}.wvs.{hec}")
            vw = [f"{pfx}.wvs.{hec}"]
            if hec >= 2:
                vw.append(f"{pfx}.kv.{hec - 2}")
            first = True
            for k in range(KC):
                for pi_ in range(3):
                    lh = av((hi_off if pi_ < 2 else lo_off) + k * 128, 128, 128, F32R)
                    rh = (S.xn_hi if pi_ != 1 else S.xn_lo)
                    P.op("tensor", lambda e, k=k, lh=lh, rh=rh, hec=hec,
                         st=(k == 0 and pi_ == 0), sp=(k == KC - 1 and pi_ == 2):
                         e.matmul(S.ps_v[hec % 2][:, :], lh,
                                  rh[:, k * 1024 + tb:k * 1024 + tb + QT],
                                  start=st, stop=sp),
                         waits=(vw if first else ()), chain=False,
                         tag=(f"{pfx}.vp.{hec}" if (k == KC - 1 and pi_ == 2) else None))
                    first = False
            if hec + 2 < HEC:
                P.op("sync", lambda e, hec=hec: e.dma_start(
                    out=av(OFF.wv_st[hec % 2], 128, 1024), in_=dr.wv[hec + 2, :, :]),
                    dn=16, tag=f"{pfx}.wv.{hec + 2}", chain=False,
                    waits=[f"{pfx}.vp.{hec}"], sem=("sWV0", "sWV1")[hec % 2])
            for j, (trg, evdst) in enumerate(((S.t_ck, OFF.bc_c), (S.t_sk, OFF.bc_s),
                                              (S.t_cq, OFF.bc_cq), (S.t_sq, OFF.bc_sq))):
                bw = [f"{pfx}.trig"] if hec == 0 else [f"{pfx}.ev.{hec - 1}.{j}"]
                P.op("tensor", lambda e, hec=hec, trg=trg, j=j: e.matmul(
                    S.ps_bc4[j][:, :], S.ind[:, hec * 128:(hec + 1) * 128], trg[:, :],
                    start=True, stop=True),
                    waits=bw, chain=False, tag=f"{pfx}.bc.{hec}.{j}")
                ew = [f"{pfx}.bc.{hec}.{j}"]
                if hec >= 1:
                    ew.append(f"{pfx}.bcr.{hec - 1}.{j}")
                P.op("scalar", lambda e, evdst=evdst, j=j: e.copy(
                    av(evdst, 128, QT), S.ps_bc4[j][:, :]),
                    waits=ew, chain=False, tag=f"{pfx}.ev.{hec}.{j}")
            vop(lambda e, hec=hec: e.tensor_mul(
                av(OFF.kv_c, 128, QT), S.ps_v[hec % 2][:, :], av(OFF.bc_c, 128, QT)),
                waits=[f"{pfx}.vp.{hec}", f"{pfx}.ev.{hec}.0"],
                tag=f"{pfx}.bcr.{hec}.0")
            vop(lambda e, hec=hec: e.tensor_mul(
                av(OFF.kv_s, 128, QT), S.ps_v[hec % 2][:, :], av(OFF.bc_s, 128, QT)),
                waits=[f"{pfx}.ev.{hec}.1"], tag=f"{pfx}.kv.{hec}")
            vop(lambda e, hec=hec: e.tensor_tensor_scan(
                av(OFF.cum_c, 128, QT), S.ones[:, :], av(OFF.kv_c, 128, QT),
                S.carry[:, hec:hec + 1], ALU.mult, ALU.add),
                tag=f"{pfx}.bcr.{hec}.1")
            vop(lambda e, hec=hec: e.tensor_copy(
                S.carry[:, hec:hec + 1], av(OFF.cum_c + QT - 1, 128, 1)))
            vop(lambda e, hec=hec: e.tensor_tensor_scan(
                av(OFF.cum_s, 128, QT), S.ones[:, :], av(OFF.kv_s, 128, QT),
                S.carry[:, HEC + hec:HEC + hec + 1], ALU.mult, ALU.add))
            vop(lambda e, hec=hec: e.tensor_copy(
                S.carry[:, HEC + hec:HEC + hec + 1], av(OFF.cum_s + QT - 1, 128, 1)))
            vop(lambda e: e.tensor_mul(av(OFF.kv_c, 128, QT),
                                       av(OFF.bc_cq, 128, QT),
                                       av(OFF.cum_c, 128, QT)),
                waits=[f"{pfx}.ev.{hec}.2"], tag=f"{pfx}.bcr.{hec}.2")
            vop(lambda e: e.tensor_mul(av(OFF.kv_s, 128, QT),
                                       av(OFF.bc_sq, 128, QT),
                                       av(OFF.cum_s, 128, QT)),
                waits=[f"{pfx}.ev.{hec}.3"], tag=f"{pfx}.bcr.{hec}.3")
            vop(lambda e: e.tensor_add(av(OFF.kv_c, 128, QT),
                                       av(OFF.kv_c, 128, QT),
                                       av(OFF.kv_s, 128, QT)))
            vop(lambda e, hec=hec: e.tensor_copy(
                av(OFF.heads_hi + hec * QT, 128, QT, F32R), av(OFF.kv_c, 128, QT)),
                tag=f"{pfx}.hh.{hec}")
            vop(lambda e, hec=hec: e.tensor_sub(
                av(OFF.heads_lo + hec * QT, 128, QT, F32R), av(OFF.kv_c, 128, QT),
                av(OFF.heads_hi + hec * QT, 128, QT, F32R)),
                tag=f"{pfx}.hd.{hec}")
        P.reset_chain()
        if "dbg_headshi" in dbg and q == 0:
            P.op("gpsimd", lambda e: e.dma_start(
                out=dbg_dr["dbg_headshi"], in_=av(OFF.heads_hi, 128, HEC * QT)), dn=16,
                waits=[f"{pfx}.hd.{HEC - 1}"])
        for hec in range(min(2, HEC)):
            P.op("sync", lambda e, hec=hec: e.dma_start(
                out=av(OFF.wv_st[hec % 2], 128, 1024), in_=dr.wo[hec, :, :]),
                dn=16, tag=f"{pfx}.wo.{hec}", chain=False,
                waits=[f"{pfx}.vp.{HEC - 1}"], sem=("sWV0", "sWV1")[hec % 2])
        for hec in range(HEC):
            st_off = OFF.wv_st[hec % 2]
            hi_off = OFF.wv_hi[hec % 2]
            lo_off = OFF.wv_lo[hec % 2]
            sw = [f"{pfx}.wo.{hec}"]
            if hec < 2:
                sw.append(f"{pfx}.vp.{HEC - 1}")
            else:
                sw.append(f"{pfx}.op.{hec - 2}")
            P.op("scalar", lambda e, st_off=st_off, hi_off=hi_off: e.copy(
                av(hi_off, 128, 1024, F32R), av(st_off, 128, 1024)),
                waits=sw, chain=False, tag=f"{pfx}.woh.{hec}")
            P.op("vector", lambda e, st_off=st_off, hi_off=hi_off, lo_off=lo_off:
                 e.tensor_sub(av(lo_off, 128, 1024, F32R), av(st_off, 128, 1024),
                              av(hi_off, 128, 1024, F32R)),
                 waits=[f"{pfx}.woh.{hec}"], chain=False, tag=f"{pfx}.wos.{hec}")
            first = True
            for tau in range(4):
                for dh in range(2):
                    for pi_ in range(3):
                        lh = av((OFF.heads_hi if pi_ != 2 else OFF.heads_lo)
                                + hec * QT + tau * 128, 128, 128, F32R)
                        rh = av((hi_off if pi_ != 1 else lo_off) + dh * 512,
                                128, 512, F32R)
                        fw = [f"{pfx}.wos.{hec}"]
                        if hec == 0:
                            fw.append(f"{pfx}.hd.{HEC - 1}")
                        P.op("tensor", lambda e, lh=lh, rh=rh, tau=tau, dh=dh,
                             st=(hec == 0 and pi_ == 0),
                             sp=(hec == HEC - 1 and pi_ == 2):
                             e.matmul(S.ps_out[tau][:, dh * 512:(dh + 1) * 512],
                                      lh, rh, start=st, stop=sp),
                             waits=(fw if first else ()), chain=False,
                             tag=(f"{pfx}.op.{hec}"
                                  if (tau == 3 and dh == 1 and pi_ == 2) else None))
                        first = False
            if hec + 2 < HEC:
                P.op("sync", lambda e, hec=hec: e.dma_start(
                    out=av(OFF.wv_st[hec % 2], 128, 1024), in_=dr.wo[hec + 2, :, :]),
                    dn=16, tag=f"{pfx}.wo.{hec + 2}", chain=False,
                    waits=[f"{pfx}.op.{hec}"], sem=("sWV0", "sWV1")[hec % 2])
        P.reset_chain()
        P.after(f"{pfx}.op.{HEC - 1}")
        for tau in range(4):
            P.op("scalar", lambda e, tau=tau: e.copy(
                S.acc[:, (qh * 4 + tau) * D:(qh * 4 + tau + 1) * D],
                S.ps_out[tau][:, :]))

    def ffn_sync(P, hh):
        pfx = f"f{hh}"
        for i in range(NI):
            ee, hc = divmod(i, NHC)
            waits = [f"{pfx}.eo.{i - 2}.{NTAU - 1}"] if i >= 2 else []
            for j, drw in enumerate((dr.w1, dr.w2, dr.w3)):
                P.op("sync", lambda e, j=j, drw=drw, ee=ee, hc=hc, i=i: e.dma_start(
                    out=av(OFF.wst[j][i % 2], 128, 1024), in_=drw[ee, hc, :, :]),
                    dn=16, tag=f"{pfx}.w.{i}.{j}", chain=False,
                    waits=(waits if j == 0 else ()), sem=("sW0", "sW1")[i % 2])

    def ffn_scalar(P, hh):
        pfx = f"f{hh}"

        def hi_copies(i):
            waits = [f"{pfx}.w.{i}.0", f"{pfx}.w.{i}.1", f"{pfx}.w.{i}.2"]
            if i >= 2:
                waits.append(f"{pfx}.eo.{i - 2}.{NTAU - 1}")
            for j in range(3):
                P.op("scalar", lambda e, j=j, i=i: e.copy(
                    av(OFF.whi[j][i % 2], 128, 1024, F32R),
                    av(OFF.wst[j][i % 2], 128, 1024)),
                    waits=(waits if j == 0 else ()), chain=False,
                    tag=(f"{pfx}.hi.{i}" if j == 2 else None))

        hi_copies(0)
        if NI > 1:
            hi_copies(1)
        for i in range(NI):
            P.op("scalar", lambda e, i=i: e.activation(
                av(OFF.t2, 128, 1024), S.ps_h1[:, :], ACT.Silu),
                waits=([f"{pfx}.h1.{i}"] + ([f"{pfx}.t.{i - 1}"] if i >= 1 else [])),
                chain=False, tag=f"{pfx}.sig.{i}")
            if i + 2 < NI:
                hi_copies(i + 2)

    def acc_adds(P, pfx, i):
        ee = i // NHC
        for tau in range(NTAU):
            if ee == 0:
                P.op("vector", lambda e, tau=tau: e.tensor_add(
                    S.acc[:, tau * D:(tau + 1) * D], S.ps_eo[tau % 2][:, :],
                    S.acc[:, tau * D:(tau + 1) * D]),
                    waits=[f"{pfx}.eo.{i}.{tau}"], chain=False,
                    tag=f"{pfx}.acc.{i}.{tau}")
            else:
                P.op("vector", lambda e, tau=tau, ee=ee: e.scalar_tensor_tensor(
                    S.acc[:, tau * D:(tau + 1) * D], S.ps_eo[tau % 2][:, :],
                    S.w_all[:, tau * 8 + ee - 1:tau * 8 + ee],
                    S.acc[:, tau * D:(tau + 1) * D], ALU.mult, ALU.add),
                    waits=[f"{pfx}.eo.{i}.{tau}"], chain=False,
                    tag=f"{pfx}.acc.{i}.{tau}")

    def ffn_vector(P, hh):
        pfx = f"f{hh}"

        def lo_subs(i):
            waits = [f"{pfx}.hi.{i}"]
            for j in range(3):
                P.op("vector", lambda e, j=j, i=i: e.tensor_sub(
                    av(OFF.wlo[j][i % 2], 128, 1024, F32R),
                    av(OFF.wst[j][i % 2], 128, 1024),
                    av(OFF.whi[j][i % 2], 128, 1024, F32R)),
                    waits=(waits if j == 0 else ()), chain=False,
                    tag=(f"{pfx}.lo.{i}" if j == 2 else None))

        lo_subs(0)
        if NI > 1:
            lo_subs(1)
        for i in range(NI):
            if i >= 1:
                acc_adds(P, pfx, i - 1)
            if i >= 1 and i + 1 < NI:
                lo_subs(i + 1)
            P.op("vector", lambda e, i=i: e.tensor_mul(
                av(OFF.t2, 128, 1024), av(OFF.t2, 128, 1024),
                S.ps_h2[:, :]),
                waits=[f"{pfx}.sig.{i}", f"{pfx}.h2.{i}"], chain=False,
                tag=f"{pfx}.t2m.{i}")
            P.op("vector", lambda e, i=i: e.tensor_copy(
                av(OFF.thi[i % 2], 128, 1024, F32R), av(OFF.t2, 128, 1024)),
                waits=[f"{pfx}.t2m.{i}"], chain=False, tag=f"{pfx}.thi.{i}")
            P.op("vector", lambda e, i=i: e.tensor_sub(
                av(OFF.tlo[i % 2], 128, 1024, F32R), av(OFF.t2, 128, 1024),
                av(OFF.thi[i % 2], 128, 1024, F32R)), chain=False,
                waits=[f"{pfx}.thi.{i}"], tag=f"{pfx}.t.{i}")
        acc_adds(P, pfx, NI - 1)
        for tau in range(NTAU):
            w = [f"{pfx}.acc.{NI - 1}.{tau}", f"{pfx}.xr.{tau}", f"{pfx}.hi.{NI - 1}"]
            if tau >= 2:
                w.append(f"{pfx}.y.{tau - 2}")
            P.op("vector", lambda e, tau=tau: e.tensor_add(
                av(OFF.ob[tau % 2], 128, 1024), S.acc[:, tau * D:(tau + 1) * D],
                av(OFF.xr[tau % 2], 128, 1024)),
                waits=w, chain=False, tag=f"{pfx}.add.{tau}")

    def eo_mms(P, pfx, i):
        for tau in range(NTAU):
            waits = []
            if tau == 0:
                waits.append(f"{pfx}.t.{i}")
            if tau >= 2:
                waits.append(f"{pfx}.acc.{i}.{tau - 2}")
            elif i >= 1:
                waits.append(f"{pfx}.acc.{i - 1}.{tau - 2 + NTAU}")
            first = True
            for dh in range(2):
                for pi_ in range(3):
                    thof = OFF.thi[i % 2] if pi_ != 2 else OFF.tlo[i % 2]
                    w3of = OFF.whi[2][i % 2] if pi_ != 1 else OFF.wlo[2][i % 2]
                    P.op("tensor", lambda e, tau=tau, dh=dh, thof=thof, w3of=w3of,
                         st=(pi_ == 0), sp=(pi_ == 2): e.matmul(
                        S.ps_eo[tau % 2][:, dh * 512:(dh + 1) * 512],
                        av(thof + tau * 128, 128, 128, F32R),
                        av(w3of + dh * 512, 128, 512, F32R),
                        start=st, stop=sp),
                        waits=(waits if first else ()), chain=False,
                        tag=(f"{pfx}.eo.{i}.{tau}" if (dh == 1 and pi_ == 2) else None))
                    first = False

    def ffn_h_mms(P, pfx, i, j, psn):
        waits = [f"{pfx}.lo.{i}"]
        if i >= 1:
            waits.append(f"{pfx}.sig.{i - 1}" if j == 0 else f"{pfx}.t2m.{i - 1}")
        first = True
        for nh in range(2):
            for k in range(KC):
                for pi_ in range(3):
                    lhof = OFF.whi[j][i % 2] if pi_ < 2 else OFF.wlo[j][i % 2]
                    rht = S.xn_hi if pi_ != 1 else S.xn_lo
                    P.op("tensor", lambda e, k=k, nh=nh, lhof=lhof, rht=rht,
                         psn=psn, st=(k == 0 and pi_ == 0),
                         sp=(k == KC - 1 and pi_ == 2): e.matmul(
                        getattr(S, psn)[:, nh * 512:(nh + 1) * 512],
                        av(lhof + k * 128, 128, 128, F32R),
                        rht[:, k * 1024 + nh * 512:k * 1024 + (nh + 1) * 512],
                        start=st, stop=sp),
                        waits=(waits if first else ()), chain=False,
                        tag=(f"{pfx}.h{j + 1}.{i}"
                             if (nh == 1 and k == KC - 1 and pi_ == 2) else None))
                    first = False

    def ffn_tensor(P, hh):
        pfx = f"f{hh}"
        for i in range(NI):
            ffn_h_mms(P, pfx, i, 0, "ps_h1")
            if i >= 1:
                eo_mms(P, pfx, i - 1)
            ffn_h_mms(P, pfx, i, 1, "ps_h2")
        eo_mms(P, pfx, NI - 1)

    def ph_p1d(P, hh):
        pfx = f"d{hh}"
        for tau in range(min(2, NTAU)):
            tg = hh * NTAU + tau
            P.op("gpsimd", lambda e, tau=tau, tg=tg: e.dma_start(
                out=av(OFF.xr[tau % 2], 128, 1024),
                in_=dr.x[tg * 128:(tg + 1) * 128, :]), dn=16, tag=f"{pfx}.xr.{tau}",
                chain=False, sem=("sXR0", "sXR1")[tau % 2])
        for tau in range(NTAU):
            tg = hh * NTAU + tau
            P.after(f"{pfx}.xr.{tau}")
            P.op("vector", lambda e, tau=tau: e.tensor_add(
                av(OFF.ob[tau % 2], 128, 1024), S.acc[:, tau * D:(tau + 1) * D],
                av(OFF.xr[tau % 2], 128, 1024)), tag=f"{pfx}.add.{tau}")
            if tau + 2 < NTAU:
                tg2 = hh * NTAU + tau + 2
                P.op("gpsimd", lambda e, tau=tau, tg2=tg2: e.dma_start(
                    out=av(OFF.xr[tau % 2], 128, 1024),
                    in_=dr.x[tg2 * 128:(tg2 + 1) * 128, :]), dn=16,
                    tag=f"{pfx}.xr.{tau + 2}", chain=False,
                    waits=[f"{pfx}.add.{tau}"], sem=("sXR0", "sXR1")[tau % 2])
            P.op("gpsimd", lambda e, tau=tau, tg=tg: e.dma_start(
                out=dr.y[tg * 128:(tg + 1) * 128, :], in_=av(OFF.ob[tau % 2], 128, 1024)),
                dn=16, tag=f"{pfx}.y.{tau}")
        P.wait("gpsimd", f"{pfx}.y.{NTAU - 1}")
        if "dbg_acc" in dbg and hh == 0:
            P.op("gpsimd", lambda e: e.dma_start(
                out=dbg_dr["dbg_acc"], in_=S.acc[:, :]), dn=16)

    def ph_final(P):
        if P.mode == 0:
            return
        if P.mode == 1:
            for s in ("sG", "sY0", "sY1"):
                P.tags.pop(f"__{s}_total", None)
                P.tags[f"__{s}_total"] = (s, P.counters.get(s, 0), "gpsimd", 16)
        elif P.filter == "gpsimd":
            for s in ("sG", "sY0", "sY1"):
                P.eng.wait_ge(P.sems[s], P.tags[f"__{s}_total"][1])

    def ffn_gpsimd(P, hh):
        pfx = f"f{hh}"
        for tau in range(NTAU):
            tg = hh * NTAU + tau
            w = [f"{pfx}.lo.{NI - 1}"] if tau < 2 else [f"{pfx}.add.{tau - 2}"]
            P.op("gpsimd", lambda e, tau=tau, tg=tg: e.dma_start(
                out=av(OFF.xr[tau % 2], 128, 1024),
                in_=dr.x[tg * 128:(tg + 1) * 128, :]), dn=16, tag=f"{pfx}.xr.{tau}",
                chain=False, waits=w, sem=("sXR0", "sXR1")[tau % 2])
            P.op("gpsimd", lambda e, tau=tau, tg=tg: e.dma_start(
                out=dr.y[tg * 128:(tg + 1) * 128, :],
                in_=av(OFF.ob[tau % 2], 128, 1024)),
                dn=16, tag=f"{pfx}.y.{tau}", chain=False,
                waits=[f"{pfx}.add.{tau}"], sem=("sY0", "sY1")[tau % 2])
        P.wait("gpsimd", f"{pfx}.y.{NTAU - 2}", f"{pfx}.y.{NTAU - 1}")

    def ffn_phase(hh):
        return {"sync": lambda P: ffn_sync(P, hh),
                "tensor": lambda P: ffn_tensor(P, hh),
                "vector": lambda P: ffn_vector(P, hh),
                "scalar": lambda P: ffn_scalar(P, hh),
                "gpsimd": lambda P: ffn_gpsimd(P, hh)}

    PH = [("init", "seq", ph_init)]
    if with_attn:
        PH.append(("p0", "seq", ph_p0))
    PH.append(("p1a0", "seq", lambda P: ph_p1a(P, 0)))
    if with_attn:
        PH.append(("atn0", "seq", lambda P: ph_atn(P, 0)))
        PH.append(("atn1", "seq", lambda P: ph_atn(P, 1)))
    PH.append(("ffn0", "streams", ffn_phase(0)))
    PH.append(("p1a1", "seq", lambda P: ph_p1a(P, 1)))
    if with_attn:
        PH.append(("atn2", "seq", lambda P: ph_atn(P, 2)))
        PH.append(("atn3", "seq", lambda P: ph_atn(P, 3)))
    PH.append(("ffn1", "streams", ffn_phase(1)))
    PH.append(("fin", "seq", ph_final))
    if only is not None:
        keep = set(only) | {"init", "fin"}
        PH = [p for p in PH if p[0] in keep]

    STREAM_ORDER = ("sync", "tensor", "vector", "scalar", "gpsimd")

    for mode in (0, 1):
        P.mode = mode
        P.anon = 0
        P.anon_marks = {}
        for name, style, body in PH:
            P.anon_marks[name] = P.anon
            P.reset_chain()
            if style == "seq":
                body(P)
            else:
                for engname in STREAM_ORDER:
                    if engname in body:
                        body[engname](P)

    P.mode = 2
    P.sems = sems

    for name, style, body in PH:
        with ExitStack() as pst:
            if name == "init":
                S.ps_srow = pst.enter_context(nc.psum_tensor("ini_srow", [128, 16], F32))
            elif name == "p0":
                S.ps_wsc = pst.enter_context(nc.psum_tensor(name + "_wsc", [NH, D], F32))
                S.ps_wss = pst.enter_context(nc.psum_tensor(name + "_wss", [NH, D], F32))
                kkb = pst.enter_context(nc.psum_tensor(name + "_kk", [NH, 512], F32))
                ccb = pst.enter_context(nc.psum_tensor(name + "_cc", [NH, 512], F32))
                sh1 = pst.enter_context(nc.psum_tensor(name + "_sh1", [128, 512], F32))
                sh2 = pst.enter_context(nc.psum_tensor(name + "_sh2", [128, 512], F32))
                S.ps_kk = kkb[:, 0:2]
                S.ps_cc = ccb[:, 0:2]
                S.ps_k = [sh1[:, 128:144], sh2[:, 160:176]]
                S.ps_trc = sh1[:, 144:160]
                S.ps_trs = sh2[:, 144:160]
                S.ps_car2 = S.ps_wsc
                S.ps_ctr = sh2[0:64, 128:130]
            elif name.startswith("p1a"):
                S.ps_tr8 = pst.enter_context(nc.psum_tensor(name + "_tr8", [128, 1024], F32))
                S.ps_g = pst.enter_context(nc.psum_tensor(name + "_g", [128, 8], F32))
            elif name.startswith("atn"):
                big1 = pst.enter_context(nc.psum_tensor(name + "_b1", [128, 2048], F32))
                big2 = pst.enter_context(nc.psum_tensor(name + "_b2", [128, 2048], F32))
                S.ps_q = big2[0:NH, 0:QT]
                S.ps_k16 = big2[0:NH, 512:512 + QT]
                S.ps_v = [big2[:, 0:512], big2[:, 512:1024]]
                S.ps_bc4 = [big1[:, j * 512:(j + 1) * 512] for j in range(4)]
                S.ps_out = [big1[:, 0:1024], big1[:, 1024:2048],
                            big2[:, 0:1024], big2[:, 1024:2048]]
            elif name.startswith("ffn"):
                S.ps_h1 = pst.enter_context(nc.psum_tensor(name + "_h1", [128, 1024], F32))
                S.ps_h2 = pst.enter_context(nc.psum_tensor(name + "_h2", [128, 1024], F32))
                S.ps_eo = [pst.enter_context(nc.psum_tensor(name + "_e0", [128, 1024], F32)),
                           pst.enter_context(nc.psum_tensor(name + "_e1", [128, 1024], F32))]
            with nc.Block() as blk:
                if style == "seq":
                    for engname in ENGINES:
                        def run(engobj, engname=engname):
                            P.filter = engname
                            P.eng = engobj
                            P.anon = P.anon_marks[name]
                            P.reset_chain()
                            body(P)
                        getattr(blk, engname)(run)
                else:
                    P.anon = P.anon_marks[name]
                    for engname in STREAM_ORDER:
                        def run(engobj, engname=engname, fn=body[engname]):
                            P.filter = engname
                            P.eng = engobj
                            fn(P)
                        getattr(blk, engname)(run)

    ctx.close()
    return nc, list(dbg_dr.keys())


def kernel_run(inputs, with_attn=True, dbg=(), core_ids=None, **kw):
    shared, per_core = host_prepare(inputs)
    nc, dbg_names = make_kernel(with_attn=with_attn, dbg=dbg)
    in_maps = []
    for c in range(NCORES):
        m = dict(shared)
        m.update(per_core[c])
        in_maps.append(m)
    res = run_bass_kernel_spmd(nc, in_maps, core_ids=list(range(NCORES)), **kw)
    out = np.empty((B, T, D), dtype=np.float32)
    for c in range(NCORES):
        b, s = divmod(c, 4)
        out[b, s * TPC:(s + 1) * TPC, :] = res.results[c]["y"]
    return out, res


def kernel(**inputs):
    out, _ = kernel_run(inputs, with_attn=True)
    return out
